# version 40
# baseline (speedup 1.0000x reference)
import sys
sys.path.insert(0, "/opt/trn_rl_repo")
from contextlib import ExitStack
import numpy as np
import concourse.bass as bass
import concourse.mybir as mybir

F32 = mybir.dt.float32
BF16 = mybir.dt.bfloat16
AF = mybir.ActivationFunctionType
ALU = mybir.AluOpType
AX = mybir.AxisListType

L = 4096
D = 1024
H = 16
DKH = 64
E = 8
FF = 4096
NCH = 8
CW = L + 2
TS = 512
NTS = L // TS
NT = L // 128
QS = 512
NQS = L // QS
EPS = 1e-5

ENGINES = ("tensor", "vector", "scalar", "gpsimd", "sync")


class Sched:

    def __init__(self, nc, stack):
        self.nc = nc
        self.stack = stack
        self.semh = {}
        self.count = {}
        for e in ENGINES:
            k = e + "_c"
            self.semh[k] = stack.enter_context(nc.semaphore("S_" + k))
            self.count[k] = 0
        self.ops = {e: [] for e in ENGINES}
        self.hw = {e: {} for e in ENGINES}
        self.touched = {e: {} for e in ENGINES}

    def _slotkey(self, name):
        k = "sl_" + name
        if k not in self.semh:
            self.semh[k] = self.stack.enter_context(self.nc.semaphore(k))
            self.count[k] = 0
        return k

    def add(self, eng, emit, after=(), dma_slot=None, tag=""):
        waits = {}
        for tok in after:
            if tok is None:
                continue
            (pk, pc) = tok
            if pk == eng + "_c" and eng == "tensor":
                continue
            if pk.startswith("sl_"):
                pc = self.count[pk]
            waits[pk] = max(waits.get(pk, 0), pc)
        if dma_slot is None:
            key = eng + "_c"
            amt = 1
        else:
            key = self._slotkey(dma_slot)
            amt = 16
        self.count[key] += amt
        tok = (key, self.count[key])
        if dma_slot is not None:
            self.touched[eng][key] = self.count[key]
        self.ops[eng].append((emit, waits, key, amt, tag))
        return tok

    def dma(self, eng, out, in_, after=(), slot="misc", tag=""):
        return self.add(eng, (lambda e, o=out, i=in_: e.dma_start(out=o, in_=i)),
                        after=after, dma_slot=slot, tag=tag)

    def nopop(self, eng, after=(), tag="fence"):
        return self.add(eng, (lambda e: e.nop()), after=after, tag=tag)

    def emit_block(self):
        nc = self.nc
        for e in ("sync", "gpsimd", "scalar", "vector"):
            if self.touched[e]:
                toks = [(k, c) for (k, c) in self.touched[e].items()]
                self.nopop(e, after=toks, tag="dma_fence")
                self.touched[e] = {}
        with nc.Block() as blk:
            for eng in ENGINES:
                ops = self.ops[eng]
                if not ops:
                    continue

                def mk(eng=eng, ops=tuple(ops)):
                    def body(eobj):
                        for (emit, waits, key, amt, tag) in ops:
                            for (pk, pc) in sorted(waits.items()):
                                if self.hw[eng].get(pk, 0) >= pc:
                                    continue
                                eobj.wait_ge(self.semh[pk], pc)
                                self.hw[eng][pk] = pc
                            last = emit(eobj)
                            assert last is not None, f"op {tag} on {eng} returned None"
                            last.then_inc(self.semh[key], amt)
                    return body

                getattr(blk, eng)(mk())
        for e in ENGINES:
            self.ops[e] = []
        for e in ENGINES:
            for k in self.semh:
                self.hw[e][k] = max(self.hw[e].get(k, 0), self.count[k])


def build(stop_after=99, taps=(), sim=False):
    global AF_GELU, AF_SIG, AF_GELUTANH, AF_ERF
    AF_GELU = AF.Exp if sim else AF.Gelu
    AF_SIG = AF.Exp if sim else AF.Sigmoid
    AF_GELUTANH = AF.Exp if sim else AF.Gelu_apprx_tanh
    nc = bass.Bass("TRN2", target_bir_lowering=False, detect_race_conditions=not sim)
    taps = set(taps)

    def din(name, shape, dtype=F32):
        return nc.dram_tensor(name, list(shape), dtype, kind="ExternalInput").ap()

    x_in = din("x_fm", (128, NCH * L))
    wq_nat = din("wq_nat", (D, D))
    wk_nat = din("wk_nat", (D, D))
    cqT = din("cqT", (3 * D, D))
    ckT = din("ckT", (3 * D, D))
    qpA_T = din("qpA_T", (D, D))
    qpB_T = din("qpB_T", (D, D))
    kpA_T = din("kpA_T", (D, D))
    kpB_T = din("kpB_T", (D, D))
    wvT_in = din("wvT", (D, D))
    outT_in = din("outT", (D, D))
    gw1T_in = din("gw1T", (D, D))
    gw2T_in = din("gw2T", (D, E))
    w1T_in = din("w1T", (D, FF))
    w2T_in = din("w2T", (FF, D))
    tw1T_in = din("tw1T", (E * D, FF))
    tw2T_in = din("tw2T", (E * FF, D))
    tb1_in = din("tb1", (E, FF))
    tb2_in = din("tb2", (E, D))
    b1row_in = din("b1row", (1, FF))
    b2row_in = din("b2row", (1, D))
    b1pc_in = din("b1pc", (128, 32))
    b2pc_in = din("b2pc", (128, 8))
    wqb_pc = din("wqb_pc", (128, NCH))
    wkb_pc = din("wkb_pc", (128, NCH))
    qpb_pc = din("qpb_pc", (128, NCH))
    kpb_pc = din("kpb_pc", (128, NCH))
    cqb_pc = din("cqb_pc", (128, NCH))
    ckb_pc = din("ckb_pc", (128, NCH))
    outb_pc_in = din("outb_pc", (128, NCH))
    gb1_pc_in = din("gb1_pc", (128, NCH))
    gb2_col_in = din("gb2_col", (E, 1))
    wvb_row_in = din("wvb_row", (1, D))
    tau1w_in = din("tau1w_pc", (128, NCH * 6))
    del1w_in = din("del1w_pc", (128, NCH * 6))
    tau1b_in = din("tau1b_pc", (128, NCH * 2))
    del1b_in = din("del1b_pc", (128, NCH * 2))
    tau2wT_in = din("tau2wT", (128, 16 * H))
    del2wT_in = din("del2wT", (128, 16 * H))
    tau2b_in = din("tau2b_row", (1, H))
    del2b_in = din("del2b_row", (1, H))
    ones_row_in = din("ones_row", (1, 128), BF16)
    ones_rowf_in = din("ones_rowf", (1, 128))
    ones_col_in = din("ones_col", (128, 1), BF16)
    oneh2_in = din("oneh2", (2, 2 * 128), BF16)
    oneh16_in = din("oneh16", (H, H * DKH), BF16)
    ident_in = din("ident_f32", (128, 128))

    y_dev = nc.dram_tensor("y_dev", [128, NCH * L], F32, kind="ExternalOutput").ap()
    qbuf = nc.dram_tensor("qbuf", [128, NCH * L], BF16, kind="Internal").ap()
    kbuf = nc.dram_tensor("kbuf", [128, NCH * L], BF16, kind="Internal").ap()
    x2buf = nc.dram_tensor("x2buf", [128, NCH * L], F32, kind="Internal").ap()
    xn2buf = nc.dram_tensor("xn2buf", [128, NCH * L], BF16, kind="Internal").ap()
    tw1bf = nc.dram_tensor("tw1bf", [E * D, FF], BF16, kind="Internal").ap()
    tw2bf = nc.dram_tensor("tw2bf", [E * FF, D], BF16, kind="Internal").ap()

    tap_outs = {}
    tap_shapes = {
        "xn": ([128, NCH * CW], BF16),
        "tauT": ([128, NT * H], F32),
        "edelT": ([128, NT * H], F32),
        "taufm": ([H, L], BF16),
        "q": ([128, NCH * L], BF16),
        "k": ([128, NCH * L], BF16),
        "vaug": ([128, NT * H * 65], BF16),
        "mbias": ([128, H], F32),
        "attn": ([128, NCH * QS], BF16),
        "x2": ([128, NCH * L], F32),
        "xn2": ([128, NCH * L], BF16),
        "g": ([E, 2], F32),
        "mw1t": ([128, NCH * FF], BF16),
        "mw2t": ([128, 32 * D], BF16),
        "mb1pc": ([128, 32], F32),
        "mb2pc": ([128, 8], F32),
        "biask": ([128, NCH], F32),
        "vkcorr": ([128, 4 * NCH], F32),
        "vraw": ([128, TS], F32),
        "xn2nd": ([128, NCH * CW], BF16),
        "gbc": ([128, E], F32),
    }
    for t in taps:
        sh, dt = tap_shapes[t]
        tap_outs[t] = nc.dram_tensor("tap_" + t, sh, dt, kind="ExternalOutput").ap()

    with ExitStack() as top:
        TE = top.enter_context
        S = Sched(nc, top)

        def apx(sb_tensor, part0, nparts, off, dims):
            t = sb_tensor.ap().tensor
            w = sb_tensor.ap().ap[0][0]
            return bass.AP(t, part0 * w + off, [[w, nparts]] + [[s, c] for (s, c) in dims])


        tauT = None
        delT = None
        edelT = None
        tau_fm = None
        mbias = None
        nrm = None
        nrow = None
        negm_bf = None
        ones_row_sb = TE(nc.sbuf_tensor("ones_row_sb", [1, 128], BF16))
        ones_rowf_sb = TE(nc.sbuf_tensor("ones_rowf_sb", [1, 128], F32))
        ones_col_sb = TE(nc.sbuf_tensor("ones_col_sb", [128, 1], BF16))
        oneh2_sb = TE(nc.sbuf_tensor("oneh2_sb", [2, 2 * 128], BF16))
        oneh16_sb = None
        ident_sb = TE(nc.sbuf_tensor("ident_sb", [128, 128], F32))
        negm0 = TE(nc.sbuf_tensor("negm0", [1, TS], F32))
        rstd0 = TE(nc.sbuf_tensor("rstd0", [1, TS], F32))
        scr0 = TE(nc.sbuf_tensor("scr0", [1, TS], F32))
        scr1 = TE(nc.sbuf_tensor("scr1", [1, TS], F32))
        eps_sb = TE(nc.sbuf_tensor("eps_sb", [1, 1], F32))
        zb_sb = TE(nc.sbuf_tensor("zb_sb", [128, 1], F32))
        biasq = TE(nc.sbuf_tensor("biasq", [128, NCH], F32))
        biask = TE(nc.sbuf_tensor("biask", [128, NCH], F32))
        vq_corr = TE(nc.sbuf_tensor("vq_corr", [128, 4 * NCH], F32))
        vk_corr = TE(nc.sbuf_tensor("vk_corr", [128, 4 * NCH], F32))
        g_sb = TE(nc.sbuf_tensor("g_sb", [E, 2], F32))
        g_bf = TE(nc.sbuf_tensor("g_bf", [E, 1], BF16))
        g_bc = TE(nc.sbuf_tensor("g_bc", [128, E], F32))
        grow_f = TE(nc.sbuf_tensor("grow_f", [1, E], F32))
        grow_bf = TE(nc.sbuf_tensor("grow_bf", [1, E], BF16))
        mb1_pc = TE(nc.sbuf_tensor("mb1_pc", [128, 32], F32))
        mb2_pc = TE(nc.sbuf_tensor("mb2_pc", [128, 8], F32))

        t_or = S.dma("sync", ones_row_sb[:], ones_row_in, slot="consts")
        t_orf = S.dma("sync", ones_rowf_sb[:], ones_rowf_in, slot="consts")
        t_oc = S.dma("sync", ones_col_sb[:], ones_col_in, slot="consts")
        t_o2 = S.dma("sync", oneh2_sb[:], oneh2_in, slot="consts")
        t_id = S.dma("sync", ident_sb[:], ident_in, slot="consts")
        S.add("gpsimd", lambda g: g.memset(eps_sb[:], EPS))
        tok_zb = S.add("gpsimd", lambda g: g.memset(zb_sb[:], 0.0))
        S.add("gpsimd", lambda g: g.memset(nrm[:], 0.0))
        CONSTS = [t_or, t_orf, t_oc, t_o2, t_id]

        def make_ln(sq_st, ps_sum, ps_sq, ps_bc):
            def ln_stats_apply(slices, resid_slices=None, out_slices=None,
                               after=(), prev_apply=None, prev_stats=None,
                               tmpf=None, pair_extra=None, on_pair=None):
                after = list(after)
                def mm_sum(t):
                    r = None
                    for c in range(NCH):
                        r = t.matmul(ps_sum[:, 0:TS], ones_col_sb[:], slices[c],
                                     start=(c == 0), stop=(c == NCH - 1))
                    return r
                tok_sum = S.add("tensor", mm_sum, after=after + CONSTS + [prev_stats], tag="ln_sum")
                sqb = sq_st[0]
                mm_sq_toks = []
                for h in range(4):
                    def sqh(v, h=h):
                        return [v.tensor_tensor(out=sqb[:, i * TS:(i + 1) * TS],
                                                in0=slices[2 * h + i], in1=slices[2 * h + i],
                                                op=ALU.mult) for i in range(2)][-1]
                    war = mm_sq_toks[h - 1] if h else prev_stats
                    tok_sq = S.add("vector", sqh, after=after + [war], tag="ln_sq")
                    def mmsq(t, h=h):
                        r = None
                        for i in range(2):
                            r = t.matmul(ps_sq[:, 0:TS], ones_col_sb[:],
                                         sqb[:, i * TS:(i + 1) * TS],
                                         start=(h == 0 and i == 0),
                                         stop=(h == 3 and i == 1))
                        return r
                    mm_sq_toks.append(S.add("tensor", mmsq, after=[tok_sq, prev_stats], tag="ln_sqmm"))
                def rmath_a(v):
                    v.tensor_scalar(out=negm0[0:1, 0:TS], in0=ps_sum[:, 0:TS],
                                    scalar1=-1.0 / D, scalar2=None, op0=ALU.mult)
                    return v.tensor_scalar(out=scr0[0:1, 0:TS], in0=ps_sq[:, 0:TS],
                                           scalar1=1.0 / D, scalar2=None, op0=ALU.mult)
                tok_ra = S.add("vector", rmath_a, after=[tok_sum, mm_sq_toks[-1]], tag="ln_ra")
                tok_rb = S.add("vector", lambda v: v.tensor_tensor(
                    out=scr1[0:1, 0:TS], in0=negm0[0:1, 0:TS],
                    in1=negm0[0:1, 0:TS], op=ALU.mult), after=[tok_ra], tag="ln_rb")
                tok_rm = S.add("vector", lambda v: v.tensor_tensor(
                    out=scr0[0:1, 0:TS], in0=scr0[0:1, 0:TS], in1=scr1[0:1, 0:TS],
                    op=ALU.subtract), after=[tok_rb], tag="ln_rc")
                tok_ln1 = S.add("scalar", lambda s: s.activation(
                    out=scr1[0:1, 0:TS], in_=scr0[0:1, 0:TS], func=AF.Ln,
                    bias=eps_sb[0:1, 0:1], scale=1.0), after=[tok_rm], tag="ln_ln")
                tok_rs = S.add("scalar", lambda s: s.activation(
                    out=rstd0[0:1, 0:TS], in_=scr1[0:1, 0:TS], func=AF.Exp, scale=-0.5),
                    after=[tok_ln1], tag="ln_exp")
                def bcast(t):
                    t.matmul(ps_bc[:, 0:TS], ones_rowf_sb[0:1, 0:128], negm0[0:1, 0:TS],
                             start=True, stop=True)
                    return t.matmul(ps_bc[:, TS:2 * TS], ones_rowf_sb[0:1, 0:128],
                                    rstd0[0:1, 0:TS], start=True, stop=True)
                tok_bc = S.add("tensor", bcast, after=[tok_rs, tok_rm, prev_apply], tag="ln_bc")
                tok_ap = tok_bc
                for c0 in range(0, NCH, 2):
                    cpair = [c0, c0 + 1]
                    extra = list(pair_extra(c0)) if pair_extra is not None else []
                    if out_slices is None:
                        tok_a = S.add("vector", (lambda v, cp=cpair: [
                            v.tensor_tensor(out=slices[c], in0=slices[c],
                                            in1=ps_bc[:, 0:TS], op=ALU.add)
                            for c in cp][-1]), after=[tok_ap], tag="ap_add")
                        tok_ap = S.add("vector", (lambda v, cp=cpair: [
                            v.tensor_tensor(out=slices[c], in0=slices[c],
                                            in1=ps_bc[:, TS:2 * TS], op=ALU.mult)
                            for c in cp][-1]), after=[tok_a], tag="ap_mul")
                    else:
                        tok_a = S.add("vector", (lambda v, cp=cpair: [
                            v.tensor_tensor(out=tmpf[c % 2][:], in0=slices[c],
                                            in1=ps_bc[:, 0:TS], op=ALU.add)
                            for c in cp][-1]), after=[tok_ap], tag="ap_add")
                        tok_b = S.add("vector", (lambda v, cp=cpair: [
                            v.tensor_tensor(out=tmpf[c % 2][:], in0=tmpf[c % 2][:],
                                            in1=ps_bc[:, TS:2 * TS], op=ALU.mult)
                            for c in cp][-1]), after=[tok_a], tag="ap_mul")
                        if resid_slices is not None:
                            tok_ap = S.add("vector", (lambda v, cp=cpair: [
                                v.tensor_tensor(out=out_slices[c], in0=tmpf[c % 2][:],
                                                in1=resid_slices[c], op=ALU.add)
                                for c in cp][-1]), after=[tok_b] + extra, tag="ap_out")
                        else:
                            tok_ap = S.add("vector", (lambda v, cp=cpair: [
                                v.tensor_copy(out_slices[c], tmpf[c % 2][:])
                                for c in cp][-1]), after=[tok_b] + extra, tag="ap_out")
                    if on_pair is not None:
                        on_pair(c0, tok_ap)
                return tok_ap, tok_rm
            return ln_stats_apply

        FB = {}
        fold_war = {"cT_rd": [], "mjt_rd": [], "pA_rd": [], "wn_rd": [],
                    "pB_rd": [], "sm_rd": [], "psv_rd": None,
                    "last_evac": {0: None, 1: None}, "n": 0}

        def fold_path(pref, wnat_ap, cT_ap, pA_ap, pB_ap, wb_ap, cb_ap, pb_ap,
                      weff_sb, bias_sb, vcorr_sb):
            wn_sb, pB_sb, cT_h, mjt_h, pA_h = (FB[k] for k in
                                               ("wn", "pB", "cT", "mjt", "pA"))
            wb_bf, cb_bf, pb_f, ps, ps_v = (FB[k] for k in
                                            ("wb", "cb", "pb", "ps", "ps_v"))
            st = {}

            def setup():
                ldB, ldW = [], []
                for cc in range(NCH):
                    ldB.append(S.dma("gpsimd", pB_sb[:, cc * D:(cc + 1) * D],
                                     pB_ap[cc * 128:(cc + 1) * 128, :],
                                     after=fold_war["pB_rd"], slot=pref + "pB"))
                    ldW.append(S.dma("gpsimd", wn_sb[:, cc * D:(cc + 1) * D],
                                     wnat_ap[cc * 128:(cc + 1) * 128, :],
                                     after=fold_war["wn_rd"], slot=pref + "wn"))
                ld_wb = S.dma("gpsimd", wb_bf[:], wb_ap, after=fold_war["sm_rd"], slot=pref + "sm")
                ld_cb = S.dma("gpsimd", cb_bf[:], cb_ap, slot=pref + "sm")
                ld_pb = S.dma("gpsimd", pb_f[:], pb_ap, slot=pref + "sm")
                fold_war["pB_rd"] = []
                fold_war["wn_rd"] = []
                fold_war["sm_rd"] = []
                def bmm(t):
                    r = None
                    for oc in range(NCH):
                        for cc in range(NCH):
                            r = t.matmul(ps_v[:, 3 * NCH + oc: 3 * NCH + oc + 1],
                                         pB_sb[:, cc * D + oc * 128: cc * D + (oc + 1) * 128],
                                         cb_bf[:, cc:cc + 1],
                                         start=(cc == 0), stop=(cc == NCH - 1))
                    return r
                tok_bcb = S.add("tensor", bmm,
                                after=[ld_cb, fold_war["psv_rd"]] + ldB, tag="bcb")
                fold_war["sm_rd"].append(tok_bcb)
                st.update(ldB=ldB, ldW=ldW, ld_wb=ld_wb, ld_pb=ld_pb,
                          vm_toks=[tok_bcb], wn_reads=[], pB_reads=[tok_bcb])

            def one_iter(j, oh):
                if "ldB" not in st:
                    setup()
                ldA = []
                if j == 1:
                    for mc in range(NCH):
                        ldA.append(S.dma(
                            "gpsimd", pA_h[:, mc * TS:(mc + 1) * TS],
                            pA_ap[mc * 128:(mc + 1) * 128, oh * TS: oh * TS + TS],
                            after=fold_war["pA_rd"] if mc == 0 else (), slot=pref + "pA"))
                    fold_war["pA_rd"] = []
                mjt_toks = []
                mjt_mm_toks = []
                for mh in range(2):
                    ldc = []
                    for cc in range(NCH):
                        ldc.append(S.dma(
                            "gpsimd", cT_h[:, cc * TS:(cc + 1) * TS],
                            cT_ap[j * D + cc * 128: j * D + (cc + 1) * 128,
                                  mh * TS: mh * TS + TS],
                            after=fold_war["cT_rd"] if cc == 0 else (), slot=pref + "cT"))
                    fold_war["cT_rd"] = []
                    for mi in range(4):
                        mc = mh * 4 + mi
                        pi = fold_war["n"] % 2
                        def mm(t, mi=mi, oh=oh, pi=pi):
                            r = None
                            for cc in range(NCH):
                                r = t.matmul(ps[pi][:, 0:TS],
                                             cT_h[:, cc * TS + mi * 128: cc * TS + (mi + 1) * 128],
                                             pB_sb[:, cc * D + oh * TS: cc * D + oh * TS + TS],
                                             start=(cc == 0), stop=(cc == NCH - 1))
                            return r
                        tok_mm = S.add("tensor", mm,
                                       after=ldc + st["ldB"] + [fold_war["last_evac"][pi]] +
                                             fold_war["mjt_rd"], tag="mjt_mm")
                        fold_war["mjt_rd"] = []
                        st["pB_reads"].append(tok_mm)
                        fold_war["cT_rd"].append(tok_mm)
                        def ev(sc, mc=mc, pi=pi):
                            return sc.activation(out=mjt_h[:, mc * TS:(mc + 1) * TS],
                                                 in_=ps[pi][:, 0:TS], func=AF.Identity,
                                                 bias=zb_sb[:, 0:1], scale=1.0)
                        tok_ev = S.add("scalar", ev, after=[tok_mm, tok_zb], tag="mjt_ev")
                        fold_war["last_evac"][pi] = tok_ev
                        mjt_toks.append(tok_ev)
                        mjt_mm_toks.append(tok_mm)
                        fold_war["n"] += 1
                    if mh == 0:
                        S.nopop("gpsimd", after=mjt_mm_toks[-4:], tag="cT_gate")
                weff_toks = []
                for ic in range(NCH):
                    pi = fold_war["n"] % 2
                    def mm(t, ic=ic, pi=pi, j=j):
                        r = None
                        first = True
                        for mc in range(NCH):
                            r = t.matmul(ps[pi][:, 0:TS],
                                         wn_sb[:, mc * D + ic * 128: mc * D + (ic + 1) * 128],
                                         mjt_h[:, mc * TS:(mc + 1) * TS],
                                         start=first, stop=(j != 1 and mc == NCH - 1))
                            first = False
                        if j == 1:
                            for mc in range(NCH):
                                r = t.matmul(ps[pi][:, 0:TS],
                                             wn_sb[:, mc * D + ic * 128: mc * D + (ic + 1) * 128],
                                             pA_h[:, mc * TS:(mc + 1) * TS],
                                             start=False, stop=(mc == NCH - 1))
                        return r
                    tok_mm = S.add("tensor", mm,
                                   after=mjt_toks + st["ldW"] + ldA +
                                         [fold_war["last_evac"][pi]], tag="weff_mm")
                    st["wn_reads"].append(tok_mm)
                    weff_toks.append(tok_mm)
                    def ev(sc, j=j, ic=ic, oh=oh, pi=pi):
                        return sc.activation(
                            out=weff_sb[:, (j * NCH + ic) * D + oh * TS:(j * NCH + ic) * D + oh * TS + TS],
                            in_=ps[pi][:, 0:TS], func=AF.Identity,
                            bias=zb_sb[:, 0:1], scale=1.0)
                    tok_ev = S.add("scalar", ev, after=[tok_mm, tok_zb], tag="weff_ev")
                    fold_war["last_evac"][pi] = tok_ev
                    fold_war["n"] += 1
                def vmm(t, j=j, oh=oh):
                    r = None
                    for oi in range(4):
                        oc = oh * 4 + oi
                        for mc in range(NCH):
                            r = t.matmul(ps_v[:, j * NCH + oc: j * NCH + oc + 1],
                                         mjt_h[:, mc * TS + oi * 128: mc * TS + (oi + 1) * 128],
                                         wb_bf[:, mc:mc + 1],
                                         start=(mc == 0), stop=(j != 1 and mc == NCH - 1))
                        if j == 1:
                            for mc in range(NCH):
                                r = t.matmul(ps_v[:, j * NCH + oc: j * NCH + oc + 1],
                                             pA_h[:, mc * TS + oi * 128: mc * TS + (oi + 1) * 128],
                                             wb_bf[:, mc:mc + 1],
                                             start=False, stop=(mc == NCH - 1))
                    return r
                tok_v = S.add("tensor", vmm, after=mjt_toks + ldA + [st["ld_wb"]], tag="vmm")
                st["vm_toks"].append(tok_v)
                fold_war["mjt_rd"] = weff_toks + [tok_v]
                fold_war["pA_rd"] = weff_toks + [tok_v] if j == 1 else fold_war["pA_rd"]
                fold_war["sm_rd"].append(tok_v)

            def bias_emit():
                t0 = S.add("vector", lambda v: v.tensor_copy(vcorr_sb[:], ps_v[:, 0:4 * NCH]),
                           after=st["vm_toks"], tag="vc_copy")
                t1 = S.add("vector", lambda v: v.tensor_tensor(
                    out=bias_sb[:], in0=vcorr_sb[:, 0:NCH], in1=vcorr_sb[:, NCH:2 * NCH],
                    op=ALU.add), after=[t0], tag="b1")
                t2 = S.add("vector", lambda v: v.tensor_tensor(
                    out=bias_sb[:], in0=bias_sb[:], in1=vcorr_sb[:, 2 * NCH:3 * NCH],
                    op=ALU.add), after=[t1], tag="b2")
                t3 = S.add("vector", lambda v: v.tensor_tensor(
                    out=bias_sb[:], in0=bias_sb[:], in1=vcorr_sb[:, 3 * NCH:4 * NCH],
                    op=ALU.add), after=[t2], tag="b3")
                tok_bias = S.add("vector", lambda v: v.tensor_tensor(
                    out=bias_sb[:], in0=bias_sb[:], in1=pb_f[:], op=ALU.add),
                    after=[t3, st["ld_pb"]], tag="bias")
                fold_war["psv_rd"] = t0
                fold_war["wn_rd"] = st["wn_reads"]
                fold_war["pB_rd"] = st["pB_reads"]
                fold_war["sm_rd"].append(tok_bias)
                st["bias_tok"] = tok_bias
                return tok_bias

            iters = [(lambda j=j, oh=oh: one_iter(j, oh))
                     for j in range(3) for oh in range(2)]
            return iters, bias_emit, st

        def qk_production(weff_sb, bias_sb, vcorr_sb, buf, scaled, nrm_off, tap_name, tok_bias=None):
            pref = tap_name
            with ExitStack() as stP:
                EE = stP.enter_context
                tbc_sb = EE(nc.sbuf_tensor(pref + "_tbc", [128, L], BF16, side="right")) if scaled else None
                qstage = [EE(nc.sbuf_tensor(pref + "_st%d" % i, [128, TS], BF16, side="right")) for i in range(2)]
                sqq = [EE(nc.sbuf_tensor(pref + "_sqq%d" % i, [128, TS], BF16, side="right")) for i in range(2)]
                ps_q = [EE(nc.psum_tensor(pref + "_ps%d" % i, [128, TS], F32)) for i in range(2)]
                ps_tb = EE(nc.psum_tensor(pref + "_pstb", [128, TS], F32)) if scaled else None
                ps_n = [EE(nc.psum_tensor(pref + "_psn%d" % i, [1, TS], F32)) for i in range(2)]

                toks_spill = []
                prev_evac = [None, None]
                prev_spill = [None, None]
                prev_sq = [None, None]
                prev_tbc = None
                prev_nred = None
                pend_nmm = [None]
                for oc in range(NCH):
                    h0, h1 = 2 * oc, 2 * oc + 1
                    tbc_toks = []
                    if scaled:
                        for tsl in range(NTS):
                            def tbm(t, oc=oc, tsl=tsl):
                                t.matmul(ps_tb[0:64, 0:TS],
                                         oneh16_sb[:, (2 * oc) * 64:(2 * oc + 1) * 64],
                                         tau_fm[0:16, tsl * TS:(tsl + 1) * TS],
                                         start=True, stop=True)
                                return t.matmul(ps_tb[64:128, 0:TS],
                                                oneh16_sb[:, (2 * oc + 1) * 64:(2 * oc + 2) * 64],
                                                tau_fm[0:16, tsl * TS:(tsl + 1) * TS],
                                                start=True, stop=True)
                            tok_tbm = S.add("tensor", tbm, after=[prev_tbc], tag="tbm")
                            def tbe(v, tsl=tsl):
                                return v.tensor_copy(tbc_sb[:, tsl * TS:(tsl + 1) * TS], ps_tb[:, 0:TS])
                            prev_tbc = S.add("vector", tbe, after=[tok_tbm], tag="tbe")
                            tbc_toks.append(prev_tbc)
                    for tsl in range(NTS):
                        bi = tsl % 2
                        def qmm(t, oc=oc, tsl=tsl, bi=bi, weff_sb=weff_sb):
                            r = None
                            first = True
                            for j in range(3):
                                for ic in range(NCH):
                                    r = t.matmul(ps_q[bi][:, 0:TS],
                                                 weff_sb[:, (j * NCH + ic) * D + oc * 128:(j * NCH + ic) * D + (oc + 1) * 128],
                                                 xn[:, ic * CW + tsl * TS + j: ic * CW + tsl * TS + j + TS],
                                                 start=first, stop=(j == 2 and ic == NCH - 1))
                                    first = False
                            return r
                        tok_mm = S.add("tensor", qmm, after=[prev_evac[bi]], tag="qkmm")
                        fix = None
                        if tsl == 0:
                            def fx(v, bi=bi, oc=oc, vcorr_sb=vcorr_sb):
                                return v.tensor_tensor(out=ps_q[bi][:, 0:1], in0=ps_q[bi][:, 0:1],
                                                       in1=vcorr_sb[:, oc:oc + 1], op=ALU.subtract)
                            fix = S.add("vector", fx, after=[tok_mm, tok_bias], tag="qfix")
                        if tsl == NTS - 1:
                            def fx(v, bi=bi, oc=oc, vcorr_sb=vcorr_sb):
                                return v.tensor_tensor(out=ps_q[bi][:, TS - 1:TS], in0=ps_q[bi][:, TS - 1:TS],
                                                       in1=vcorr_sb[:, 2 * NCH + oc: 2 * NCH + oc + 1],
                                                       op=ALU.subtract)
                            fix = S.add("vector", fx, after=[tok_mm, tok_bias], tag="qfix")
                        war = [prev_spill[bi], prev_sq[bi]]
                        if scaled:
                            def evac(v, bi=bi, oc=oc, tsl=tsl, bias_sb=bias_sb):
                                return v.scalar_tensor_tensor(out=qstage[bi][:], in0=ps_q[bi][:, 0:TS],
                                                              scalar=bias_sb[:, oc:oc + 1],
                                                              in1=tbc_sb[:, tsl * TS:(tsl + 1) * TS],
                                                              op0=ALU.add, op1=ALU.mult)
                            tok_ev = S.add("vector", evac, after=[tok_mm, fix, tok_bias] + tbc_toks + war, tag="qevac")
                        else:
                            def evac(s, bi=bi, oc=oc, bias_sb=bias_sb):
                                return s.activation(out=qstage[bi][:], in_=ps_q[bi][:, 0:TS],
                                                    func=AF.Identity, bias=bias_sb[:, oc:oc + 1],
                                                    scale=1.0)
                            tok_ev = S.add("scalar", evac, after=[tok_mm, fix, tok_bias] + war, tag="kevac")
                        prev_evac[bi] = tok_ev
                        tok_sp = S.dma("sync", buf[:, oc * L + tsl * TS: oc * L + (tsl + 1) * TS],
                                       qstage[bi].ap(), after=[tok_ev], tag="spill",
                                       slot=pref + "sp%d" % bi)
                        prev_spill[bi] = tok_sp
                        toks_spill.append(tok_sp)
                        def sq(v, bi=bi):
                            return v.tensor_tensor(out=sqq[bi][:], in0=qstage[bi][:], in1=qstage[bi][:],
                                                   op=ALU.mult)
                        tok_sq = S.add("vector", sq, after=[tok_ev, prev_nred], tag="nsq")
                        prev_sq[bi] = tok_sq
                        def emit_nmm(tok_sq=tok_sq, bi=bi, h0=h0, h1=h1):
                            def nmm(t):
                                t.matmul(ps_n[0][:, 0:TS], ones_col_sb[0:64, :], sqq[bi][0:64, :],
                                         start=True, stop=True)
                                return t.matmul(ps_n[1][:, 0:TS], ones_col_sb[64:128, :],
                                                sqq[bi][64:128, :], start=True, stop=True)
                            tok_nmm = S.add("tensor", nmm, after=[tok_sq, prev_nred], tag="nmm")
                            def nred(v, h0=h0, h1=h1, nrm_off=nrm_off):
                                v.tensor_reduce(out=nrow[0:1, 30:31], in_=ps_n[0][:, 0:TS],
                                                axis=AX.X, op=ALU.max)
                                v.tensor_reduce(out=nrow[0:1, 31:32], in_=ps_n[1][:, 0:TS],
                                                axis=AX.X, op=ALU.max)
                                v.tensor_tensor(out=nrm[0:1, nrm_off + h0: nrm_off + h0 + 1],
                                                in0=nrm[0:1, nrm_off + h0: nrm_off + h0 + 1],
                                                in1=nrow[0:1, 30:31], op=ALU.max)
                                return v.tensor_tensor(out=nrm[0:1, nrm_off + h1: nrm_off + h1 + 1],
                                                       in0=nrm[0:1, nrm_off + h1: nrm_off + h1 + 1],
                                                       in1=nrow[0:1, 31:32], op=ALU.max)
                            return S.add("vector", nred, after=[tok_nmm], tag="nred")
                        if pend_nmm[0] is not None:
                            prev_nred = pend_nmm[0]()
                        pend_nmm[0] = emit_nmm
                if pend_nmm[0] is not None:
                    prev_nred = pend_nmm[0]()
                    pend_nmm[0] = None
                if tap_name in taps:
                    S.dma("sync", tap_outs[tap_name], buf, after=toks_spill, slot="tap")
                S.emit_block()

        _tau_cms = []
        def _lalloc(name, shape, dt):
            cm = nc.sbuf_tensor(name, shape, dt, side="left")
            _tau_cms.append(cm)
            return cm.__enter__()
        xn_cm = nc.sbuf_tensor("xn", [128, NCH * CW], BF16, side="right")
        xn = xn_cm.__enter__()
        mbias = _lalloc("mbias", [128, H], F32)
        edelT = _lalloc("edelT", [128, NT * H], F32)
        tau_fm = _lalloc("tau_fm", [H, L], BF16)
        _tau3_cms = []
        def _lalloc3(name, shape, dt):
            cm = nc.sbuf_tensor(name, shape, dt, side="left")
            _tau3_cms.append(cm)
            return cm.__enter__()
        tauT = _lalloc3("tauT", [128, NT * H], F32)
        delT = _lalloc3("delT", [128, NT * H], F32)
        nrm = _lalloc3("nrm", [1, 2 * H], F32)
        nrow = _lalloc3("nrow", [1, 2 * H], F32)
        negm_bf = _lalloc3("negm_bf", [1, H], BF16)
        oneh16_sb = _lalloc3("oneh16_sb", [H, H * DKH], BF16)
        t_o16 = S.dma("sync", oneh16_sb[:], oneh16_in, slot="consts")
        CONSTS.append(t_o16)
        S.add("gpsimd", lambda g: g.memset(
            apx(xn, 0, 128, 0, [(CW, NCH), (CW - 1, 2)]), 0.0))

        with ExitStack() as stWQ:
            WE = stWQ.enter_context
            weff = WE(nc.sbuf_tensor("weff", [128, 3 * NCH * D], BF16, side="right"))
            FB["wn"] = WE(nc.sbuf_tensor("f_wn", [128, NCH * D], BF16, side="right"))
            FB["pB"] = WE(nc.sbuf_tensor("f_pB", [128, NCH * D], BF16, side="right"))
            FB["cT"] = WE(nc.sbuf_tensor("f_cTh", [128, NCH * TS], BF16, side="right"))
            FB["mjt"] = WE(nc.sbuf_tensor("f_mjth", [128, NCH * TS], BF16, side="right"))
            FB["pA"] = WE(nc.sbuf_tensor("f_pAh", [128, NCH * TS], BF16, side="right"))
            FB["wb"] = WE(nc.sbuf_tensor("f_wb", [128, NCH], BF16, side="right"))
            FB["cb"] = WE(nc.sbuf_tensor("f_cb", [128, NCH], BF16, side="right"))
            FB["pb"] = WE(nc.sbuf_tensor("f_pb", [128, NCH], F32, side="right"))
            FB["ps"] = [WE(nc.psum_tensor("f_ps%d" % i, [128, TS], F32)) for i in range(2)]
            FB["ps_v"] = WE(nc.psum_tensor("f_psv", [128, 4 * NCH], F32))

            def apx(sb_tensor, part0, nparts, off, dims):
                t = sb_tensor.ap().tensor
                w = sb_tensor.ap().ap[0][0]
                return bass.AP(t, part0 * w + off, [[w, nparts]] + [[s, c] for (s, c) in dims])

            fq_iters, fq_bias, fq_st = fold_path(
                "fq", wq_nat, cqT, qpA_T, qpB_T, wqb_pc, cqb_pc, qpb_pc,
                weff, biasq, vq_corr)

            stH = ExitStack()
            HE = stH.enter_context
            sqst4 = HE(nc.sbuf_tensor("p1_sq4", [128, 2 * TS], BF16))
            psb2d = [HE(nc.psum_tensor("p1_psb%d" % i, [128, TS], F32)) for i in range(2)]
            t1w = HE(nc.sbuf_tensor("t1w", [128, NCH * 6], F32))
            d1w = HE(nc.sbuf_tensor("d1w", [128, NCH * 6], F32))
            t1b = HE(nc.sbuf_tensor("t1b", [128, NCH * 2], F32))
            d1b = HE(nc.sbuf_tensor("d1b", [128, NCH * 2], F32))
            t2w = HE(nc.sbuf_tensor("t2w", [128, 16 * H], BF16))
            d2w = HE(nc.sbuf_tensor("d2w", [128, 16 * H], BF16))
            t2b = HE(nc.sbuf_tensor("t2b", [1, H], BF16))
            d2b = HE(nc.sbuf_tensor("d2b", [1, H], BF16))
            dwst = [HE(nc.sbuf_tensor("dwst%d" % i, [128, TS], F32)) for i in range(2)]
            gstage = HE(nc.sbuf_tensor("gstage", [128, 4 * TS], BF16))
            ps_pw_t = HE(nc.psum_tensor("ps_pw_t", [128, 512], F32))
            ps_pw_d = HE(nc.psum_tensor("ps_pw_d", [128, 512], F32))
            ps_tr = HE(nc.psum_tensor("ps_tr", [16, 128], F32))

            l1 = S.dma("sync", t1w[:], tau1w_in, slot="t1")
            l2 = S.dma("sync", d1w[:], del1w_in, slot="t1")
            l3 = S.dma("sync", t1b[:], tau1b_in, slot="t1")
            l4 = S.dma("sync", d1b[:], del1b_in, slot="t1")
            l5 = S.dma("gpsimd", t2w[:], tau2wT_in, slot="t2")
            l6 = S.dma("gpsimd", d2w[:], del2wT_in, slot="t2")
            l7 = S.dma("gpsimd", t2b[:], tau2b_in, slot="t2")
            l8 = S.dma("gpsimd", d2b[:], del2b_in, slot="t2")
            LD3 = [l1, l2, l3, l4, l5, l6, l7, l8]

            xld = {}

            def emit_xload(ti):
                if ti % 2 == 1:
                    xld[ti] = xld[ti - 1]
                    return
                tok = None
                for c in range(NCH):
                    tok = S.dma("gpsimd",
                                xn[:, c * CW + 1 + ti * TS: c * CW + 1 + (ti + 2) * TS],
                                x_in[:, c * L + ti * TS: c * L + (ti + 2) * TS],
                                slot="xload")
                xld[ti] = tok
                xld[ti + 1] = tok

            b2war = {"bank": {0: [], 1: []}, "mmsq": None}
            xn_toks = {}

            def mk_ln_x(ti):
                psb2 = psb2d[ti % 2]
                slc = lambda c: xn[:, c * CW + 1 + ti * TS: c * CW + 1 + (ti + 1) * TS]
                st = {}

                def s_stats():
                    def mm_sum(t):
                        r = None
                        for c in range(NCH):
                            r = t.matmul(psb2[0:1, 0:TS], ones_col_sb[:], slc(c),
                                         start=(c == 0), stop=(c == NCH - 1))
                        return r
                    t_sum = S.add("tensor", mm_sum,
                                  after=[xld[ti]] + CONSTS + b2war["bank"][ti % 2],
                                  tag="ln_sum")
                    mmh = None
                    for h in range(4):
                        def sqh(v, h=h):
                            return [v.tensor_tensor(out=sqst4[:, i * TS:(i + 1) * TS],
                                                    in0=slc(2 * h + i), in1=slc(2 * h + i),
                                                    op=ALU.mult) for i in range(2)][-1]
                        t_sq = S.add("vector", sqh,
                                     after=[xld[ti], mmh if h else b2war["mmsq"]],
                                     tag="ln_sq")
                        def mmsq(t, h=h):
                            r = None
                            for i in range(2):
                                r = t.matmul(psb2[32:33, 0:TS], ones_col_sb[:],
                                             sqst4[:, i * TS:(i + 1) * TS],
                                             start=(h == 0 and i == 0),
                                             stop=(h == 3 and i == 1))
                            return r
                        mmh = S.add("tensor", mmsq, after=[t_sq], tag="ln_sqmm")
                    b2war["mmsq"] = mmh
                    st["sum"], st["sq"] = t_sum, mmh

                def s_rmath():
                    def rmath_a(v):
                        v.tensor_scalar(out=negm0[0:1, 0:TS], in0=psb2[0:1, 0:TS],
                                        scalar1=-1.0 / D, scalar2=None, op0=ALU.mult)
                        return v.tensor_scalar(out=scr0[0:1, 0:TS], in0=psb2[32:33, 0:TS],
                                               scalar1=1.0 / D, scalar2=None, op0=ALU.mult)
                    t_ra = S.add("vector", rmath_a,
                                 after=[st["sum"], st["sq"], b2war.get("lnact")],
                                 tag="ln_ra")
                    t_rb = S.add("vector", lambda v: v.tensor_tensor(
                        out=scr1[0:1, 0:TS], in0=negm0[0:1, 0:TS], in1=negm0[0:1, 0:TS],
                        op=ALU.mult), after=[t_ra], tag="ln_rb")
                    st["rm"] = S.add("vector", lambda v: v.tensor_tensor(
                        out=scr0[0:1, 0:TS], in0=scr0[0:1, 0:TS], in1=scr1[0:1, 0:TS],
                        op=ALU.subtract), after=[t_rb], tag="ln_rc")
                    st["ra"] = t_ra

                def s_act():
                    t_ln = S.add("scalar", lambda sc: sc.activation(
                        out=scr1[0:1, 0:TS], in_=scr0[0:1, 0:TS], func=AF.Ln,
                        bias=eps_sb[0:1, 0:1], scale=1.0), after=[st["rm"]], tag="ln_ln")
                    st["rs"] = S.add("scalar", lambda sc: sc.activation(
                        out=rstd0[0:1, 0:TS], in_=scr1[0:1, 0:TS], func=AF.Exp, scale=-0.5),
                        after=[t_ln], tag="ln_exp")
                    b2war["lnact"] = st["rs"]
                    def bcn(t):
                        return t.matmul(psb2[:, 0:TS], ones_rowf_sb[0:1, 0:128],
                                        negm0[0:1, 0:TS], start=True, stop=True)
                    st["bn"] = S.add("tensor", bcn, after=[st["ra"]], tag="ln_bcn")

                def s_adds():
                    add_toks = []
                    prev = None
                    for c0 in range(0, NCH, 2):
                        def ap_add(v, c0=c0):
                            return [v.tensor_tensor(out=slc(c), in0=slc(c),
                                                    in1=psb2[:, 0:TS], op=ALU.add)
                                    for c in (c0, c0 + 1)][-1]
                        prev = S.add("vector", ap_add, after=[st["bn"], prev], tag="ap_add")
                        add_toks.append(prev)
                    st["adds"] = add_toks

                def s_muls():
                    def bcr(t):
                        return t.matmul(psb2[:, 0:TS], ones_rowf_sb[0:1, 0:128],
                                        rstd0[0:1, 0:TS], start=True, stop=True)
                    t_br = S.add("tensor", bcr, after=st["adds"] + [st["rs"]], tag="ln_bcr")
                    mul_toks = []
                    prev = None
                    for c0 in range(0, NCH, 2):
                        def ap_mul(v, c0=c0):
                            return [v.tensor_tensor(out=slc(c), in0=slc(c),
                                                    in1=psb2[:, 0:TS], op=ALU.mult)
                                    for c in (c0, c0 + 1)][-1]
                        prev = S.add("vector", ap_mul, after=[t_br, prev], tag="ap_mul")
                        mul_toks.append(prev)
                    b2war["bank"][ti % 2] = mul_toks
                    xn_toks[ti] = prev

                return [s_stats, s_rmath, s_act, s_adds, s_muls]

            all_pw = {"tau": [], "del": []}
            dw_state = {"prev_pw": [], "prev_ge": [None, None]}

            def mk_dw(ts_i):
                units = []
                pst = {}
                for path, w1, b1p, w2, b2p in (
                        ("tau", t1w, t1b, t2w, t2b),
                        ("del", d1w, d1b, d2w, d2b)):
                    pspw = ps_pw_t if path == "tau" else ps_pw_d
                    for q in range(4):
                        def unit(path=path, w1=w1, b1p=b1p, w2=w2, b2p=b2p,
                                 pspw=pspw, q=q):
                            pw_toks = pst.setdefault(path, [])
                            ge_toks = []
                            for cj in range(4 * q, 4 * q + 4):
                                c, j = divmod(cj, 2)
                                bi = cj % 2
                                def dw_s(v, k, c=c, j=j, bi=bi, w1=w1, ts_i=ts_i):
                                    base = c * CW + ts_i * TS
                                    wcol = w1[:, c * 6 + j * 3 + k: c * 6 + j * 3 + k + 1]
                                    if k == 0:
                                        return v.tensor_scalar(out=dwst[bi][:], in0=xn[:, base: base + TS],
                                                               scalar1=wcol, scalar2=None, op0=ALU.mult)
                                    return v.scalar_tensor_tensor(
                                        out=dwst[bi][:], in0=xn[:, base + k: base + k + TS],
                                        scalar=wcol, in1=dwst[bi][:], op0=ALU.mult, op1=ALU.add)
                                t_dw = S.add("vector", (lambda v, f=dw_s: f(v, 0)),
                                             after=LD3 + [dw_state["prev_ge"][bi]], tag="dw0")
                                t_dw = S.add("vector", (lambda v, f=dw_s: f(v, 1)), after=[t_dw], tag="dw1")
                                t_dw = S.add("vector", (lambda v, f=dw_s: f(v, 2)), after=[t_dw], tag="dw2")
                                def ge(s, c=c, j=j, bi=bi, cj=cj, b1p=b1p):
                                    return s.activation(out=gstage[:, (cj % 4) * TS:((cj % 4) + 1) * TS],
                                                        in_=dwst[bi][:], func=AF_GELU,
                                                        bias=b1p[:, c * 2 + j: c * 2 + j + 1], scale=1.0)
                                war = dw_state["prev_pw"] if q == 0 else pw_toks[-4:]
                                tok_ge = S.add("scalar", ge, after=[t_dw] + war, tag="gelu")
                                dw_state["prev_ge"][bi] = tok_ge
                                ge_toks.append(tok_ge)
                            for tt in range(4):
                                def pw(t, tt=tt, w2=w2, b2p=b2p, pspw=pspw, ts_i=ts_i, q=q):
                                    co = (ts_i * 4 + tt) * 16
                                    r = None
                                    for cj4 in range(4):
                                        cj = q * 4 + cj4
                                        r = t.matmul(pspw[:, co: co + 16],
                                                     gstage[:, cj4 * TS + tt * 128: cj4 * TS + (tt + 1) * 128],
                                                     w2[:, cj * H:(cj + 1) * H],
                                                     start=(cj == 0), stop=False)
                                    if q == 3:
                                        r = t.matmul(pspw[:, co: co + 16],
                                                     ones_row_sb[0:1, 0:128], b2p[0:1, :],
                                                     start=False, stop=True)
                                    return r
                                pw_toks.append(S.add("tensor", pw, after=ge_toks, tag="pw"))
                            if q == 3:
                                all_pw[path] += pw_toks[-4:]
                                dw_state["prev_pw"] = pw_toks[-4:]
                        units.append(unit)
                return units

            for ti in range(NTS):
                emit_xload(ti)
            tok_bq = None
            for ti in range(NTS):
                stages = mk_ln_x(ti)
                dwq = mk_dw(ti - 2) if ti >= 2 else None
                stages[0]()
                if dwq:
                    dwq[0](); dwq[1]()
                stages[1]()
                stages[2]()
                if dwq:
                    dwq[2](); dwq[3]()
                stages[3]()
                if dwq:
                    dwq[4](); dwq[5]()
                stages[4]()
                if dwq:
                    dwq[6](); dwq[7]()
                if ti >= 2 and ti - 2 < len(fq_iters):
                    fq_iters[ti - 2]()
            for ts_t in (6, 7):
                for u in mk_dw(ts_t):
                    u()
            tok_bq = fq_bias()

            def sig_t(s):
                return s.activation(out=tauT[:], in_=ps_pw_t[:], func=AF_SIG)
            tok_sig_t = S.add("scalar", sig_t, after=all_pw["tau"], tag="sig_t")
            def sig_d(s):
                return s.activation(out=delT[:], in_=ps_pw_d[:], func=AF_SIG)
            tok_sig_d = S.add("scalar", sig_d, after=all_pw["del"], tag="sig_d")
            tok_sig = {"tau": tok_sig_t, "del": tok_sig_d}
            def edel(s):
                return s.activation(out=edelT[:], in_=delT[:], func=AF.Exp)
            tok_edel = S.add("scalar", edel, after=[tok_sig["del"]], tag="edel")
            tfm_toks = []
            prev_ev_t = None
            for tt in range(NT):
                def tr(t, tt=tt):
                    return t.transpose(ps_tr[:], tauT[:, tt * H:(tt + 1) * H], ident_sb[:])
                tok_tr = S.add("tensor", tr, after=[tok_sig["tau"], prev_ev_t], tag="tfm_tr")
                def ev(v, tt=tt):
                    return v.tensor_scalar(out=tau_fm[0:16, tt * 128:(tt + 1) * 128],
                                           in0=ps_tr[0:16, :], scalar1=0.125,
                                           scalar2=None, op0=ALU.mult)
                prev_ev_t = S.add("vector", ev, after=[tok_tr], tag="tfm_ev")
                tfm_toks.append(prev_ev_t)
            if "xn" in taps:
                S.dma("sync", tap_outs["xn"], xn.ap(),
                      after=[xn_toks[ti] for ti in range(NTS)], slot="tap")
            if "tauT" in taps:
                S.dma("sync", tap_outs["tauT"], tauT.ap(), after=[tok_sig["tau"]], slot="tap")
            if "edelT" in taps:
                S.dma("sync", tap_outs["edelT"], edelT.ap(), after=[tok_edel], slot="tap")
            if "taufm" in taps:
                S.dma("sync", tap_outs["taufm"], tau_fm.ap(), after=tfm_toks, slot="tap")
            S.emit_block()
            stH.close()
            if stop_after <= 3:
                return nc, tap_outs
            qk_production(weff, biasq, vq_corr, qbuf, True, 0, "q", tok_bq)
            fk_iters, fk_bias, fk_st = fold_path(
                "fk", wk_nat, ckT, kpA_T, kpB_T, wkb_pc, ckb_pc, kpb_pc,
                weff, biask, vk_corr)
            for it in fk_iters:
                it()
            tok_bk = fk_bias()
            if "biask" in taps:
                S.dma("sync", tap_outs["biask"], biask.ap(), after=[], slot="tap")
            if "vkcorr" in taps:
                S.dma("sync", tap_outs["vkcorr"], vk_corr.ap(), after=[], slot="tap")
            S.emit_block()
            qk_production(weff, biask, vk_corr, kbuf, False, H, "k", tok_bk)
        if stop_after <= 4:
            return nc, tap_outs

        vaug_cm = nc.sbuf_tensor("vaug", [128, NT * H * 65], BF16, side="left")
        vaug = vaug_cm.__enter__()
        outT_cm = nc.sbuf_tensor("outT_sb", [128, NCH * D], BF16, side="left")
        outT_sb = outT_cm.__enter__()
        with ExitStack() as stV:
            EE = stV.enter_context
            wvT_sb = EE(nc.sbuf_tensor("wvT_sb", [128, NCH * D], BF16, side="right"))
            wvb_bf = EE(nc.sbuf_tensor("wvb_bf", [1, D], BF16, side="right"))
            ps_v = [EE(nc.psum_tensor("v_ps%d" % i, [128, TS], F32)) for i in range(2)]
            ps_m = EE(nc.psum_tensor("v_psm", [128, H], F32))

            vraw_sb = EE(nc.sbuf_tensor("vraw_sb", [128, TS], F32, side="right")) if "vraw" in taps else None
            for cc in range(NCH):
                S.dma("gpsimd", wvT_sb[:, cc * D:(cc + 1) * D],
                      wvT_in[cc * 128:(cc + 1) * 128, :], slot="vw")
                S.dma("gpsimd", outT_sb[:, cc * D:(cc + 1) * D],
                      outT_in[cc * 128:(cc + 1) * 128, :], slot="vw")
            lvb = S.dma("gpsimd", wvb_bf[:], wvb_row_in, slot="vw")
            lv = lo = lvb

            def mrow(v):
                return v.tensor_tensor(out=nrow[0:1, 0:H], in0=nrm[0:1, 0:H],
                                       in1=nrm[0:1, H:2 * H], op=ALU.mult)
            tok_mr = S.add("vector", mrow, tag="mrow")
            def mln(s):
                s.activation(out=nrow[0:1, 0:H], in_=nrow[0:1, 0:H], func=AF.Ln)
                return s.activation(out=nrow[0:1, 0:H], in_=nrow[0:1, 0:H], func=AF.Exp, scale=0.5)
            tok_mln = S.add("scalar", mln, after=[tok_mr], tag="mln")
            def mneg(v):
                return v.tensor_scalar(out=negm_bf[0:1, 0:H], in0=nrow[0:1, 0:H],
                                       scalar1=-1.0, scalar2=None, op0=ALU.mult)
            tok_mneg = S.add("vector", mneg, after=[tok_mln], tag="mneg")
            def mbc(t):
                r = None
                for h in range(H):
                    r = t.matmul(ps_m[:, h:h + 1], ones_row_sb[0:1, 0:128],
                                 negm_bf[0:1, h:h + 1], start=True, stop=True)
                return r
            tok_mbc = S.add("tensor", mbc, after=[tok_mneg], tag="mbc")
            tok_mev = S.add("vector", lambda v: v.tensor_copy(mbias[:], ps_m[:]),
                            after=[tok_mbc], tag="mev")

            vaug_toks = []
            prev_vev = [None, None]
            for tt in range(NT):
                def augc(v, tt=tt):
                    return v.tensor_copy(apx(vaug, 0, 128, tt * 1040 + 64, [(65, H)]),
                                         edelT[:, tt * H:(tt + 1) * H])
                tok_aug = S.add("vector", augc, tag="aug")
                vaug_toks.append(tok_aug)
                for dh in range(2):
                    bi = (tt * 2 + dh) % 2
                    def vmm(t, tt=tt, dh=dh, bi=bi):
                        r = None
                        for ic in range(NCH):
                            r = t.matmul(ps_v[bi][:, 0:TS],
                                         xn[:, ic * CW + 1 + tt * 128: ic * CW + 1 + (tt + 1) * 128],
                                         wvT_sb[:, ic * D + dh * TS: ic * D + (dh + 1) * TS],
                                         start=(ic == 0), stop=False)
                        return t.matmul(ps_v[bi][:, 0:TS], ones_row_sb[0:1, 0:128],
                                        wvb_bf[0:1, dh * TS:(dh + 1) * TS], start=False, stop=True)
                    tok_vmm = S.add("tensor", vmm, after=[lv, lvb, prev_vev[bi]], tag="vmm")
                    def vev(v, tt=tt, dh=dh, bi=bi):
                        r = None
                        for hh in range(8):
                            h = dh * 8 + hh
                            r = v.tensor_scalar(
                                out=vaug[:, tt * 1040 + h * 65: tt * 1040 + h * 65 + 64],
                                in0=ps_v[bi][:, hh * 64:(hh + 1) * 64],
                                scalar1=edelT[:, tt * H + h: tt * H + h + 1],
                                scalar2=None, op0=ALU.mult)
                        return r
                    if tt == 0 and dh == 0 and "vraw" in taps:
                        tr = S.add("vector", (lambda v, bi=bi: v.tensor_copy(
                            vraw_sb[:], ps_v[bi][:, 0:TS])), after=[tok_vmm], tag="vrawcp")
                        S.dma("sync", tap_outs["vraw"], vraw_sb.ap(), after=[tr], slot="tap")
                    tok_vev = S.add("vector", vev, after=[tok_vmm], tag="vev")
                    prev_vev[bi] = tok_vev
                    vaug_toks.append(tok_vev)
            if "xn2nd" in taps:
                S.dma("sync", tap_outs["xn2nd"], xn.ap(), after=[], slot="tap")
            if "mbias" in taps:
                S.dma("sync", tap_outs["mbias"], mbias.ap(), after=[tok_mev], slot="tap")
            if "vaug" in taps:
                S.dma("sync", tap_outs["vaug"], vaug.ap(), after=vaug_toks, slot="tap")
            S.emit_block()
        if stop_after <= 5:
            return nc, tap_outs

        xn_cm.__exit__(None, None, None)
        with ExitStack() as st5:
            EE = st5.enter_context
            qsuper = [EE(nc.sbuf_tensor("qsuper%d" % i, [128, NCH * QS], BF16, side="right"))
                      for i in range(2)]
            khp = [EE(nc.sbuf_tensor("khp%d" % i, [128, L], BF16, side="right")) for i in range(2)]
            probs0 = [EE(nc.sbuf_tensor("pb0%d" % i, [128, QS], BF16, side="right")) for i in range(2)]
            probs1 = [EE(nc.sbuf_tensor("pb1%d" % i, [128, QS], BF16, side="right")) for i in range(2)]
            attn_fm = [EE(nc.sbuf_tensor("attn_fm%d" % i, [128, NCH * QS], BF16, side="right"))
                       for i in range(2)]
            zr0b = EE(nc.sbuf_tensor("zr0b", [1, QS], BF16, side="right"))
            zr1b = EE(nc.sbuf_tensor("zr1b", [1, QS], BF16, side="right"))
            zrecf = EE(nc.sbuf_tensor("zrecf", [128, QS], F32, side="right"))
            attn_st = EE(nc.sbuf_tensor("attn_st", [128, QS], BF16, side="right"))
            outb_pc = EE(nc.sbuf_tensor("outb_pc_sb", [128, NCH], F32, side="right"))
            proj_t = EE(nc.sbuf_tensor("proj_t", [128, NCH * TS], BF16, side="right"))
            tmpf5 = [EE(nc.sbuf_tensor("tmpf50", [128, TS], F32, side="right"))] * 2
            xres = EE(nc.sbuf_tensor("xres", [128, NCH * TS], F32, side="right"))
            xn2_t = EE(nc.sbuf_tensor("xn2_t", [128, NCH * TS], BF16, side="right"))
            sq_all = EE(nc.sbuf_tensor("sq_all", [128, NCH * TS], BF16, side="right"))
            zop = EE(nc.psum_tensor("b5_zop", [128, QS], F32))
            ps_bc5p = EE(nc.psum_tensor("b5_bcp", [128, TS], F32))

            lob = S.dma("sync", outb_pc[:], outb_pc_in, slot="tap")

            tw_chunks = ([(tw1bf, tw1T_in, i * 512, 512) for i in range(16)] +
                         [(tw2bf, tw2T_in, i * 2048, 2048) for i in range(16)])
            t_twcast = None
            _twci = [0, 0]

            def emit_twcast(pace):
                if _twci[0] < len(tw_chunks):
                    dstb, srcb, r0, nr = tw_chunks[_twci[0]]
                    _twci[0] += 1
                    return S.dma("gpsimd", dstb[r0:r0 + nr, :], srcb[r0:r0 + nr, :],
                                 after=[pace], slot="twc")
                return None

            kwar = [None, None]
            qwar = [None, None]
            x2_spills = []
            xn2_spills = []
            prev_xres_user = None
            prev_xn2_user = None
            prev_n0 = prev_n1 = None
            prev_pa0 = []
            prev_pa1 = []
            prev_av0_tail = {}
            prev_av1_tail = {}
            prev_e0_tail = {-1: None, -2: None}
            prev_e1_tail = {-1: None, -2: None}
            prev_zbc = None
            pending_znorm = None
            zop_chain = [None]
            attnfm_war = [None, None]
            pb_bank_war = [None]
            sq_war = [None]
            pb_plan = None

            def make_phaseB(s, afm, attn_deps):
                st = {}

                def pmm_one(oc):
                    def go():
                        def pmm(t, oc=oc):
                            r = None
                            for hc in range(NCH):
                                r = t.matmul(zop[:, 0:TS],
                                             outT_sb[:, hc * D + oc * 128: hc * D + (oc + 1) * 128],
                                             afm[:, hc * QS: hc * QS + TS],
                                             start=(hc == 0), stop=(hc == NCH - 1))
                            return r
                        t_pm = S.add("tensor", pmm,
                                     after=attn_deps + [zop_chain[0]], tag="pmm")
                        def pevd(v, oc=oc):
                            return v.tensor_scalar(
                                out=proj_t[:, oc * TS:(oc + 1) * TS],
                                in0=zop[:, 0:TS], scalar1=outb_pc[:, oc:oc + 1],
                                scalar2=None, op0=ALU.add)
                        t_pe = S.add("vector", pevd, after=[t_pm, lob], tag="pevd")
                        zop_chain[0] = t_pe
                        st.setdefault("pev", []).append(t_pe)
                        if oc == 7:
                            attnfm_war[s % 2] = t_pm
                    return go

                def xres_load():
                    st["xr"] = S.dma("sync", xres.ap(),
                                     bass.AP(x_in.tensor, s * QS,
                                             [[NCH * L, 128], [L, NCH], [1, TS]]),
                                     after=[prev_xres_user], slot="xres", tag="xres")

                def sq_half(slc, h, dep_extra, key):
                    def sqf(v, h=h):
                        return [v.tensor_tensor(out=sq_all[:, c * TS:(c + 1) * TS],
                                                in0=slc(c), in1=slc(c), op=ALU.mult)
                                for c in range(4 * h, 4 * h + 4)][-1]
                    st.setdefault(key, []).append(
                        S.add("vector", sqf, after=dep_extra + [sq_war[0]], tag="ln_sq"))

                def stats_sum(slc, dep_extra):
                    def mm_sum(t):
                        r = None
                        for c in range(NCH):
                            r = t.matmul(ps_bc5p[0:1, 0:TS], ones_col_sb[:], slc(c),
                                         start=(c == 0), stop=(c == NCH - 1))
                        return r
                    return S.add("tensor", mm_sum,
                                 after=dep_extra + CONSTS + [pb_bank_war[0]], tag="ln_sum")

                def stats_sq(sq_deps):
                    def mmsq(t):
                        r = None
                        for c in range(NCH):
                            r = t.matmul(ps_bc5p[32:33, 0:TS], ones_col_sb[:],
                                         sq_all[:, c * TS:(c + 1) * TS],
                                         start=(c == 0), stop=(c == NCH - 1))
                        return r
                    tok = S.add("tensor", mmsq, after=sq_deps + [pb_bank_war[0]],
                                tag="ln_sqmm")
                    sq_war[0] = tok
                    return tok

                def ln_math_dve(t_sum, t_sq):
                    def rmath_a(v):
                        v.tensor_scalar(out=negm0[0:1, 0:TS], in0=ps_bc5p[0:1, 0:TS],
                                        scalar1=-1.0 / D, scalar2=None, op0=ALU.mult)
                        return v.tensor_scalar(out=scr0[0:1, 0:TS], in0=ps_bc5p[32:33, 0:TS],
                                               scalar1=1.0 / D, scalar2=None, op0=ALU.mult)
                    t_ra = S.add("vector", rmath_a, after=[t_sum, t_sq], tag="ln_ra")
                    t_rb = S.add("vector", lambda v: v.tensor_tensor(
                        out=scr1[0:1, 0:TS], in0=negm0[0:1, 0:TS], in1=negm0[0:1, 0:TS],
                        op=ALU.mult), after=[t_ra], tag="ln_rb")
                    t_rm = S.add("vector", lambda v: v.tensor_tensor(
                        out=scr0[0:1, 0:TS], in0=scr0[0:1, 0:TS], in1=scr1[0:1, 0:TS],
                        op=ALU.subtract), after=[t_rb], tag="ln_rc")
                    return t_ra, t_rm

                def ln_math_act(t_rm):
                    t_ln = S.add("scalar", lambda sc: sc.activation(
                        out=scr1[0:1, 0:TS], in_=scr0[0:1, 0:TS], func=AF.Ln,
                        bias=eps_sb[0:1, 0:1], scale=1.0), after=[t_rm], tag="ln_ln")
                    return S.add("scalar", lambda sc: sc.activation(
                        out=rstd0[0:1, 0:TS], in_=scr1[0:1, 0:TS], func=AF.Exp, scale=-0.5),
                        after=[t_ln], tag="ln_exp")

                def emit_bcn(t_ra):
                    def bcn(t):
                        return t.matmul(ps_bc5p[:, 0:TS], ones_rowf_sb[0:1, 0:128],
                                        negm0[0:1, 0:TS], start=True, stop=True)
                    return S.add("tensor", bcn, after=[t_ra], tag="ln_bcn")

                def ln_addpass(slc, t_bn):
                    toks = []
                    prev = None
                    for c0 in range(0, NCH, 2):
                        def ap_add(v, c0=c0):
                            return [v.tensor_tensor(out=slc(c), in0=slc(c),
                                                    in1=ps_bc5p[:, 0:TS], op=ALU.add)
                                    for c in (c0, c0 + 1)][-1]
                        prev = S.add("vector", ap_add, after=[t_bn, prev], tag="ap_add")
                        toks.append(prev)
                    return toks

                def ln_bcr(add_toks, t_rs):
                    def bcr(t):
                        return t.matmul(ps_bc5p[:, 0:TS], ones_rowf_sb[0:1, 0:128],
                                        rstd0[0:1, 0:TS], start=True, stop=True)
                    return S.add("tensor", bcr, after=add_toks + [t_rs], tag="ln_bcr")

                slc1 = lambda c: proj_t[:, c * TS:(c + 1) * TS]
                slc2 = lambda c: xn2_t[:, c * TS:(c + 1) * TS]

                def sq1h(h):
                    def go():
                        sq_half(slc1, h, st["pev"], "sq1")
                    return go

                def stats1_sum():
                    st["sum1"] = stats_sum(slc1, st["pev"])

                def stats1_sq():
                    st["sq1mm"] = stats_sq(st["sq1"])

                def ln1_math():
                    st["mathn"] = ln_math_dve(st["sum1"], st["sq1mm"])

                def ln1_mact():
                    st["rs1"] = ln_math_act(st["mathn"][1])

                def ln1_bcn():
                    st["bcn1"] = emit_bcn(st["mathn"][0])

                def ln1_add():
                    st["addp"] = ln_addpass(slc1, st["bcn1"])

                def ln1_bcr():
                    st["bcr1"] = ln_bcr(st["addp"], st["rs1"])

                def ln1_out():
                    t_br = st["bcr1"]
                    prev = None
                    toks = []
                    for c in range(NCH):
                        def ap_mul(v, c=c):
                            return v.tensor_tensor(out=tmpf5[0][:],
                                                   in0=proj_t[:, c * TS:(c + 1) * TS],
                                                   in1=ps_bc5p[:, 0:TS], op=ALU.mult)
                        t_m = S.add("vector", ap_mul, after=[t_br, prev, st["xr"]], tag="ap_mul")
                        def ap_out(v, c=c):
                            return v.tensor_tensor(out=xres[:, c * TS:(c + 1) * TS],
                                                   in0=tmpf5[0][:],
                                                   in1=xres[:, c * TS:(c + 1) * TS], op=ALU.add)
                        prev = S.add("vector", ap_out, after=[t_m], tag="ap_out")
                        toks.append(prev)
                    st["outp"] = toks
                    st["mul1"] = toks

                def x2_and_cp():
                    nonlocal prev_xres_user, prev_xn2_user
                    tok_sp = S.dma("sync",
                                   bass.AP(x2buf.tensor, s * QS,
                                           [[NCH * L, 128], [L, NCH], [1, TS]]),
                                   xres.ap(), after=st["outp"], slot="x2o", tag="x2spill")
                    x2_spills.append(tok_sp)
                    prev_xres_user = tok_sp
                    def cpf(v):
                        return [v.tensor_copy(xn2_t[:, c * TS:(c + 1) * TS],
                                              xres[:, c * TS:(c + 1) * TS])
                                for c in range(NCH)][-1]
                    st["cp"] = S.add("vector", cpf,
                                     after=st["outp"] + [prev_xn2_user], tag="xn2cp")

                def sq2h(h):
                    def go():
                        sq_half(slc2, h, [st["cp"]], "sq2")
                    return go

                def stats2_sum():
                    st["sum2"] = stats_sum(slc2, [st["cp"]] + st["mul1"])

                def stats2_sq():
                    st["sq2mm"] = stats_sq(st["sq2"])

                def ln2_math():
                    st["mathn2"] = ln_math_dve(st["sum2"], st["sq2mm"])

                def ln2_mact():
                    st["rs2"] = ln_math_act(st["mathn2"][1])

                def ln2_bcn():
                    st["bcn2"] = emit_bcn(st["mathn2"][0])

                def ln2_add():
                    st["addp2"] = ln_addpass(slc2, st["bcn2"])

                def ln2_out():
                    nonlocal prev_xn2_user
                    t_br = ln_bcr(st["addp2"], st["rs2"])
                    prev = None
                    mul_toks = []
                    for c0 in range(0, NCH, 2):
                        def ap_mul2(v, c0=c0):
                            return [v.tensor_tensor(out=slc2(c), in0=slc2(c),
                                                    in1=ps_bc5p[:, 0:TS], op=ALU.mult)
                                    for c in (c0, c0 + 1)][-1]
                        prev = S.add("vector", ap_mul2, after=[t_br, prev], tag="ap_mul2")
                        mul_toks.append(prev)
                    pb_bank_war[0] = prev
                    tok_sp = S.dma("sync",
                                   bass.AP(xn2buf.tensor, s * QS,
                                           [[NCH * L, 128], [L, NCH], [1, TS]]),
                                   xn2_t.ap(), after=mul_toks, slot="xn2o", tag="xn2spill")
                    xn2_spills.append(tok_sp)
                    prev_xn2_user = tok_sp

                return {(0, 8): [pmm_one(0)], (0, 20): [pmm_one(1)],
                        (1, 4): [pmm_one(2)], (1, 16): [pmm_one(3)],
                        (2, 4): [pmm_one(4)], (2, 16): [pmm_one(5)],
                        (3, 4): [pmm_one(6)], (3, 16): [pmm_one(7)],
                        (3, 24): [xres_load],
                        (4, 4): [sq1h(0)], (4, 12): [sq1h(1)],
                        (4, 20): [stats1_sum], (4, 26): [stats1_sq],
                        (5, 2): [ln1_math], (5, 8): [ln1_mact],
                        (5, 12): [ln1_bcn], (5, 18): [ln1_add],
                        (6, 4): [ln1_bcr], (6, 10): [ln1_out], (6, 22): [x2_and_cp],
                        (7, 2): [sq2h(0)], (7, 4): [sq2h(1)],
                        (7, 8): [stats2_sum], (7, 10): [stats2_sq],
                        (7, 14): [ln2_math], (7, 18): [ln2_mact],
                        (7, 20): [ln2_bcn], (7, 22): [ln2_add],
                        (7, 26): [ln2_out]}

            for sidx in range(NQS):
                with ExitStack() as stA:
                    PA = stA.enter_context
                    psS0 = [PA(nc.psum_tensor("b5_psA%d_%d" % (i, sidx), [128, QS], F32)) for i in range(2)]
                    psS1 = [PA(nc.psum_tensor("b5_psB%d_%d" % (i, sidx), [128, QS], F32)) for i in range(2)]
                    psOA = PA(nc.psum_tensor("b5_psOA_%d" % sidx, [65, QS], F32))
                    psOB = PA(nc.psum_tensor("b5_psOB_%d" % sidx, [65, QS], F32))

                    afm_cur = attn_fm[sidx % 2]
                    qsb = qsuper[sidx % 2]
                    tok_q = S.dma("sync", qsb.ap(),
                                  bass.AP(qbuf.tensor, sidx * QS,
                                          [[NCH * L, 128], [L, NCH], [1, QS]]),
                                  after=[qwar[sidx % 2]], slot="qsup%d" % (sidx % 2),
                                  tag="qload")
                    attn_toks = []
                    last_qk_super = None

                    def flush_znorm():
                        nonlocal pending_znorm, prev_zbc
                        nonlocal prev_n0, prev_n1
                        zpc, t_zc0, t_zc1, t_evA, t_evB, zafm, zpar, ztoks = pending_znorm
                        pending_znorm = None

                        def zbc(t):
                            t.matmul(zop[0:64, 0:QS], ones_row_sb[0:1, 0:64],
                                     zr0b[0:1, 0:QS], start=True, stop=True)
                            return t.matmul(zop[64:128, 0:QS], ones_row_sb[0:1, 0:64],
                                            zr1b[0:1, 0:QS], start=True, stop=True)
                        t_zbc = S.add("tensor", zbc, after=[t_zc0, t_zc1, zop_chain[0]],
                                      tag="zbc")
                        t_zrec = S.add("vector",
                                       lambda v: v.reciprocal(zrecf[:, 0:QS], zop[:, 0:QS]),
                                       after=[t_zbc, prev_n0, prev_n1],
                                       tag="zrec")
                        zop_chain[0] = t_zrec

                        def nrm0(v, zpc=zpc, afm=zafm):
                            return v.tensor_tensor(out=afm[0:64, zpc * QS:(zpc + 1) * QS],
                                                   in0=attn_st[0:64, 0:QS],
                                                   in1=zrecf[0:64, 0:QS], op=ALU.mult)
                        t_n0 = S.add("vector", nrm0,
                                     after=[t_zrec, t_evA, attnfm_war[zpar]], tag="nrm0")

                        def nrm1(v, zpc=zpc, afm=zafm):
                            return v.tensor_tensor(out=afm[64:128, zpc * QS:(zpc + 1) * QS],
                                                   in0=attn_st[64:128, 0:QS],
                                                   in1=zrecf[64:128, 0:QS], op=ALU.mult)
                        t_n1 = S.add("vector", nrm1,
                                     after=[t_zrec, t_evB, attnfm_war[zpar]], tag="nrm1")
                        ztoks.extend([t_n0, t_n1])
                        prev_zbc = t_zbc
                        prev_n0, prev_n1 = t_n0, t_n1

                    for pc in range(NCH):
                        h0, h1 = 2 * pc, 2 * pc + 1
                        tok_k = S.dma("sync", khp[pc % 2].ap(),
                                      kbuf[:, pc * L:(pc + 1) * L],
                                      after=[kwar[pc % 2]],
                                      slot="khp%d" % (pc % 2), tag="kload")
                        tok_exp0 = dict(prev_e0_tail)
                        tok_exp1 = dict(prev_e1_tail)
                        tok_av0 = dict(prev_av0_tail)
                        tok_av1 = dict(prev_av1_tail)
                        tok_qk0 = {}
                        tok_qk1 = {}
                        def qk0(t, pc=pc, kt=0, qsb=qsb):
                            return t.matmul(psS0[kt % 2][:, 0:QS],
                                            khp[pc % 2][0:64, kt * 128:(kt + 1) * 128],
                                            qsb[0:64, pc * QS: (pc + 1) * QS],
                                            start=True, stop=True)
                        def qk1(t, pc=pc, kt=0, qsb=qsb):
                            return t.matmul(psS1[kt % 2][:, 0:QS],
                                            khp[pc % 2][64:128, kt * 128:(kt + 1) * 128],
                                            qsb[64:128, pc * QS: (pc + 1) * QS],
                                            start=True, stop=True)
                        for kt0 in range(2):
                            tok_qk0[kt0] = S.add("tensor", (lambda t, kt=kt0, f=qk0: f(t, kt=kt)),
                                                 after=[tok_q, tok_k, tok_exp0[kt0 - 2]], tag="qk0")
                            tok_qk1[kt0] = S.add("tensor", (lambda t, kt=kt0, f=qk1: f(t, kt=kt)),
                                                 after=[tok_q, tok_k, tok_exp1[kt0 - 2]], tag="qk1")
                        if pending_znorm is not None:
                            flush_znorm()
                        _twci[1] += 1
                        if _twci[1] % 2 == 0:
                            tcast = emit_twcast(tok_k)
                            if tcast is not None:
                                t_twcast = tcast
                        for kt in range(NT):
                            if pb_plan is not None:
                                for f in pb_plan.get((pc, kt), []):
                                    f()
                            def ex0(sc, kt=kt, h0=h0):
                                return sc.activation(out=probs0[kt % 2][:], in_=psS0[kt % 2][:, 0:QS],
                                                     func=AF.Exp, bias=mbias[:, h0:h0 + 1], scale=1.0)
                            tok_exp0[kt] = S.add("scalar", ex0,
                                                 after=[tok_qk0[kt], tok_av0.get(kt - 2)], tag="exp0")
                            def ex1(sc, kt=kt, h1=h1):
                                return sc.activation(out=probs1[kt % 2][:], in_=psS1[kt % 2][:, 0:QS],
                                                     func=AF.Exp, bias=mbias[:, h1:h1 + 1], scale=1.0)
                            tok_exp1[kt] = S.add("scalar", ex1,
                                                 after=[tok_qk1[kt], tok_av1.get(kt - 2)], tag="exp1")
                            if kt + 2 < NT:
                                tok_qk0[kt + 2] = S.add("tensor", (lambda t, kt=kt, f=qk0: f(t, kt=kt + 2)),
                                                        after=[tok_exp0[kt]], tag="qk0")
                                tok_qk1[kt + 2] = S.add("tensor", (lambda t, kt=kt, f=qk1: f(t, kt=kt + 2)),
                                                        after=[tok_exp1[kt]], tag="qk1")
                            def av0(t, kt=kt, h0=h0):
                                return t.matmul(psOA[:, 0:QS],
                                                vaug[:, kt * 1040 + h0 * 65: kt * 1040 + h0 * 65 + 65],
                                                probs0[kt % 2][:, 0:QS],
                                                start=(kt == 0), stop=(kt == NT - 1))
                            tok_av0[kt] = S.add("tensor", av0,
                                after=[tok_exp0[kt]] + (prev_pa0 if kt == 0 else []), tag="av0")
                            def av1(t, kt=kt, h1=h1):
                                return t.matmul(psOB[:, 0:QS],
                                                vaug[:, kt * 1040 + h1 * 65: kt * 1040 + h1 * 65 + 65],
                                                probs1[kt % 2][:, 0:QS],
                                                start=(kt == 0), stop=(kt == NT - 1))
                            tok_av1[kt] = S.add("tensor", av1,
                                after=[tok_exp1[kt]] + (prev_pa1 if kt == 0 else []), tag="av1")
                        last_qk_super = tok_qk1[NT - 1]
                        kwar[pc % 2] = last_qk_super
                        t_zc0 = S.add("vector",
                                      lambda v: v.tensor_copy(zr0b[0:1, 0:QS], psOA[64:65, 0:QS]),
                                      after=[tok_av0[NT - 1], prev_zbc], tag="zc0")
                        t_evA = S.add("vector",
                                      lambda v: v.tensor_copy(attn_st[0:64, 0:QS], psOA[0:64, 0:QS]),
                                      after=[tok_av0[NT - 1], prev_n0], tag="evA")
                        t_zc1 = S.add("vector",
                                      lambda v: v.tensor_copy(zr1b[0:1, 0:QS], psOB[64:65, 0:QS]),
                                      after=[tok_av1[NT - 1], prev_zbc], tag="zc1")
                        t_evB = S.add("vector",
                                      lambda v: v.tensor_copy(attn_st[64:128, 0:QS], psOB[0:64, 0:QS]),
                                      after=[tok_av1[NT - 1], prev_n1], tag="evB")
                        pending_znorm = (pc, t_zc0, t_zc1, t_evA, t_evB,
                                         afm_cur, sidx % 2, attn_toks)
                        prev_pa0 = [t_zc0, t_evA]
                        prev_pa1 = [t_zc1, t_evB]
                        prev_e0_tail = {-1: tok_exp0[NT - 1], -2: tok_exp0[NT - 2]}
                        prev_e1_tail = {-1: tok_exp1[NT - 1], -2: tok_exp1[NT - 2]}
                        prev_av0_tail = {-2: tok_av0[NT - 2], -1: tok_av0[NT - 1]}
                        prev_av1_tail = {-2: tok_av1[NT - 2], -1: tok_av1[NT - 1]}
                    qwar[sidx % 2] = last_qk_super
                    pb_plan = make_phaseB(sidx, afm_cur, attn_toks)
            flush_znorm()
            for key in sorted(pb_plan.keys()):
                for f in pb_plan[key]:
                    f()
            S.emit_block()
        if "x2" in taps:
            S.dma("sync", tap_outs["x2"], x2buf, after=[], slot="tap")
        if "xn2" in taps:
            S.dma("sync", tap_outs["xn2"], xn2buf, after=[], slot="tap")
        S.emit_block()
        if stop_after <= 6:
            return nc, tap_outs

        outT_cm.__exit__(None, None, None)
        vaug_cm.__exit__(None, None, None)
        for _cm in reversed(_tau3_cms):
            _cm.__exit__(None, None, None)
        for _cm in reversed(_tau_cms):
            _cm.__exit__(None, None, None)
        mw1_cm = nc.sbuf_tensor("mW1T", [128, NCH * FF], BF16, side="left")
        mW1T = mw1_cm.__enter__()
        mw2_cm = nc.sbuf_tensor("mW2T", [128, 32 * D], BF16, side="left")
        mW2T = mw2_cm.__enter__()
        with ExitStack() as st6:
            EE = st6.enter_context
            gw1T_sb = EE(nc.sbuf_tensor("gw1T_sb", [128, NCH * D], BF16, side="right"))
            gw2T_sb = EE(nc.sbuf_tensor("gw2T_sb", [128, NCH * E], BF16, side="right"))
            gb1_sb = EE(nc.sbuf_tensor("gb1_sb", [128, NCH], F32, side="right"))
            gb2_sb = EE(nc.sbuf_tensor("gb2_sb", [E, 1], F32, side="right"))
            xn2s = EE(nc.sbuf_tensor("g_xn2s", [128, NCH * TS], BF16, side="right"))
            h_sb = [EE(nc.sbuf_tensor("g_h0", [128, TS], BF16, side="right"))] * 2
            tbs = [EE(nc.sbuf_tensor("g_tbs%d" % i, [E, 128], BF16, side="right")) for i in range(2)]
            b1pc_sb = EE(nc.sbuf_tensor("b1pc_sb", [128, 32], F32, side="right"))
            b2pc_sb = EE(nc.sbuf_tensor("b2pc_sb", [128, 8], F32, side="right"))
            acc6 = EE(nc.sbuf_tensor("acc6", [128, FF], F32, side="right"))
            twb = [EE(nc.sbuf_tensor("twb%d" % i, [128, FF // 2], BF16, side="right"))
                   for i in range(3)]
            ps_h = [EE(nc.psum_tensor("g_psh%d" % i, [128, TS], F32)) for i in range(2)]
            ps_g2 = EE(nc.psum_tensor("g_psg2", [E, TS], F32))
            ps_gr = EE(nc.psum_tensor("ps_gr", [1, E], F32))
            ps_gb = EE(nc.psum_tensor("ps_gb", [128, E], F32))
            ps_mb = EE(nc.psum_tensor("ps_mb", [128, 40], F32))

            lg = []
            for cc in range(NCH):
                lg.append(S.dma("gpsimd", gw1T_sb[:, cc * D:(cc + 1) * D],
                                gw1T_in[cc * 128:(cc + 1) * 128, :], slot="gw"))
                lg.append(S.dma("gpsimd", gw2T_sb[:, cc * E:(cc + 1) * E],
                                gw2T_in[cc * 128:(cc + 1) * 128, :], slot="gw"))
            lg.append(S.dma("sync", gb1_sb[:], gb1_pc_in, slot="gwf"))
            lg.append(S.dma("sync", gb2_sb[:], gb2_col_in, slot="gwf"))
            lb1 = S.dma("sync", b1pc_sb[:, 0:32], b1pc_in, slot="gwf")
            lb2 = S.dma("sync", b2pc_sb[:, 0:8], b2pc_in, slot="gwf")

            prev_he = [None, None]
            prev_g2 = None
            prev_acc = None
            prev_xl = None
            for tt in range(NTS):
                tok_x = S.dma("sync", xn2s.ap(),
                              bass.AP(xn2buf.tensor, tt * TS,
                                      [[NCH * L, 128], [L, NCH], [1, TS]]),
                              after=[prev_xl], slot="gxn", tag="gxn")
                last_g2 = None
                last_h = None
                for oc in range(NCH):
                    bi = oc % 2
                    def hmm_(t, oc=oc, bi=bi):
                        r = None
                        for ic in range(NCH):
                            r = t.matmul(ps_h[bi][:, 0:TS],
                                         gw1T_sb[:, ic * D + oc * 128: ic * D + (oc + 1) * 128],
                                         xn2s[:, ic * TS:(ic + 1) * TS],
                                         start=(ic == 0), stop=(ic == NCH - 1))
                        return r
                    t_h = S.add("tensor", hmm_, after=lg + [tok_x, prev_he[bi]], tag="gh")
                    last_h = t_h
                    def hev(sc, bi=bi, oc=oc):
                        return sc.activation(out=h_sb[bi][:], in_=ps_h[bi][:, 0:TS],
                                             func=AF.Relu, bias=gb1_sb[:, oc:oc + 1], scale=1.0)
                    t_he = S.add("scalar", hev, after=[t_h, last_g2, prev_g2], tag="ghe")
                    def g2mm(t, oc=oc, bi=bi):
                        return t.matmul(ps_g2[0:E, 0:TS], gw2T_sb[:, oc * E:(oc + 1) * E],
                                        h_sb[bi][:, 0:TS], start=(oc == 0), stop=(oc == NCH - 1))
                    last_g2 = S.add("tensor", g2mm, after=[t_he, prev_g2], tag="g2mm")
                    prev_he[bi] = last_g2
                prev_xl = last_h
                t_red = S.add("vector", lambda v: v.tensor_reduce(
                    out=g_sb[0:E, 1:2], in_=ps_g2[0:E, 0:TS], axis=AX.X, op=ALU.add),
                    after=[last_g2], tag="gred")
                prev_acc = S.add("vector", lambda v: v.tensor_tensor(
                    out=g_sb[0:E, 0:1], in0=g_sb[0:E, 0:1], in1=g_sb[0:E, 1:2], op=ALU.add),
                    after=[t_red], tag="gacc")
                prev_g2 = prev_acc
            t_gs = S.add("vector", lambda v: v.tensor_scalar(
                out=g_sb[0:E, 0:1], in0=g_sb[0:E, 0:1], scalar1=1.0 / L, scalar2=None,
                op0=ALU.mult), after=[prev_acc], tag="gscale")
            t_gb = S.add("vector", lambda v: v.tensor_tensor(
                out=g_sb[0:E, 0:1], in0=g_sb[0:E, 0:1], in1=gb2_sb[0:E, 0:1], op=ALU.add),
                after=[t_gs, lg[-1]], tag="gbias")
            t_gbf = S.add("vector", lambda v: v.tensor_copy(g_bf[0:E, 0:1], g_sb[0:E, 0:1]),
                          after=[t_gb], tag="gbf")
            t_grt = S.add("tensor", lambda t: t.transpose(ps_gr[0:1, 0:E], g_sb[0:E, 0:1],
                                                          ident_sb[0:E, 0:E]),
                          after=[t_gb], tag="grt")
            t_growb = S.add("vector", lambda v: v.tensor_copy(grow_bf[0:1, 0:E], ps_gr[0:1, 0:E]),
                            after=[t_grt], tag="growb")
            if "g" in taps:
                S.dma("sync", tap_outs["g"], g_sb[0:E, 0:2], after=[t_gb], slot="tap")

            def gbc(t):
                r = None
                for e in range(E):
                    r = t.matmul(ps_gb[:, e:e + 1], ones_row_sb[0:1, 0:128],
                                 grow_bf[0:1, e:e + 1], start=True, stop=True)
                return r
            t_gbc = S.add("tensor", gbc, after=[t_growb], tag="gbc")
            t_gbce = S.add("vector", lambda v: v.tensor_copy(g_bc[:], ps_gb[:]),
                           after=[t_gbc], tag="gbce")
            if "gbc" in taps:
                S.dma("sync", tap_outs["gbc"], g_bc.ap(), after=[t_gbce], slot="tap")

            for nm, src_in, bpc, npc, cbase, dstpc, lb in (
                    ("mb1", tb1_in, b1pc_sb, 32, 0, mb1_pc, lb1),
                    ("mb2", tb2_in, b2pc_sb, 8, 32, mb2_pc, lb2)):
                mb_toks = []
                prev_tb = [None, None]
                for cf in range(npc):
                    ltb = S.dma("gpsimd", tbs[cf % 2][0:E, 0:128],
                                src_in[0:E, cf * 128:(cf + 1) * 128],
                                after=[prev_tb[cf % 2]], slot="mbld")
                    def mbm(t, cf=cf, cbase=cbase):
                        return t.matmul(ps_mb[:, cbase + cf: cbase + cf + 1],
                                        tbs[cf % 2][0:E, 0:128], g_bf[0:E, 0:1],
                                        start=True, stop=True)
                    t_m = S.add("tensor", mbm, after=[ltb, t_gbf], tag="mbm")
                    prev_tb[cf % 2] = t_m
                    mb_toks.append(t_m)
                def mbe(v, dstpc=dstpc, bpc=bpc, npc=npc, cbase=cbase):
                    return v.tensor_tensor(out=dstpc[:, 0:npc],
                                           in0=ps_mb[:, cbase:cbase + npc],
                                           in1=bpc[:, 0:npc], op=ALU.add)
                t_te = S.add("vector", mbe, after=mb_toks + [lb], tag="mbe")
                if nm == "mb1" and "mb1pc" in taps:
                    S.dma("sync", tap_outs["mb1pc"], mb1_pc.ap(), after=[t_te], slot="tap")
                if nm == "mb2" and "mb2pc" in taps:
                    S.dma("sync", tap_outs["mb2pc"], mb2_pc.ap(), after=[t_te], slot="tap")

            def combine(dst, base_in, tw_in, nch_out, width, tag, seed=None):
                prev_cast = seed
                grp = FF // width

                def twload(j, ch, war):
                    e, hf = j // 2, j % 2
                    hw = grp * width // 2
                    if grp == 1:
                        src = bass.AP(tw_in.tensor,
                                      (e * nch_out * 128 + ch * 128) * width + hf * hw,
                                      [[width, 128], [1, hw]])
                    else:
                        src = bass.AP(tw_in.tensor,
                                      (e * nch_out * 128 + (ch + hf * (grp // 2)) * 128) * width,
                                      [[width, 128], [128 * width, grp // 2], [1, width]])
                    return S.dma("gpsimd", twb[j % 3][:, 0:hw], src,
                                 after=[war, t_twcast], slot="tw%d" % (j % 3), tag="tw")

                def wload(ch, hf, war):
                    hw = grp * width // 2
                    if grp == 1:
                        src = bass.AP(base_in.tensor, ch * 128 * width + hf * hw,
                                      [[width, 128], [1, hw]])
                    else:
                        src = bass.AP(base_in.tensor,
                                      (ch + hf * (grp // 2)) * 128 * width,
                                      [[width, 128], [128 * width, grp // 2], [1, width]])
                    return S.dma("sync", acc6[:, hf * hw:(hf + 1) * hw], src,
                                 after=[war], slot="wbase%d" % hf, tag="wb")

                pend = {}
                gw = None
                for ch in range(0, nch_out, grp):
                    gw = grp * width
                    hw = gw // 2
                    tok_w = [wload(ch, 0, prev_cast), wload(ch, 1, prev_cast)]
                    if ch == 0:
                        for j0 in range(3):
                            pend[j0] = twload(j0, ch, prev_cast)
                    lastH = [None, None]
                    lastSlot = {}
                    for j in range(2 * E):
                        e, hf = j // 2, j % 2
                        def stt(v, e=e, hf=hf, j=j, w=hw):
                            return v.scalar_tensor_tensor(
                                out=acc6[:, hf * w:(hf + 1) * w],
                                in0=twb[j % 3][:, 0:w],
                                scalar=g_bc[:, e:e + 1],
                                in1=acc6[:, hf * w:(hf + 1) * w],
                                op0=ALU.mult, op1=ALU.add)
                        deps = [pend[j % 3], lastH[hf] if e else tok_w[hf], t_gbce]
                        lastH[hf] = S.add("vector", stt, after=deps, tag="stt")
                        lastSlot[j % 3] = lastH[hf]
                        if j + 3 < 2 * E:
                            pend[j % 3] = twload(j + 3, ch, lastH[hf])
                    if ch + grp < nch_out:
                        for j0 in range(3):
                            pend[j0] = twload(j0, ch + grp, lastSlot[j0])
                    def cast(sc, ch=ch, dst=dst, w=gw):
                        return sc.copy(out=dst[:, ch * width: ch * width + w], in_=acc6[:, 0:w])
                    prev_cast = S.add("scalar", cast, after=lastH, tag="ccast")
                return prev_cast

            c1_done = combine(mW1T, w1T_in, tw1bf, NCH, FF, "c1")
            c2_done = combine(mW2T, w2T_in, tw2bf, 32, D, "c2", seed=c1_done)
            if "mw1t" in taps:
                S.dma("sync", tap_outs["mw1t"], mW1T.ap(), after=[c1_done], slot="tap")
            if "mw2t" in taps:
                S.dma("sync", tap_outs["mw2t"], mW2T.ap(), after=[c2_done], slot="tap")
            S.emit_block()
        if stop_after <= 7:
            mw2_cm.__exit__(None, None, None)
            mw1_cm.__exit__(None, None, None)
            return nc, tap_outs

        with ExitStack() as st7:
            EE = st7.enter_context
            u_sb = EE(nc.sbuf_tensor("u_sb", [128, 32 * TS], BF16, side="right"))
            xn2s7 = [EE(nc.sbuf_tensor("f_xn2s%d" % i, [128, NCH * TS], BF16, side="right"))
                     for i in range(2)]
            x2r = EE(nc.sbuf_tensor("f_x2r", [128, 2 * TS], F32, side="right"))
            fstage = EE(nc.sbuf_tensor("fstage", [128, NCH * TS], BF16, side="right"))
            tmpf7 = [EE(nc.sbuf_tensor("tmpf7%d" % i, [128, TS], F32, side="right")) for i in range(2)]
            sq7 = [EE(nc.sbuf_tensor("sq70", [128, 2 * TS], BF16, side="right"))] * 2
            ps_u = [EE(nc.psum_tensor("f_psu%d" % i, [128, TS], F32)) for i in range(2)]
            ps_o = [EE(nc.psum_tensor("f_pso%d" % i, [128, TS], F32)) for i in range(2)]
            ps_sum7 = EE(nc.psum_tensor("f_sum", [1, TS], F32))
            ps_sq7 = EE(nc.psum_tensor("f_sq", [1, TS], F32))
            ps_bc7 = EE(nc.psum_tensor("f_bc", [128, 2 * TS], F32))
            ln7 = make_ln(sq7, ps_sum7, ps_sq7, ps_bc7)

            prev_xl7 = [None, None]
            prev_ap7, prev_st7 = None, None
            prev_ue = [None, None]
            prev_oe = [None, None]
            ring_war = [None, None]
            for tt in range(NTS):
                tok_xn2 = S.dma("gpsimd", xn2s7[tt % 2].ap(),
                                bass.AP(xn2buf.tensor, tt * TS,
                                        [[NCH * L, 128], [L, NCH], [1, TS]]),
                                after=[prev_xl7[tt % 2]], slot="fxng%d" % (tt % 2), tag="fxn")
                u_toks = []
                last_umm = None
                for cf in range(32):
                    bi = cf % 2
                    def umm(t, cf=cf, tt=tt, bi=bi):
                        r = None
                        for ch in range(NCH):
                            r = t.matmul(ps_u[bi][:, 0:TS],
                                         mW1T[:, ch * FF + cf * 128: ch * FF + (cf + 1) * 128],
                                         xn2s7[tt % 2][:, ch * TS:(ch + 1) * TS],
                                         start=(ch == 0), stop=(ch == NCH - 1))
                        return r
                    t_u = S.add("tensor", umm, after=[tok_xn2, prev_ue[bi]], tag="umm")
                    last_umm = t_u
                    def uev(sc, cf=cf, bi=bi):
                        return sc.activation(out=u_sb[:, cf * TS:(cf + 1) * TS],
                                             in_=ps_u[bi][:, 0:TS], func=AF_GELUTANH,
                                             bias=mb1_pc[:, cf:cf + 1], scale=1.0)
                    t_ue = S.add("scalar", uev, after=[t_u], tag="uev")
                    prev_ue[bi] = t_ue
                    u_toks.append(t_ue)
                prev_xl7[tt % 2] = last_umm
                o_toks = []
                for ch in range(NCH):
                    bi = ch % 2
                    def omm(t, ch=ch, bi=bi):
                        r = None
                        for cf in range(32):
                            r = t.matmul(ps_o[bi][:, 0:TS],
                                         mW2T[:, cf * D + ch * 128: cf * D + (ch + 1) * 128],
                                         u_sb[:, cf * TS:(cf + 1) * TS],
                                         start=(cf == 0), stop=(cf == 31))
                        return r
                    t_o = S.add("tensor", omm, after=u_toks + [prev_oe[bi]], tag="omm")
                    def oev(sc, ch=ch, bi=bi):
                        return sc.activation(out=fstage[:, ch * TS:(ch + 1) * TS],
                                             in_=ps_o[bi][:, 0:TS], func=AF.Identity,
                                             bias=mb2_pc[:, ch:ch + 1], scale=1.0)
                    t_oe = S.add("scalar", oev, after=[t_o], tag="oev")
                    prev_oe[bi] = t_oe
                    o_toks.append(t_oe)
                slices = [fstage[:, c * TS:(c + 1) * TS] for c in range(NCH)]
                resid = [x2r[:, (c % 2) * TS:(c % 2 + 1) * TS] for c in range(NCH)]
                r_loads = {}

                def load_x2(c, tt=tt, r_loads=r_loads):
                    if c not in r_loads and c < NCH:
                        r_loads[c] = S.dma(
                            "sync", x2r[:, (c % 2) * TS:(c % 2 + 1) * TS],
                            bass.AP(x2buf.tensor, c * L + tt * TS, [[NCH * L, 128], [1, TS]]),
                            after=[ring_war[c % 2]], slot="fx2", tag="fx2")
                    return r_loads.get(c)

                def pair_extra7(c0):
                    return [load_x2(c0), load_x2(c0 + 1)]

                def on_pair7(c0, tok_ap, tt=tt, r_loads=r_loads):
                    tok_y = S.dma("sync",
                                  bass.AP(y_dev.tensor, c0 * L + tt * TS,
                                          [[NCH * L, 128], [L, 2], [1, TS]]),
                                  x2r.ap(), after=[tok_ap], slot="yout", tag="yout")
                    ring_war[0] = ring_war[1] = tok_y
                    load_x2(c0 + 2)
                    load_x2(c0 + 3)

                prev_ap7, prev_st7 = ln7(slices, resid_slices=resid, out_slices=resid,
                                         after=o_toks,
                                         prev_apply=prev_ap7, prev_stats=prev_st7,
                                         tmpf=tmpf7, pair_extra=pair_extra7,
                                         on_pair=on_pair7)
            S.emit_block()
        mw2_cm.__exit__(None, None, None)
        mw1_cm.__exit__(None, None, None)
    return nc, tap_outs


def prep_core_inputs(inputs):
    import ml_dtypes
    bf = ml_dtypes.bfloat16
    A = lambda v: np.ascontiguousarray(np.asarray(v), dtype=np.float32)

    def pc(v):
        return A(np.asarray(v).reshape(NCH, 128).T)

    i = inputs
    sh = {}
    sh["wq_nat"] = A(i["L0_attn_Wq_w"])
    sh["wk_nat"] = A(i["L0_attn_Wk_w"])
    sh["cqT"] = A(np.asarray(i["L0_attn_convq_w"]).transpose(2, 0, 1).reshape(3 * D, D))
    sh["ckT"] = A(np.asarray(i["L0_attn_convk_w"]).transpose(2, 0, 1).reshape(3 * D, D))
    qp = np.asarray(i["L0_attn_qproj_w"])
    kp = np.asarray(i["L0_attn_kproj_w"])
    sh["qpA_T"] = A(qp[:, :D].T)
    sh["qpB_T"] = A(qp[:, D:].T)
    sh["kpA_T"] = A(kp[:, :D].T)
    sh["kpB_T"] = A(kp[:, D:].T)
    sh["wvT"] = A(np.asarray(i["L0_attn_Wv_w"]).T)
    sh["outT"] = A(np.asarray(i["L0_attn_out_w"]).T)
    sh["gw1T"] = A(np.asarray(i["L0_ffn_gate_w1"]).T)
    sh["gw2T"] = A(np.asarray(i["L0_ffn_gate_w2"]).T)
    sh["w1T"] = A(np.asarray(i["L0_ffn_W1"]).T)
    sh["w2T"] = A(np.asarray(i["L0_ffn_W2"]).T)
    sh["tw1T"] = A(np.asarray(i["L0_ffn_TW1"]).transpose(0, 2, 1).reshape(E * D, FF))
    sh["tw2T"] = A(np.asarray(i["L0_ffn_TW2"]).transpose(0, 2, 1).reshape(E * FF, D))
    sh["tb1"] = A(i["L0_ffn_Tb1"])
    sh["tb2"] = A(i["L0_ffn_Tb2"])
    sh["b1row"] = A(np.asarray(i["L0_ffn_b1"])[None, :])
    sh["b2row"] = A(np.asarray(i["L0_ffn_b2"])[None, :])
    sh["b1pc"] = A(np.asarray(i["L0_ffn_b1"]).reshape(32, 128).T)
    sh["b2pc"] = A(np.asarray(i["L0_ffn_b2"]).reshape(8, 128).T)
    sh["wqb_pc"] = pc(i["L0_attn_Wq_b"])
    sh["wkb_pc"] = pc(i["L0_attn_Wk_b"])
    sh["qpb_pc"] = pc(i["L0_attn_qproj_b"])
    sh["kpb_pc"] = pc(i["L0_attn_kproj_b"])
    sh["cqb_pc"] = pc(i["L0_attn_convq_b"])
    sh["ckb_pc"] = pc(i["L0_attn_convk_b"])
    sh["outb_pc"] = pc(i["L0_attn_out_b"])
    sh["gb1_pc"] = pc(i["L0_ffn_gate_b1"])
    sh["gb2_col"] = A(np.asarray(i["L0_ffn_gate_b2"])[:, None])
    sh["wvb_row"] = A(np.asarray(i["L0_attn_Wv_b"])[None, :])
    t1w = np.asarray(i["L0_attn_tau1_w"])[:, 0, :]
    d1w = np.asarray(i["L0_attn_del1_w"])[:, 0, :]
    sh["tau1w_pc"] = A(t1w.reshape(NCH, 128, 2, 3).transpose(1, 0, 2, 3).reshape(128, NCH * 6))
    sh["del1w_pc"] = A(d1w.reshape(NCH, 128, 2, 3).transpose(1, 0, 2, 3).reshape(128, NCH * 6))
    sh["tau1b_pc"] = A(np.asarray(i["L0_attn_tau1_b"]).reshape(NCH, 128, 2).transpose(1, 0, 2).reshape(128, NCH * 2))
    sh["del1b_pc"] = A(np.asarray(i["L0_attn_del1_b"]).reshape(NCH, 128, 2).transpose(1, 0, 2).reshape(128, NCH * 2))
    t2w = np.asarray(i["L0_attn_tau2_w"])[:, :, 0]
    d2w = np.asarray(i["L0_attn_del2_w"])[:, :, 0]
    sh["tau2wT"] = A(t2w.T.reshape(NCH, 128, 2, H).transpose(1, 0, 2, 3).reshape(128, 16 * H))
    sh["del2wT"] = A(d2w.T.reshape(NCH, 128, 2, H).transpose(1, 0, 2, 3).reshape(128, 16 * H))
    sh["tau2b_row"] = A(np.asarray(i["L0_attn_tau2_b"])[None, :])
    sh["del2b_row"] = A(np.asarray(i["L0_attn_del2_b"])[None, :])
    sh["ones_row"] = np.ones((1, 128), dtype=bf)
    sh["ones_rowf"] = np.ones((1, 128), dtype=np.float32)
    sh["ones_col"] = np.ones((128, 1), dtype=bf)
    o2 = np.zeros((2, 256), dtype=bf)
    o2[0, :128] = 1
    o2[1, 128:] = 1
    sh["oneh2"] = o2
    o16 = np.zeros((H, H * DKH), dtype=bf)
    for h in range(H):
        o16[h, h * DKH:(h + 1) * DKH] = 1
    sh["oneh16"] = o16
    sh["ident_f32"] = np.eye(128, dtype=np.float32)

    x = np.asarray(i["L0_attn_x"], dtype=np.float32)
    cores = []
    for b in range(x.shape[0]):
        d = dict(sh)
        d["x_fm"] = A(x[b].reshape(L, NCH, 128).transpose(2, 1, 0).reshape(128, NCH * L))
        cores.append(d)
    return cores


def assemble_y(y_dev):
    return np.ascontiguousarray(
        y_dev.reshape(128, NCH, L).transpose(2, 1, 0).reshape(L, D))


def kernel(**inputs):
    from concourse.bass_utils import run_bass_kernel_spmd
    cores = prep_core_inputs(inputs)
    nc, _ = build(stop_after=99, taps=())
    res = run_bass_kernel_spmd(nc, cores, core_ids=list(range(len(cores))))
    outs = []
    for r in res.results:
        outs.append(assemble_y(np.asarray(r["y_dev"], dtype=np.float32)))
    return np.stack(outs, axis=0)


# version 41 (final)
# speedup vs baseline: 1.1836x; 1.1836x over previous
import sys
sys.path.insert(0, "/opt/trn_rl_repo")
from contextlib import ExitStack
import numpy as np
import concourse.bass as bass
import concourse.mybir as mybir

F32 = mybir.dt.float32
BF16 = mybir.dt.bfloat16
AF = mybir.ActivationFunctionType
ALU = mybir.AluOpType
AX = mybir.AxisListType

L = 4096
D = 1024
H = 16
DKH = 64
E = 8
FF = 4096
NCH = 8
CW = L + 2
TS = 512
NTS = L // TS
NT = L // 128
QS = 512
NQS = L // QS
EPS = 1e-5

ENGINES = ("tensor", "vector", "scalar", "gpsimd", "sync")


class Sched:

    def __init__(self, nc, stack):
        self.nc = nc
        self.stack = stack
        self.semh = {}
        self.count = {}
        for e in ENGINES:
            k = e + "_c"
            self.semh[k] = stack.enter_context(nc.semaphore("S_" + k))
            self.count[k] = 0
        self.ops = {e: [] for e in ENGINES}
        self.hw = {e: {} for e in ENGINES}
        self.touched = {e: {} for e in ENGINES}

    def _slotkey(self, name):
        k = "sl_" + name
        if k not in self.semh:
            self.semh[k] = self.stack.enter_context(self.nc.semaphore(k))
            self.count[k] = 0
        return k

    def add(self, eng, emit, after=(), dma_slot=None, tag=""):
        waits = {}
        for tok in after:
            if tok is None:
                continue
            (pk, pc) = tok
            if pk == eng + "_c" and eng == "tensor":
                continue
            if pk.startswith("sl_"):
                pc = self.count[pk]
            waits[pk] = max(waits.get(pk, 0), pc)
        if dma_slot is None:
            key = eng + "_c"
            amt = 1
        else:
            key = self._slotkey(dma_slot)
            amt = 16
        self.count[key] += amt
        tok = (key, self.count[key])
        if dma_slot is not None:
            self.touched[eng][key] = self.count[key]
        self.ops[eng].append((emit, waits, key, amt, tag))
        return tok

    def dma(self, eng, out, in_, after=(), slot="misc", tag=""):
        return self.add(eng, (lambda e, o=out, i=in_: e.dma_start(out=o, in_=i)),
                        after=after, dma_slot=slot, tag=tag)

    def nopop(self, eng, after=(), tag="fence"):
        return self.add(eng, (lambda e: e.nop()), after=after, tag=tag)

    def emit_block(self):
        nc = self.nc
        for e in ("sync", "gpsimd", "scalar", "vector"):
            if self.touched[e]:
                toks = [(k, c) for (k, c) in self.touched[e].items()]
                self.nopop(e, after=toks, tag="dma_fence")
                self.touched[e] = {}
        with nc.Block() as blk:
            for eng in ENGINES:
                ops = self.ops[eng]
                if not ops:
                    continue

                def mk(eng=eng, ops=tuple(ops)):
                    def body(eobj):
                        for (emit, waits, key, amt, tag) in ops:
                            for (pk, pc) in sorted(waits.items()):
                                if self.hw[eng].get(pk, 0) >= pc:
                                    continue
                                eobj.wait_ge(self.semh[pk], pc)
                                self.hw[eng][pk] = pc
                            last = emit(eobj)
                            assert last is not None, f"op {tag} on {eng} returned None"
                            last.then_inc(self.semh[key], amt)
                    return body

                getattr(blk, eng)(mk())
        for e in ENGINES:
            self.ops[e] = []
        for e in ENGINES:
            for k in self.semh:
                self.hw[e][k] = max(self.hw[e].get(k, 0), self.count[k])


def build(stop_after=99, taps=(), sim=False):
    global AF_GELU, AF_SIG, AF_GELUTANH, AF_ERF
    AF_GELU = AF.Exp if sim else AF.Gelu
    AF_SIG = AF.Exp if sim else AF.Sigmoid
    AF_GELUTANH = AF.Exp if sim else AF.Gelu_apprx_tanh
    nc = bass.Bass("TRN2", target_bir_lowering=False, detect_race_conditions=not sim)
    taps = set(taps)

    def din(name, shape, dtype=F32):
        return nc.dram_tensor(name, list(shape), dtype, kind="ExternalInput").ap()

    x_in = din("x_fm", (128, NCH * L))
    wq_nat = din("wq_nat", (D, D))
    wk_nat = din("wk_nat", (D, D))
    cqT = din("cqT", (3 * D, D))
    ckT = din("ckT", (3 * D, D))
    qpA_T = din("qpA_T", (D, D))
    qpB_T = din("qpB_T", (D, D))
    kpA_T = din("kpA_T", (D, D))
    kpB_T = din("kpB_T", (D, D))
    wvT_in = din("wvT", (D, D))
    outT_in = din("outT", (D, D))
    gw1T_in = din("gw1T", (D, D))
    gw2T_in = din("gw2T", (D, E))
    w1T_in = din("w1T", (D, FF))
    w2T_in = din("w2T", (FF, D))
    tw1T_in = din("tw1T", (E * D, FF))
    tw2T_in = din("tw2T", (E * FF, D))
    tb1_in = din("tb1", (E, FF))
    tb2_in = din("tb2", (E, D))
    b1row_in = din("b1row", (1, FF))
    b2row_in = din("b2row", (1, D))
    b1pc_in = din("b1pc", (128, 32))
    b2pc_in = din("b2pc", (128, 8))
    wqb_pc = din("wqb_pc", (128, NCH))
    wkb_pc = din("wkb_pc", (128, NCH))
    qpb_pc = din("qpb_pc", (128, NCH))
    kpb_pc = din("kpb_pc", (128, NCH))
    cqb_pc = din("cqb_pc", (128, NCH))
    ckb_pc = din("ckb_pc", (128, NCH))
    outb_pc_in = din("outb_pc", (128, NCH))
    gb1_pc_in = din("gb1_pc", (128, NCH))
    gb2_col_in = din("gb2_col", (E, 1))
    wvb_row_in = din("wvb_row", (1, D))
    tau1w_in = din("tau1w_pc", (128, NCH * 6))
    del1w_in = din("del1w_pc", (128, NCH * 6))
    tau1b_in = din("tau1b_pc", (128, NCH * 2))
    del1b_in = din("del1b_pc", (128, NCH * 2))
    tau2wT_in = din("tau2wT", (128, 16 * H))
    del2wT_in = din("del2wT", (128, 16 * H))
    tau2b_in = din("tau2b_row", (1, H))
    del2b_in = din("del2b_row", (1, H))
    ones_row_in = din("ones_row", (1, 128), BF16)
    ones_rowf_in = din("ones_rowf", (1, 128))
    ones_col_in = din("ones_col", (128, 1), BF16)
    oneh2_in = din("oneh2", (2, 2 * 128), BF16)
    oneh16_in = din("oneh16", (H, H * DKH), BF16)
    ident_in = din("ident_f32", (128, 128))

    y_dev = nc.dram_tensor("y_dev", [128, NCH * L], F32, kind="ExternalOutput").ap()
    qbuf = nc.dram_tensor("qbuf", [128, NCH * L], BF16, kind="Internal").ap()
    kbuf = nc.dram_tensor("kbuf", [128, NCH * L], BF16, kind="Internal").ap()
    x2buf = nc.dram_tensor("x2buf", [128, NCH * L], F32, kind="Internal").ap()
    xn2buf = nc.dram_tensor("xn2buf", [128, NCH * L], BF16, kind="Internal").ap()
    tw1bf = nc.dram_tensor("tw1bf", [E * D, FF], BF16, kind="Internal").ap()
    tw2bf = nc.dram_tensor("tw2bf", [E * FF, D], BF16, kind="Internal").ap()

    tap_outs = {}
    tap_shapes = {
        "xn": ([128, NCH * CW], BF16),
        "tauT": ([128, NT * H], F32),
        "edelT": ([128, NT * H], F32),
        "taufm": ([H, L], BF16),
        "q": ([128, NCH * L], BF16),
        "k": ([128, NCH * L], BF16),
        "vaug": ([128, NT * H * 65], BF16),
        "mbias": ([128, H], F32),
        "attn": ([128, NCH * QS], BF16),
        "x2": ([128, NCH * L], F32),
        "xn2": ([128, NCH * L], BF16),
        "g": ([E, 2], F32),
        "mw1t": ([128, NCH * FF], BF16),
        "mw2t": ([128, 32 * D], BF16),
        "mb1pc": ([128, 32], F32),
        "mb2pc": ([128, 8], F32),
        "biask": ([128, NCH], F32),
        "vkcorr": ([128, 4 * NCH], F32),
        "vraw": ([128, TS], F32),
        "xn2nd": ([128, NCH * CW], BF16),
        "gbc": ([128, E], F32),
    }
    for t in taps:
        sh, dt = tap_shapes[t]
        tap_outs[t] = nc.dram_tensor("tap_" + t, sh, dt, kind="ExternalOutput").ap()

    with ExitStack() as top:
        TE = top.enter_context
        S = Sched(nc, top)

        def apx(sb_tensor, part0, nparts, off, dims):
            t = sb_tensor.ap().tensor
            w = sb_tensor.ap().ap[0][0]
            return bass.AP(t, part0 * w + off, [[w, nparts]] + [[s, c] for (s, c) in dims])


        tauT = None
        delT = None
        edelT = None
        tau_fm = None
        mbias = None
        nrm = None
        nrow = None
        negm_bf = None
        ones_row_sb = TE(nc.sbuf_tensor("ones_row_sb", [1, 128], BF16))
        ones_rowf_sb = TE(nc.sbuf_tensor("ones_rowf_sb", [1, 128], F32))
        ones_col_sb = TE(nc.sbuf_tensor("ones_col_sb", [128, 1], BF16))
        oneh2_sb = TE(nc.sbuf_tensor("oneh2_sb", [2, 2 * 128], BF16))
        oneh16_sb = None
        ident_sb = TE(nc.sbuf_tensor("ident_sb", [128, 128], F32))
        negm0 = TE(nc.sbuf_tensor("negm0", [1, TS], F32))
        rstd0 = TE(nc.sbuf_tensor("rstd0", [1, TS], F32))
        scr0 = TE(nc.sbuf_tensor("scr0", [1, TS], F32))
        scr1 = TE(nc.sbuf_tensor("scr1", [1, TS], F32))
        eps_sb = TE(nc.sbuf_tensor("eps_sb", [1, 1], F32))
        zb_sb = TE(nc.sbuf_tensor("zb_sb", [128, 1], F32))
        biasq = TE(nc.sbuf_tensor("biasq", [128, NCH], F32))
        biask = TE(nc.sbuf_tensor("biask", [128, NCH], F32))
        vq_corr = TE(nc.sbuf_tensor("vq_corr", [128, 4 * NCH], F32))
        vk_corr = TE(nc.sbuf_tensor("vk_corr", [128, 4 * NCH], F32))
        g_sb = TE(nc.sbuf_tensor("g_sb", [E, 2], F32))
        g_bf = TE(nc.sbuf_tensor("g_bf", [E, 1], BF16))
        g_bc = TE(nc.sbuf_tensor("g_bc", [128, E], F32))
        grow_f = TE(nc.sbuf_tensor("grow_f", [1, E], F32))
        grow_bf = TE(nc.sbuf_tensor("grow_bf", [1, E], BF16))
        mb1_pc = TE(nc.sbuf_tensor("mb1_pc", [128, 32], F32))
        mb2_pc = TE(nc.sbuf_tensor("mb2_pc", [128, 8], F32))

        t_or = S.dma("sync", ones_row_sb[:], ones_row_in, slot="consts")
        t_orf = S.dma("sync", ones_rowf_sb[:], ones_rowf_in, slot="consts")
        t_oc = S.dma("sync", ones_col_sb[:], ones_col_in, slot="consts")
        t_o2 = S.dma("sync", oneh2_sb[:], oneh2_in, slot="consts")
        t_id = S.dma("sync", ident_sb[:], ident_in, slot="consts")
        S.add("gpsimd", lambda g: g.memset(eps_sb[:], EPS))
        tok_zb = S.add("gpsimd", lambda g: g.memset(zb_sb[:], 0.0))
        S.add("gpsimd", lambda g: g.memset(nrm[:], 0.0))
        CONSTS = [t_or, t_orf, t_oc, t_o2, t_id]

        def make_ln(sq_st, ps_sum, ps_sq, ps_bc):
            def ln_stats_apply(slices, resid_slices=None, out_slices=None,
                               after=(), prev_apply=None, prev_stats=None,
                               tmpf=None, pair_extra=None, on_pair=None):
                after = list(after)
                def mm_sum(t):
                    r = None
                    for c in range(NCH):
                        r = t.matmul(ps_sum[:, 0:TS], ones_col_sb[:], slices[c],
                                     start=(c == 0), stop=(c == NCH - 1))
                    return r
                tok_sum = S.add("tensor", mm_sum, after=after + CONSTS + [prev_stats], tag="ln_sum")
                sq_toks = []
                mm_sq_toks = []
                single_sq = sq_st[0] is sq_st[1]
                for c in range(NCH):
                    def sq(v, c=c):
                        return v.tensor_tensor(out=sq_st[c % 2][:], in0=slices[c],
                                               in1=slices[c], op=ALU.mult)
                    wi = c - 1 if single_sq else c - 2
                    war = mm_sq_toks[wi] if wi >= 0 else None
                    tok_sq = S.add("vector", sq, after=after + [war], tag="ln_sq")
                    def mmsq(t, c=c):
                        return t.matmul(ps_sq[:, 0:TS], ones_col_sb[:], sq_st[c % 2][:],
                                        start=(c == 0), stop=(c == NCH - 1))
                    tok_mm = S.add("tensor", mmsq, after=[tok_sq, prev_stats], tag="ln_sqmm")
                    mm_sq_toks.append(tok_mm)
                    sq_toks.append(tok_sq)
                def rmath_a(v):
                    v.tensor_scalar(out=negm0[0:1, 0:TS], in0=ps_sum[:, 0:TS],
                                    scalar1=-1.0 / D, scalar2=None, op0=ALU.mult)
                    return v.tensor_scalar(out=scr0[0:1, 0:TS], in0=ps_sq[:, 0:TS],
                                           scalar1=1.0 / D, scalar2=None, op0=ALU.mult)
                tok_ra = S.add("vector", rmath_a, after=[tok_sum, mm_sq_toks[-1]], tag="ln_ra")
                tok_rb = S.add("vector", lambda v: v.tensor_tensor(
                    out=scr1[0:1, 0:TS], in0=negm0[0:1, 0:TS],
                    in1=negm0[0:1, 0:TS], op=ALU.mult), after=[tok_ra], tag="ln_rb")
                tok_rm = S.add("vector", lambda v: v.tensor_tensor(
                    out=scr0[0:1, 0:TS], in0=scr0[0:1, 0:TS], in1=scr1[0:1, 0:TS],
                    op=ALU.subtract), after=[tok_rb], tag="ln_rc")
                tok_ln1 = S.add("scalar", lambda s: s.activation(
                    out=scr1[0:1, 0:TS], in_=scr0[0:1, 0:TS], func=AF.Ln,
                    bias=eps_sb[0:1, 0:1], scale=1.0), after=[tok_rm], tag="ln_ln")
                tok_rs = S.add("scalar", lambda s: s.activation(
                    out=rstd0[0:1, 0:TS], in_=scr1[0:1, 0:TS], func=AF.Exp, scale=-0.5),
                    after=[tok_ln1], tag="ln_exp")
                def bcast(t):
                    t.matmul(ps_bc[:, 0:TS], ones_rowf_sb[0:1, 0:128], negm0[0:1, 0:TS],
                             start=True, stop=True)
                    return t.matmul(ps_bc[:, TS:2 * TS], ones_rowf_sb[0:1, 0:128],
                                    rstd0[0:1, 0:TS], start=True, stop=True)
                tok_bc = S.add("tensor", bcast, after=[tok_rs, tok_rm, prev_apply], tag="ln_bc")
                tok_ap = tok_bc
                for c0 in range(0, NCH, 2):
                    cpair = [c0, c0 + 1]
                    extra = list(pair_extra(c0)) if pair_extra is not None else []
                    if out_slices is None:
                        tok_a = S.add("vector", (lambda v, cp=cpair: [
                            v.tensor_tensor(out=slices[c], in0=slices[c],
                                            in1=ps_bc[:, 0:TS], op=ALU.add)
                            for c in cp][-1]), after=[tok_ap], tag="ap_add")
                        tok_ap = S.add("vector", (lambda v, cp=cpair: [
                            v.tensor_tensor(out=slices[c], in0=slices[c],
                                            in1=ps_bc[:, TS:2 * TS], op=ALU.mult)
                            for c in cp][-1]), after=[tok_a], tag="ap_mul")
                    else:
                        tok_a = S.add("vector", (lambda v, cp=cpair: [
                            v.tensor_tensor(out=tmpf[c % 2][:], in0=slices[c],
                                            in1=ps_bc[:, 0:TS], op=ALU.add)
                            for c in cp][-1]), after=[tok_ap], tag="ap_add")
                        tok_b = S.add("vector", (lambda v, cp=cpair: [
                            v.tensor_tensor(out=tmpf[c % 2][:], in0=tmpf[c % 2][:],
                                            in1=ps_bc[:, TS:2 * TS], op=ALU.mult)
                            for c in cp][-1]), after=[tok_a], tag="ap_mul")
                        if resid_slices is not None:
                            tok_ap = S.add("vector", (lambda v, cp=cpair: [
                                v.tensor_tensor(out=out_slices[c], in0=tmpf[c % 2][:],
                                                in1=resid_slices[c], op=ALU.add)
                                for c in cp][-1]), after=[tok_b] + extra, tag="ap_out")
                        else:
                            tok_ap = S.add("vector", (lambda v, cp=cpair: [
                                v.tensor_copy(out_slices[c], tmpf[c % 2][:])
                                for c in cp][-1]), after=[tok_b] + extra, tag="ap_out")
                    if on_pair is not None:
                        on_pair(c0, tok_ap)
                return tok_ap, tok_rm
            return ln_stats_apply

        FB = {}
        fold_war = {"cT_rd": [], "mjt_rd": [], "pA_rd": [], "wn_rd": [],
                    "pB_rd": [], "sm_rd": [], "psv_rd": None,
                    "last_evac": {0: None, 1: None}, "n": 0}

        def fold_path(pref, wnat_ap, cT_ap, pA_ap, pB_ap, wb_ap, cb_ap, pb_ap,
                      weff_sb, bias_sb, vcorr_sb):
            wn_sb, pB_sb, cT_h, mjt_h, pA_h = (FB[k] for k in
                                               ("wn", "pB", "cT", "mjt", "pA"))
            wb_bf, cb_bf, pb_f, ps, ps_v = (FB[k] for k in
                                            ("wb", "cb", "pb", "ps", "ps_v"))
            st = {}

            def setup():
                ldB, ldW = [], []
                for cc in range(NCH):
                    ldB.append(S.dma("gpsimd", pB_sb[:, cc * D:(cc + 1) * D],
                                     pB_ap[cc * 128:(cc + 1) * 128, :],
                                     after=fold_war["pB_rd"], slot=pref + "pB"))
                    ldW.append(S.dma("gpsimd", wn_sb[:, cc * D:(cc + 1) * D],
                                     wnat_ap[cc * 128:(cc + 1) * 128, :],
                                     after=fold_war["wn_rd"], slot=pref + "wn"))
                ld_wb = S.dma("gpsimd", wb_bf[:], wb_ap, after=fold_war["sm_rd"], slot=pref + "sm")
                ld_cb = S.dma("gpsimd", cb_bf[:], cb_ap, slot=pref + "sm")
                ld_pb = S.dma("gpsimd", pb_f[:], pb_ap, slot=pref + "sm")
                fold_war["pB_rd"] = []
                fold_war["wn_rd"] = []
                fold_war["sm_rd"] = []
                def bmm(t):
                    r = None
                    for oc in range(NCH):
                        for cc in range(NCH):
                            r = t.matmul(ps_v[:, 3 * NCH + oc: 3 * NCH + oc + 1],
                                         pB_sb[:, cc * D + oc * 128: cc * D + (oc + 1) * 128],
                                         cb_bf[:, cc:cc + 1],
                                         start=(cc == 0), stop=(cc == NCH - 1))
                    return r
                tok_bcb = S.add("tensor", bmm,
                                after=[ld_cb, fold_war["psv_rd"]] + ldB, tag="bcb")
                fold_war["sm_rd"].append(tok_bcb)
                st.update(ldB=ldB, ldW=ldW, ld_wb=ld_wb, ld_pb=ld_pb,
                          vm_toks=[tok_bcb], wn_reads=[], pB_reads=[tok_bcb])

            def one_iter(j, oh):
                if "ldB" not in st:
                    setup()
                ldA = []
                if j == 1:
                    for mc in range(NCH):
                        ldA.append(S.dma(
                            "gpsimd", pA_h[:, mc * TS:(mc + 1) * TS],
                            pA_ap[mc * 128:(mc + 1) * 128, oh * TS: oh * TS + TS],
                            after=fold_war["pA_rd"] if mc == 0 else (), slot=pref + "pA"))
                    fold_war["pA_rd"] = []
                mjt_toks = []
                mjt_mm_toks = []
                for mh in range(2):
                    ldc = []
                    for cc in range(NCH):
                        ldc.append(S.dma(
                            "gpsimd", cT_h[:, cc * TS:(cc + 1) * TS],
                            cT_ap[j * D + cc * 128: j * D + (cc + 1) * 128,
                                  mh * TS: mh * TS + TS],
                            after=fold_war["cT_rd"] if cc == 0 else (), slot=pref + "cT"))
                    fold_war["cT_rd"] = []
                    for mi in range(4):
                        mc = mh * 4 + mi
                        pi = fold_war["n"] % 2
                        def mm(t, mi=mi, oh=oh, pi=pi):
                            r = None
                            for cc in range(NCH):
                                r = t.matmul(ps[pi][:, 0:TS],
                                             cT_h[:, cc * TS + mi * 128: cc * TS + (mi + 1) * 128],
                                             pB_sb[:, cc * D + oh * TS: cc * D + oh * TS + TS],
                                             start=(cc == 0), stop=(cc == NCH - 1))
                            return r
                        tok_mm = S.add("tensor", mm,
                                       after=ldc + st["ldB"] + [fold_war["last_evac"][pi]] +
                                             fold_war["mjt_rd"], tag="mjt_mm")
                        fold_war["mjt_rd"] = []
                        st["pB_reads"].append(tok_mm)
                        fold_war["cT_rd"].append(tok_mm)
                        def ev(sc, mc=mc, pi=pi):
                            return sc.activation(out=mjt_h[:, mc * TS:(mc + 1) * TS],
                                                 in_=ps[pi][:, 0:TS], func=AF.Identity,
                                                 bias=zb_sb[:, 0:1], scale=1.0)
                        tok_ev = S.add("scalar", ev, after=[tok_mm, tok_zb], tag="mjt_ev")
                        fold_war["last_evac"][pi] = tok_ev
                        mjt_toks.append(tok_ev)
                        mjt_mm_toks.append(tok_mm)
                        fold_war["n"] += 1
                    if mh == 0:
                        S.nopop("gpsimd", after=mjt_mm_toks[-4:], tag="cT_gate")
                weff_toks = []
                for ic in range(NCH):
                    pi = fold_war["n"] % 2
                    def mm(t, ic=ic, pi=pi, j=j):
                        r = None
                        first = True
                        for mc in range(NCH):
                            r = t.matmul(ps[pi][:, 0:TS],
                                         wn_sb[:, mc * D + ic * 128: mc * D + (ic + 1) * 128],
                                         mjt_h[:, mc * TS:(mc + 1) * TS],
                                         start=first, stop=(j != 1 and mc == NCH - 1))
                            first = False
                        if j == 1:
                            for mc in range(NCH):
                                r = t.matmul(ps[pi][:, 0:TS],
                                             wn_sb[:, mc * D + ic * 128: mc * D + (ic + 1) * 128],
                                             pA_h[:, mc * TS:(mc + 1) * TS],
                                             start=False, stop=(mc == NCH - 1))
                        return r
                    tok_mm = S.add("tensor", mm,
                                   after=mjt_toks + st["ldW"] + ldA +
                                         [fold_war["last_evac"][pi]], tag="weff_mm")
                    st["wn_reads"].append(tok_mm)
                    weff_toks.append(tok_mm)
                    def ev(sc, j=j, ic=ic, oh=oh, pi=pi):
                        return sc.activation(
                            out=weff_sb[:, (j * NCH + ic) * D + oh * TS:(j * NCH + ic) * D + oh * TS + TS],
                            in_=ps[pi][:, 0:TS], func=AF.Identity,
                            bias=zb_sb[:, 0:1], scale=1.0)
                    tok_ev = S.add("scalar", ev, after=[tok_mm, tok_zb], tag="weff_ev")
                    fold_war["last_evac"][pi] = tok_ev
                    fold_war["n"] += 1
                def vmm(t, j=j, oh=oh):
                    r = None
                    for oi in range(4):
                        oc = oh * 4 + oi
                        for mc in range(NCH):
                            r = t.matmul(ps_v[:, j * NCH + oc: j * NCH + oc + 1],
                                         mjt_h[:, mc * TS + oi * 128: mc * TS + (oi + 1) * 128],
                                         wb_bf[:, mc:mc + 1],
                                         start=(mc == 0), stop=(j != 1 and mc == NCH - 1))
                        if j == 1:
                            for mc in range(NCH):
                                r = t.matmul(ps_v[:, j * NCH + oc: j * NCH + oc + 1],
                                             pA_h[:, mc * TS + oi * 128: mc * TS + (oi + 1) * 128],
                                             wb_bf[:, mc:mc + 1],
                                             start=False, stop=(mc == NCH - 1))
                    return r
                tok_v = S.add("tensor", vmm, after=mjt_toks + ldA + [st["ld_wb"]], tag="vmm")
                st["vm_toks"].append(tok_v)
                fold_war["mjt_rd"] = weff_toks + [tok_v]
                fold_war["pA_rd"] = weff_toks + [tok_v] if j == 1 else fold_war["pA_rd"]
                fold_war["sm_rd"].append(tok_v)

            def bias_emit():
                t0 = S.add("vector", lambda v: v.tensor_copy(vcorr_sb[:], ps_v[:, 0:4 * NCH]),
                           after=st["vm_toks"], tag="vc_copy")
                t1 = S.add("vector", lambda v: v.tensor_tensor(
                    out=bias_sb[:], in0=vcorr_sb[:, 0:NCH], in1=vcorr_sb[:, NCH:2 * NCH],
                    op=ALU.add), after=[t0], tag="b1")
                t2 = S.add("vector", lambda v: v.tensor_tensor(
                    out=bias_sb[:], in0=bias_sb[:], in1=vcorr_sb[:, 2 * NCH:3 * NCH],
                    op=ALU.add), after=[t1], tag="b2")
                t3 = S.add("vector", lambda v: v.tensor_tensor(
                    out=bias_sb[:], in0=bias_sb[:], in1=vcorr_sb[:, 3 * NCH:4 * NCH],
                    op=ALU.add), after=[t2], tag="b3")
                tok_bias = S.add("vector", lambda v: v.tensor_tensor(
                    out=bias_sb[:], in0=bias_sb[:], in1=pb_f[:], op=ALU.add),
                    after=[t3, st["ld_pb"]], tag="bias")
                fold_war["psv_rd"] = t0
                fold_war["wn_rd"] = st["wn_reads"]
                fold_war["pB_rd"] = st["pB_reads"]
                fold_war["sm_rd"].append(tok_bias)
                st["bias_tok"] = tok_bias
                return tok_bias

            iters = [(lambda j=j, oh=oh: one_iter(j, oh))
                     for j in range(3) for oh in range(2)]
            return iters, bias_emit, st

        def qk_production(weff_sb, bias_sb, vcorr_sb, buf, scaled, nrm_off, tap_name, tok_bias=None):
            pref = tap_name
            with ExitStack() as stP:
                EE = stP.enter_context
                tbc_sb = EE(nc.sbuf_tensor(pref + "_tbc", [128, L], BF16, side="right")) if scaled else None
                qstage = [EE(nc.sbuf_tensor(pref + "_st%d" % i, [128, TS], BF16, side="right")) for i in range(2)]
                sqq = [EE(nc.sbuf_tensor(pref + "_sqq%d" % i, [128, TS], BF16, side="right")) for i in range(2)]
                ps_q = [EE(nc.psum_tensor(pref + "_ps%d" % i, [128, TS], F32)) for i in range(2)]
                ps_tb = EE(nc.psum_tensor(pref + "_pstb", [128, TS], F32)) if scaled else None
                ps_n = [EE(nc.psum_tensor(pref + "_psn%d" % i, [1, TS], F32)) for i in range(2)]

                toks_spill = []
                prev_evac = [None, None]
                prev_spill = [None, None]
                prev_sq = [None, None]
                prev_tbc = None
                prev_nred = None
                pend_nmm = [None]
                for oc in range(NCH):
                    h0, h1 = 2 * oc, 2 * oc + 1
                    tbc_toks = []
                    if scaled:
                        for tsl in range(NTS):
                            def tbm(t, oc=oc, tsl=tsl):
                                t.matmul(ps_tb[0:64, 0:TS],
                                         oneh16_sb[:, (2 * oc) * 64:(2 * oc + 1) * 64],
                                         tau_fm[0:16, tsl * TS:(tsl + 1) * TS],
                                         start=True, stop=True)
                                return t.matmul(ps_tb[64:128, 0:TS],
                                                oneh16_sb[:, (2 * oc + 1) * 64:(2 * oc + 2) * 64],
                                                tau_fm[0:16, tsl * TS:(tsl + 1) * TS],
                                                start=True, stop=True)
                            tok_tbm = S.add("tensor", tbm, after=[prev_tbc], tag="tbm")
                            def tbe(v, tsl=tsl):
                                return v.tensor_copy(tbc_sb[:, tsl * TS:(tsl + 1) * TS], ps_tb[:, 0:TS])
                            prev_tbc = S.add("vector", tbe, after=[tok_tbm], tag="tbe")
                            tbc_toks.append(prev_tbc)
                    for tsl in range(NTS):
                        bi = tsl % 2
                        def qmm(t, oc=oc, tsl=tsl, bi=bi, weff_sb=weff_sb):
                            r = None
                            first = True
                            for j in range(3):
                                for ic in range(NCH):
                                    r = t.matmul(ps_q[bi][:, 0:TS],
                                                 weff_sb[:, (j * NCH + ic) * D + oc * 128:(j * NCH + ic) * D + (oc + 1) * 128],
                                                 xn[:, ic * CW + tsl * TS + j: ic * CW + tsl * TS + j + TS],
                                                 start=first, stop=(j == 2 and ic == NCH - 1))
                                    first = False
                            return r
                        tok_mm = S.add("tensor", qmm, after=[prev_evac[bi]], tag="qkmm")
                        fix = None
                        if tsl == 0:
                            def fx(v, bi=bi, oc=oc, vcorr_sb=vcorr_sb):
                                return v.tensor_tensor(out=ps_q[bi][:, 0:1], in0=ps_q[bi][:, 0:1],
                                                       in1=vcorr_sb[:, oc:oc + 1], op=ALU.subtract)
                            fix = S.add("vector", fx, after=[tok_mm, tok_bias], tag="qfix")
                        if tsl == NTS - 1:
                            def fx(v, bi=bi, oc=oc, vcorr_sb=vcorr_sb):
                                return v.tensor_tensor(out=ps_q[bi][:, TS - 1:TS], in0=ps_q[bi][:, TS - 1:TS],
                                                       in1=vcorr_sb[:, 2 * NCH + oc: 2 * NCH + oc + 1],
                                                       op=ALU.subtract)
                            fix = S.add("vector", fx, after=[tok_mm, tok_bias], tag="qfix")
                        war = [prev_spill[bi], prev_sq[bi]]
                        if scaled:
                            def evac(v, bi=bi, oc=oc, tsl=tsl, bias_sb=bias_sb):
                                return v.scalar_tensor_tensor(out=qstage[bi][:], in0=ps_q[bi][:, 0:TS],
                                                              scalar=bias_sb[:, oc:oc + 1],
                                                              in1=tbc_sb[:, tsl * TS:(tsl + 1) * TS],
                                                              op0=ALU.add, op1=ALU.mult)
                            tok_ev = S.add("vector", evac, after=[tok_mm, fix, tok_bias] + tbc_toks + war, tag="qevac")
                        else:
                            def evac(s, bi=bi, oc=oc, bias_sb=bias_sb):
                                return s.activation(out=qstage[bi][:], in_=ps_q[bi][:, 0:TS],
                                                    func=AF.Identity, bias=bias_sb[:, oc:oc + 1],
                                                    scale=1.0)
                            tok_ev = S.add("scalar", evac, after=[tok_mm, fix, tok_bias] + war, tag="kevac")
                        prev_evac[bi] = tok_ev
                        tok_sp = S.dma("sync", buf[:, oc * L + tsl * TS: oc * L + (tsl + 1) * TS],
                                       qstage[bi].ap(), after=[tok_ev], tag="spill",
                                       slot=pref + "sp%d" % bi)
                        prev_spill[bi] = tok_sp
                        toks_spill.append(tok_sp)
                        def sq(v, bi=bi):
                            return v.tensor_tensor(out=sqq[bi][:], in0=qstage[bi][:], in1=qstage[bi][:],
                                                   op=ALU.mult)
                        tok_sq = S.add("vector", sq, after=[tok_ev, prev_nred], tag="nsq")
                        prev_sq[bi] = tok_sq
                        def emit_nmm(tok_sq=tok_sq, bi=bi, h0=h0, h1=h1):
                            def nmm(t):
                                t.matmul(ps_n[0][:, 0:TS], ones_col_sb[0:64, :], sqq[bi][0:64, :],
                                         start=True, stop=True)
                                return t.matmul(ps_n[1][:, 0:TS], ones_col_sb[64:128, :],
                                                sqq[bi][64:128, :], start=True, stop=True)
                            tok_nmm = S.add("tensor", nmm, after=[tok_sq, prev_nred], tag="nmm")
                            def nred(v, h0=h0, h1=h1, nrm_off=nrm_off):
                                v.tensor_reduce(out=nrow[0:1, 30:31], in_=ps_n[0][:, 0:TS],
                                                axis=AX.X, op=ALU.max)
                                v.tensor_reduce(out=nrow[0:1, 31:32], in_=ps_n[1][:, 0:TS],
                                                axis=AX.X, op=ALU.max)
                                v.tensor_tensor(out=nrm[0:1, nrm_off + h0: nrm_off + h0 + 1],
                                                in0=nrm[0:1, nrm_off + h0: nrm_off + h0 + 1],
                                                in1=nrow[0:1, 30:31], op=ALU.max)
                                return v.tensor_tensor(out=nrm[0:1, nrm_off + h1: nrm_off + h1 + 1],
                                                       in0=nrm[0:1, nrm_off + h1: nrm_off + h1 + 1],
                                                       in1=nrow[0:1, 31:32], op=ALU.max)
                            return S.add("vector", nred, after=[tok_nmm], tag="nred")
                        if pend_nmm[0] is not None:
                            prev_nred = pend_nmm[0]()
                        pend_nmm[0] = emit_nmm
                if pend_nmm[0] is not None:
                    prev_nred = pend_nmm[0]()
                    pend_nmm[0] = None
                if tap_name in taps:
                    S.dma("sync", tap_outs[tap_name], buf, after=toks_spill, slot="tap")
                S.emit_block()

        _tau_cms = []
        def _lalloc(name, shape, dt):
            cm = nc.sbuf_tensor(name, shape, dt, side="left")
            _tau_cms.append(cm)
            return cm.__enter__()
        xn_cm = nc.sbuf_tensor("xn", [128, NCH * CW], BF16, side="right")
        xn = xn_cm.__enter__()
        mbias = _lalloc("mbias", [128, H], F32)
        edelT = _lalloc("edelT", [128, NT * H], F32)
        tau_fm = _lalloc("tau_fm", [H, L], BF16)
        _tau3_cms = []
        def _lalloc3(name, shape, dt):
            cm = nc.sbuf_tensor(name, shape, dt, side="left")
            _tau3_cms.append(cm)
            return cm.__enter__()
        tauT = _lalloc3("tauT", [128, NT * H], F32)
        delT = _lalloc3("delT", [128, NT * H], F32)
        nrm = _lalloc3("nrm", [1, 2 * H], F32)
        nrow = _lalloc3("nrow", [1, 2 * H], F32)
        negm_bf = _lalloc3("negm_bf", [1, H], BF16)
        oneh16_sb = _lalloc3("oneh16_sb", [H, H * DKH], BF16)
        t_o16 = S.dma("sync", oneh16_sb[:], oneh16_in, slot="consts")
        CONSTS.append(t_o16)
        S.add("gpsimd", lambda g: g.memset(
            apx(xn, 0, 128, 0, [(CW, NCH), (CW - 1, 2)]), 0.0))

        with ExitStack() as stWQ:
            WE = stWQ.enter_context
            weff = WE(nc.sbuf_tensor("weff", [128, 3 * NCH * D], BF16, side="right"))
            FB["wn"] = WE(nc.sbuf_tensor("f_wn", [128, NCH * D], BF16, side="right"))
            FB["pB"] = WE(nc.sbuf_tensor("f_pB", [128, NCH * D], BF16, side="right"))
            FB["cT"] = WE(nc.sbuf_tensor("f_cTh", [128, NCH * TS], BF16, side="right"))
            FB["mjt"] = WE(nc.sbuf_tensor("f_mjth", [128, NCH * TS], BF16, side="right"))
            FB["pA"] = WE(nc.sbuf_tensor("f_pAh", [128, NCH * TS], BF16, side="right"))
            FB["wb"] = WE(nc.sbuf_tensor("f_wb", [128, NCH], BF16, side="right"))
            FB["cb"] = WE(nc.sbuf_tensor("f_cb", [128, NCH], BF16, side="right"))
            FB["pb"] = WE(nc.sbuf_tensor("f_pb", [128, NCH], F32, side="right"))
            FB["ps"] = [WE(nc.psum_tensor("f_ps%d" % i, [128, TS], F32)) for i in range(2)]
            FB["ps_v"] = WE(nc.psum_tensor("f_psv", [128, 4 * NCH], F32))

            def apx(sb_tensor, part0, nparts, off, dims):
                t = sb_tensor.ap().tensor
                w = sb_tensor.ap().ap[0][0]
                return bass.AP(t, part0 * w + off, [[w, nparts]] + [[s, c] for (s, c) in dims])

            fq_iters, fq_bias, fq_st = fold_path(
                "fq", wq_nat, cqT, qpA_T, qpB_T, wqb_pc, cqb_pc, qpb_pc,
                weff, biasq, vq_corr)

            stH = ExitStack()
            HE = stH.enter_context
            sqst4 = HE(nc.sbuf_tensor("p1_sq4", [128, 2 * TS], BF16))
            psb2d = [HE(nc.psum_tensor("p1_psb%d" % i, [128, TS], F32)) for i in range(2)]
            t1w = HE(nc.sbuf_tensor("t1w", [128, NCH * 6], F32))
            d1w = HE(nc.sbuf_tensor("d1w", [128, NCH * 6], F32))
            t1b = HE(nc.sbuf_tensor("t1b", [128, NCH * 2], F32))
            d1b = HE(nc.sbuf_tensor("d1b", [128, NCH * 2], F32))
            t2w = HE(nc.sbuf_tensor("t2w", [128, 16 * H], BF16))
            d2w = HE(nc.sbuf_tensor("d2w", [128, 16 * H], BF16))
            t2b = HE(nc.sbuf_tensor("t2b", [1, H], BF16))
            d2b = HE(nc.sbuf_tensor("d2b", [1, H], BF16))
            dwst = [HE(nc.sbuf_tensor("dwst%d" % i, [128, TS], F32)) for i in range(2)]
            gstage = HE(nc.sbuf_tensor("gstage", [128, 4 * TS], BF16))
            ps_pw_t = HE(nc.psum_tensor("ps_pw_t", [128, 512], F32))
            ps_pw_d = HE(nc.psum_tensor("ps_pw_d", [128, 512], F32))
            ps_tr = HE(nc.psum_tensor("ps_tr", [16, 128], F32))

            l1 = S.dma("sync", t1w[:], tau1w_in, slot="t1")
            l2 = S.dma("sync", d1w[:], del1w_in, slot="t1")
            l3 = S.dma("sync", t1b[:], tau1b_in, slot="t1")
            l4 = S.dma("sync", d1b[:], del1b_in, slot="t1")
            l5 = S.dma("gpsimd", t2w[:], tau2wT_in, slot="t2")
            l6 = S.dma("gpsimd", d2w[:], del2wT_in, slot="t2")
            l7 = S.dma("gpsimd", t2b[:], tau2b_in, slot="t2")
            l8 = S.dma("gpsimd", d2b[:], del2b_in, slot="t2")
            LD3 = [l1, l2, l3, l4, l5, l6, l7, l8]

            xld = {}

            def emit_xload(ti):
                if ti % 2 == 1:
                    xld[ti] = xld[ti - 1]
                    return
                tok = None
                for c in range(NCH):
                    tok = S.dma("gpsimd",
                                xn[:, c * CW + 1 + ti * TS: c * CW + 1 + (ti + 2) * TS],
                                x_in[:, c * L + ti * TS: c * L + (ti + 2) * TS],
                                slot="xload")
                xld[ti] = tok
                xld[ti + 1] = tok

            b2war = {"bank": {0: [], 1: []}, "mmsq": None}
            xn_toks = {}

            def mk_ln_x(ti):
                psb2 = psb2d[ti % 2]
                slc = lambda c: xn[:, c * CW + 1 + ti * TS: c * CW + 1 + (ti + 1) * TS]
                st = {}

                def s_stats():
                    def mm_sum(t):
                        r = None
                        for c in range(NCH):
                            r = t.matmul(psb2[0:1, 0:TS], ones_col_sb[:], slc(c),
                                         start=(c == 0), stop=(c == NCH - 1))
                        return r
                    t_sum = S.add("tensor", mm_sum,
                                  after=[xld[ti]] + CONSTS + b2war["bank"][ti % 2],
                                  tag="ln_sum")
                    mmh = None
                    for h in range(4):
                        def sqh(v, h=h):
                            return [v.tensor_tensor(out=sqst4[:, i * TS:(i + 1) * TS],
                                                    in0=slc(2 * h + i), in1=slc(2 * h + i),
                                                    op=ALU.mult) for i in range(2)][-1]
                        t_sq = S.add("vector", sqh,
                                     after=[xld[ti], mmh if h else b2war["mmsq"]],
                                     tag="ln_sq")
                        def mmsq(t, h=h):
                            r = None
                            for i in range(2):
                                r = t.matmul(psb2[32:33, 0:TS], ones_col_sb[:],
                                             sqst4[:, i * TS:(i + 1) * TS],
                                             start=(h == 0 and i == 0),
                                             stop=(h == 3 and i == 1))
                            return r
                        mmh = S.add("tensor", mmsq, after=[t_sq], tag="ln_sqmm")
                    b2war["mmsq"] = mmh
                    st["sum"], st["sq"] = t_sum, mmh

                def s_rmath():
                    def rmath_a(v):
                        v.tensor_scalar(out=negm0[0:1, 0:TS], in0=psb2[0:1, 0:TS],
                                        scalar1=-1.0 / D, scalar2=None, op0=ALU.mult)
                        return v.tensor_scalar(out=scr0[0:1, 0:TS], in0=psb2[32:33, 0:TS],
                                               scalar1=1.0 / D, scalar2=None, op0=ALU.mult)
                    t_ra = S.add("vector", rmath_a,
                                 after=[st["sum"], st["sq"], b2war.get("lnact")],
                                 tag="ln_ra")
                    t_rb = S.add("vector", lambda v: v.tensor_tensor(
                        out=scr1[0:1, 0:TS], in0=negm0[0:1, 0:TS], in1=negm0[0:1, 0:TS],
                        op=ALU.mult), after=[t_ra], tag="ln_rb")
                    st["rm"] = S.add("vector", lambda v: v.tensor_tensor(
                        out=scr0[0:1, 0:TS], in0=scr0[0:1, 0:TS], in1=scr1[0:1, 0:TS],
                        op=ALU.subtract), after=[t_rb], tag="ln_rc")
                    st["ra"] = t_ra

                def s_act():
                    t_ln = S.add("scalar", lambda sc: sc.activation(
                        out=scr1[0:1, 0:TS], in_=scr0[0:1, 0:TS], func=AF.Ln,
                        bias=eps_sb[0:1, 0:1], scale=1.0), after=[st["rm"]], tag="ln_ln")
                    st["rs"] = S.add("scalar", lambda sc: sc.activation(
                        out=rstd0[0:1, 0:TS], in_=scr1[0:1, 0:TS], func=AF.Exp, scale=-0.5),
                        after=[t_ln], tag="ln_exp")
                    b2war["lnact"] = st["rs"]
                    def bcn(t):
                        return t.matmul(psb2[:, 0:TS], ones_rowf_sb[0:1, 0:128],
                                        negm0[0:1, 0:TS], start=True, stop=True)
                    st["bn"] = S.add("tensor", bcn, after=[st["ra"]], tag="ln_bcn")

                def s_adds():
                    add_toks = []
                    prev = None
                    for c0 in range(0, NCH, 2):
                        def ap_add(v, c0=c0):
                            return [v.tensor_tensor(out=slc(c), in0=slc(c),
                                                    in1=psb2[:, 0:TS], op=ALU.add)
                                    for c in (c0, c0 + 1)][-1]
                        prev = S.add("vector", ap_add, after=[st["bn"], prev], tag="ap_add")
                        add_toks.append(prev)
                    st["adds"] = add_toks

                def s_muls():
                    def bcr(t):
                        return t.matmul(psb2[:, 0:TS], ones_rowf_sb[0:1, 0:128],
                                        rstd0[0:1, 0:TS], start=True, stop=True)
                    t_br = S.add("tensor", bcr, after=st["adds"] + [st["rs"]], tag="ln_bcr")
                    mul_toks = []
                    prev = None
                    for c0 in range(0, NCH, 2):
                        def ap_mul(v, c0=c0):
                            return [v.tensor_tensor(out=slc(c), in0=slc(c),
                                                    in1=psb2[:, 0:TS], op=ALU.mult)
                                    for c in (c0, c0 + 1)][-1]
                        prev = S.add("vector", ap_mul, after=[t_br, prev], tag="ap_mul")
                        mul_toks.append(prev)
                    b2war["bank"][ti % 2] = mul_toks
                    xn_toks[ti] = prev

                return [s_stats, s_rmath, s_act, s_adds, s_muls]

            all_pw = {"tau": [], "del": []}
            dw_state = {"prev_pw": [], "prev_ge": [None, None]}

            def mk_dw(ts_i):
                units = []
                pst = {}
                for path, w1, b1p, w2, b2p in (
                        ("tau", t1w, t1b, t2w, t2b),
                        ("del", d1w, d1b, d2w, d2b)):
                    pspw = ps_pw_t if path == "tau" else ps_pw_d
                    for q in range(4):
                        def unit(path=path, w1=w1, b1p=b1p, w2=w2, b2p=b2p,
                                 pspw=pspw, q=q):
                            pw_toks = pst.setdefault(path, [])
                            ge_toks = []
                            for cj in range(4 * q, 4 * q + 4):
                                c, j = divmod(cj, 2)
                                bi = cj % 2
                                def dw_s(v, k, c=c, j=j, bi=bi, w1=w1, ts_i=ts_i):
                                    base = c * CW + ts_i * TS
                                    wcol = w1[:, c * 6 + j * 3 + k: c * 6 + j * 3 + k + 1]
                                    if k == 0:
                                        return v.tensor_scalar(out=dwst[bi][:], in0=xn[:, base: base + TS],
                                                               scalar1=wcol, scalar2=None, op0=ALU.mult)
                                    return v.scalar_tensor_tensor(
                                        out=dwst[bi][:], in0=xn[:, base + k: base + k + TS],
                                        scalar=wcol, in1=dwst[bi][:], op0=ALU.mult, op1=ALU.add)
                                t_dw = S.add("vector", (lambda v, f=dw_s: f(v, 0)),
                                             after=LD3 + [dw_state["prev_ge"][bi]], tag="dw0")
                                t_dw = S.add("vector", (lambda v, f=dw_s: f(v, 1)), after=[t_dw], tag="dw1")
                                t_dw = S.add("vector", (lambda v, f=dw_s: f(v, 2)), after=[t_dw], tag="dw2")
                                def ge(s, c=c, j=j, bi=bi, cj=cj, b1p=b1p):
                                    return s.activation(out=gstage[:, (cj % 4) * TS:((cj % 4) + 1) * TS],
                                                        in_=dwst[bi][:], func=AF_GELU,
                                                        bias=b1p[:, c * 2 + j: c * 2 + j + 1], scale=1.0)
                                war = dw_state["prev_pw"] if q == 0 else pw_toks[-4:]
                                tok_ge = S.add("scalar", ge, after=[t_dw] + war, tag="gelu")
                                dw_state["prev_ge"][bi] = tok_ge
                                ge_toks.append(tok_ge)
                            for tt in range(4):
                                def pw(t, tt=tt, w2=w2, b2p=b2p, pspw=pspw, ts_i=ts_i, q=q):
                                    co = (ts_i * 4 + tt) * 16
                                    r = None
                                    for cj4 in range(4):
                                        cj = q * 4 + cj4
                                        r = t.matmul(pspw[:, co: co + 16],
                                                     gstage[:, cj4 * TS + tt * 128: cj4 * TS + (tt + 1) * 128],
                                                     w2[:, cj * H:(cj + 1) * H],
                                                     start=(cj == 0), stop=False)
                                    if q == 3:
                                        r = t.matmul(pspw[:, co: co + 16],
                                                     ones_row_sb[0:1, 0:128], b2p[0:1, :],
                                                     start=False, stop=True)
                                    return r
                                pw_toks.append(S.add("tensor", pw, after=ge_toks, tag="pw"))
                            if q == 3:
                                all_pw[path] += pw_toks[-4:]
                                dw_state["prev_pw"] = pw_toks[-4:]
                        units.append(unit)
                return units

            for ti in range(NTS):
                emit_xload(ti)
            tok_bq = None
            for ti in range(NTS):
                stages = mk_ln_x(ti)
                dwq = mk_dw(ti - 2) if ti >= 2 else None
                stages[0]()
                if dwq:
                    dwq[0](); dwq[1]()
                stages[1]()
                stages[2]()
                if dwq:
                    dwq[2](); dwq[3]()
                stages[3]()
                if dwq:
                    dwq[4](); dwq[5]()
                stages[4]()
                if dwq:
                    dwq[6](); dwq[7]()
                if ti >= 2 and ti - 2 < len(fq_iters):
                    fq_iters[ti - 2]()
            for ts_t in (6, 7):
                for u in mk_dw(ts_t):
                    u()
            tok_bq = fq_bias()

            def sig_t(s):
                return s.activation(out=tauT[:], in_=ps_pw_t[:], func=AF_SIG)
            tok_sig_t = S.add("scalar", sig_t, after=all_pw["tau"], tag="sig_t")
            def sig_d(s):
                return s.activation(out=delT[:], in_=ps_pw_d[:], func=AF_SIG)
            tok_sig_d = S.add("scalar", sig_d, after=all_pw["del"], tag="sig_d")
            tok_sig = {"tau": tok_sig_t, "del": tok_sig_d}
            def edel(s):
                return s.activation(out=edelT[:], in_=delT[:], func=AF.Exp)
            tok_edel = S.add("scalar", edel, after=[tok_sig["del"]], tag="edel")
            tfm_toks = []
            prev_ev_t = None
            for tt in range(NT):
                def tr(t, tt=tt):
                    return t.transpose(ps_tr[:], tauT[:, tt * H:(tt + 1) * H], ident_sb[:])
                tok_tr = S.add("tensor", tr, after=[tok_sig["tau"], prev_ev_t], tag="tfm_tr")
                def ev(v, tt=tt):
                    return v.tensor_scalar(out=tau_fm[0:16, tt * 128:(tt + 1) * 128],
                                           in0=ps_tr[0:16, :], scalar1=0.125,
                                           scalar2=None, op0=ALU.mult)
                prev_ev_t = S.add("vector", ev, after=[tok_tr], tag="tfm_ev")
                tfm_toks.append(prev_ev_t)
            if "xn" in taps:
                S.dma("sync", tap_outs["xn"], xn.ap(),
                      after=[xn_toks[ti] for ti in range(NTS)], slot="tap")
            if "tauT" in taps:
                S.dma("sync", tap_outs["tauT"], tauT.ap(), after=[tok_sig["tau"]], slot="tap")
            if "edelT" in taps:
                S.dma("sync", tap_outs["edelT"], edelT.ap(), after=[tok_edel], slot="tap")
            if "taufm" in taps:
                S.dma("sync", tap_outs["taufm"], tau_fm.ap(), after=tfm_toks, slot="tap")
            S.emit_block()
            stH.close()
            if stop_after <= 3:
                return nc, tap_outs
            qk_production(weff, biasq, vq_corr, qbuf, True, 0, "q", tok_bq)
            fk_iters, fk_bias, fk_st = fold_path(
                "fk", wk_nat, ckT, kpA_T, kpB_T, wkb_pc, ckb_pc, kpb_pc,
                weff, biask, vk_corr)
            for it in fk_iters:
                it()
            tok_bk = fk_bias()
            if "biask" in taps:
                S.dma("sync", tap_outs["biask"], biask.ap(), after=[], slot="tap")
            if "vkcorr" in taps:
                S.dma("sync", tap_outs["vkcorr"], vk_corr.ap(), after=[], slot="tap")
            S.emit_block()
            qk_production(weff, biask, vk_corr, kbuf, False, H, "k", tok_bk)
        if stop_after <= 4:
            return nc, tap_outs

        vaug_cm = nc.sbuf_tensor("vaug", [128, NT * H * 65], BF16, side="left")
        vaug = vaug_cm.__enter__()
        outT_cm = nc.sbuf_tensor("outT_sb", [128, NCH * D], BF16, side="left")
        outT_sb = outT_cm.__enter__()
        with ExitStack() as stV:
            EE = stV.enter_context
            wvT_sb = EE(nc.sbuf_tensor("wvT_sb", [128, NCH * D], BF16, side="right"))
            wvb_bf = EE(nc.sbuf_tensor("wvb_bf", [1, D], BF16, side="right"))
            ps_v = [EE(nc.psum_tensor("v_ps%d" % i, [128, TS], F32)) for i in range(2)]
            ps_m = EE(nc.psum_tensor("v_psm", [128, H], F32))

            vraw_sb = EE(nc.sbuf_tensor("vraw_sb", [128, TS], F32, side="right")) if "vraw" in taps else None
            for cc in range(NCH):
                S.dma("gpsimd", wvT_sb[:, cc * D:(cc + 1) * D],
                      wvT_in[cc * 128:(cc + 1) * 128, :], slot="vw")
                S.dma("gpsimd", outT_sb[:, cc * D:(cc + 1) * D],
                      outT_in[cc * 128:(cc + 1) * 128, :], slot="vw")
            lvb = S.dma("gpsimd", wvb_bf[:], wvb_row_in, slot="vw")
            lv = lo = lvb

            def mrow(v):
                return v.tensor_tensor(out=nrow[0:1, 0:H], in0=nrm[0:1, 0:H],
                                       in1=nrm[0:1, H:2 * H], op=ALU.mult)
            tok_mr = S.add("vector", mrow, tag="mrow")
            def mln(s):
                s.activation(out=nrow[0:1, 0:H], in_=nrow[0:1, 0:H], func=AF.Ln)
                return s.activation(out=nrow[0:1, 0:H], in_=nrow[0:1, 0:H], func=AF.Exp, scale=0.5)
            tok_mln = S.add("scalar", mln, after=[tok_mr], tag="mln")
            def mneg(v):
                return v.tensor_scalar(out=negm_bf[0:1, 0:H], in0=nrow[0:1, 0:H],
                                       scalar1=-1.0, scalar2=None, op0=ALU.mult)
            tok_mneg = S.add("vector", mneg, after=[tok_mln], tag="mneg")
            def mbc(t):
                r = None
                for h in range(H):
                    r = t.matmul(ps_m[:, h:h + 1], ones_row_sb[0:1, 0:128],
                                 negm_bf[0:1, h:h + 1], start=True, stop=True)
                return r
            tok_mbc = S.add("tensor", mbc, after=[tok_mneg], tag="mbc")
            tok_mev = S.add("vector", lambda v: v.tensor_copy(mbias[:], ps_m[:]),
                            after=[tok_mbc], tag="mev")

            vaug_toks = []
            prev_vev = [None, None]
            for tt in range(NT):
                def augc(v, tt=tt):
                    return v.tensor_copy(apx(vaug, 0, 128, tt * 1040 + 64, [(65, H)]),
                                         edelT[:, tt * H:(tt + 1) * H])
                tok_aug = S.add("vector", augc, tag="aug")
                vaug_toks.append(tok_aug)
                for dh in range(2):
                    bi = (tt * 2 + dh) % 2
                    def vmm(t, tt=tt, dh=dh, bi=bi):
                        r = None
                        for ic in range(NCH):
                            r = t.matmul(ps_v[bi][:, 0:TS],
                                         xn[:, ic * CW + 1 + tt * 128: ic * CW + 1 + (tt + 1) * 128],
                                         wvT_sb[:, ic * D + dh * TS: ic * D + (dh + 1) * TS],
                                         start=(ic == 0), stop=False)
                        return t.matmul(ps_v[bi][:, 0:TS], ones_row_sb[0:1, 0:128],
                                        wvb_bf[0:1, dh * TS:(dh + 1) * TS], start=False, stop=True)
                    tok_vmm = S.add("tensor", vmm, after=[lv, lvb, prev_vev[bi]], tag="vmm")
                    def vev(v, tt=tt, dh=dh, bi=bi):
                        r = None
                        for hh in range(8):
                            h = dh * 8 + hh
                            r = v.tensor_scalar(
                                out=vaug[:, tt * 1040 + h * 65: tt * 1040 + h * 65 + 64],
                                in0=ps_v[bi][:, hh * 64:(hh + 1) * 64],
                                scalar1=edelT[:, tt * H + h: tt * H + h + 1],
                                scalar2=None, op0=ALU.mult)
                        return r
                    if tt == 0 and dh == 0 and "vraw" in taps:
                        tr = S.add("vector", (lambda v, bi=bi: v.tensor_copy(
                            vraw_sb[:], ps_v[bi][:, 0:TS])), after=[tok_vmm], tag="vrawcp")
                        S.dma("sync", tap_outs["vraw"], vraw_sb.ap(), after=[tr], slot="tap")
                    tok_vev = S.add("vector", vev, after=[tok_vmm], tag="vev")
                    prev_vev[bi] = tok_vev
                    vaug_toks.append(tok_vev)
            if "xn2nd" in taps:
                S.dma("sync", tap_outs["xn2nd"], xn.ap(), after=[], slot="tap")
            if "mbias" in taps:
                S.dma("sync", tap_outs["mbias"], mbias.ap(), after=[tok_mev], slot="tap")
            if "vaug" in taps:
                S.dma("sync", tap_outs["vaug"], vaug.ap(), after=vaug_toks, slot="tap")
            S.emit_block()
        if stop_after <= 5:
            return nc, tap_outs

        xn_cm.__exit__(None, None, None)
        with ExitStack() as st5:
            EE = st5.enter_context
            qsuper = [EE(nc.sbuf_tensor("qsuper%d" % i, [128, NCH * QS], BF16, side="right"))
                      for i in range(2)]
            khp = [EE(nc.sbuf_tensor("khp%d" % i, [128, L], BF16, side="right")) for i in range(2)]
            probs0 = [EE(nc.sbuf_tensor("pb0%d" % i, [128, QS], BF16, side="right")) for i in range(2)]
            probs1 = [EE(nc.sbuf_tensor("pb1%d" % i, [128, QS], BF16, side="right")) for i in range(2)]
            attn_fm = [EE(nc.sbuf_tensor("attn_fm%d" % i, [128, NCH * QS], BF16, side="right"))
                       for i in range(2)]
            zr0b = EE(nc.sbuf_tensor("zr0b", [1, QS], BF16, side="right"))
            zr1b = EE(nc.sbuf_tensor("zr1b", [1, QS], BF16, side="right"))
            zrecf = EE(nc.sbuf_tensor("zrecf", [128, QS], F32, side="right"))
            attn_st = EE(nc.sbuf_tensor("attn_st", [128, QS], BF16, side="right"))
            outb_pc = EE(nc.sbuf_tensor("outb_pc_sb", [128, NCH], F32, side="right"))
            proj_t = EE(nc.sbuf_tensor("proj_t", [128, NCH * TS], BF16, side="right"))
            tmpf5 = [EE(nc.sbuf_tensor("tmpf50", [128, TS], F32, side="right"))] * 2
            xres = EE(nc.sbuf_tensor("xres", [128, NCH * TS], F32, side="right"))
            xn2_t = EE(nc.sbuf_tensor("xn2_t", [128, NCH * TS], BF16, side="right"))
            sq_all = EE(nc.sbuf_tensor("sq_all", [128, NCH * TS], BF16, side="right"))
            zop = EE(nc.psum_tensor("b5_zop", [128, QS], F32))
            ps_bc5p = EE(nc.psum_tensor("b5_bcp", [128, TS], F32))

            lob = S.dma("sync", outb_pc[:], outb_pc_in, slot="tap")

            tw_chunks = ([(tw1bf, tw1T_in, i * 512, 512) for i in range(16)] +
                         [(tw2bf, tw2T_in, i * 2048, 2048) for i in range(16)])
            t_twcast = None
            _twci = [0, 0]

            def emit_twcast(pace):
                if _twci[0] < len(tw_chunks):
                    dstb, srcb, r0, nr = tw_chunks[_twci[0]]
                    _twci[0] += 1
                    return S.dma("gpsimd", dstb[r0:r0 + nr, :], srcb[r0:r0 + nr, :],
                                 after=[pace], slot="twc")
                return None

            kwar = [None, None]
            qwar = [None, None]
            x2_spills = []
            xn2_spills = []
            prev_xres_user = None
            prev_xn2_user = None
            prev_n0 = prev_n1 = None
            prev_pa0 = []
            prev_pa1 = []
            prev_av0_tail = {}
            prev_av1_tail = {}
            prev_e0_tail = {-1: None, -2: None}
            prev_e1_tail = {-1: None, -2: None}
            prev_zbc = None
            pending_znorm = None
            zop_chain = [None]
            attnfm_war = [None, None]
            pb_bank_war = [None]
            sq_war = [None]
            pb_plan = None

            def make_phaseB(s, afm, attn_deps):
                st = {}

                def pmm_one(oc):
                    def go():
                        def pmm(t, oc=oc):
                            r = None
                            for hc in range(NCH):
                                r = t.matmul(zop[:, 0:TS],
                                             outT_sb[:, hc * D + oc * 128: hc * D + (oc + 1) * 128],
                                             afm[:, hc * QS: hc * QS + TS],
                                             start=(hc == 0), stop=(hc == NCH - 1))
                            return r
                        t_pm = S.add("tensor", pmm,
                                     after=attn_deps + [zop_chain[0]], tag="pmm")
                        def pevd(v, oc=oc):
                            return v.tensor_scalar(
                                out=proj_t[:, oc * TS:(oc + 1) * TS],
                                in0=zop[:, 0:TS], scalar1=outb_pc[:, oc:oc + 1],
                                scalar2=None, op0=ALU.add)
                        t_pe = S.add("vector", pevd, after=[t_pm, lob], tag="pevd")
                        zop_chain[0] = t_pe
                        st.setdefault("pev", []).append(t_pe)
                        if oc == 7:
                            attnfm_war[s % 2] = t_pm
                    return go

                def xres_load():
                    st["xr"] = S.dma("gpsimd", xres.ap(),
                                     bass.AP(x_in.tensor, s * QS,
                                             [[NCH * L, 128], [L, NCH], [1, TS]]),
                                     after=[prev_xres_user], slot="xresg", tag="xres")

                def sq_half(slc, h, dep_extra, key):
                    def sqf(v, h=h):
                        return [v.tensor_tensor(out=sq_all[:, c * TS:(c + 1) * TS],
                                                in0=slc(c), in1=slc(c), op=ALU.mult)
                                for c in range(4 * h, 4 * h + 4)][-1]
                    st.setdefault(key, []).append(
                        S.add("vector", sqf, after=dep_extra + [sq_war[0]], tag="ln_sq"))

                def stats_sum(slc, dep_extra):
                    def mm_sum(t):
                        r = None
                        for c in range(NCH):
                            r = t.matmul(ps_bc5p[0:1, 0:TS], ones_col_sb[:], slc(c),
                                         start=(c == 0), stop=(c == NCH - 1))
                        return r
                    return S.add("tensor", mm_sum,
                                 after=dep_extra + CONSTS + [pb_bank_war[0]], tag="ln_sum")

                def stats_sq(sq_deps):
                    def mmsq(t):
                        r = None
                        for c in range(NCH):
                            r = t.matmul(ps_bc5p[32:33, 0:TS], ones_col_sb[:],
                                         sq_all[:, c * TS:(c + 1) * TS],
                                         start=(c == 0), stop=(c == NCH - 1))
                        return r
                    tok = S.add("tensor", mmsq, after=sq_deps + [pb_bank_war[0]],
                                tag="ln_sqmm")
                    sq_war[0] = tok
                    return tok

                def ln_math_dve(t_sum, t_sq):
                    def rmath_a(v):
                        v.tensor_scalar(out=negm0[0:1, 0:TS], in0=ps_bc5p[0:1, 0:TS],
                                        scalar1=-1.0 / D, scalar2=None, op0=ALU.mult)
                        return v.tensor_scalar(out=scr0[0:1, 0:TS], in0=ps_bc5p[32:33, 0:TS],
                                               scalar1=1.0 / D, scalar2=None, op0=ALU.mult)
                    t_ra = S.add("vector", rmath_a, after=[t_sum, t_sq], tag="ln_ra")
                    t_rb = S.add("vector", lambda v: v.tensor_tensor(
                        out=scr1[0:1, 0:TS], in0=negm0[0:1, 0:TS], in1=negm0[0:1, 0:TS],
                        op=ALU.mult), after=[t_ra], tag="ln_rb")
                    t_rm = S.add("vector", lambda v: v.tensor_tensor(
                        out=scr0[0:1, 0:TS], in0=scr0[0:1, 0:TS], in1=scr1[0:1, 0:TS],
                        op=ALU.subtract), after=[t_rb], tag="ln_rc")
                    return t_ra, t_rm

                def ln_math_act(t_rm):
                    t_ln = S.add("scalar", lambda sc: sc.activation(
                        out=scr1[0:1, 0:TS], in_=scr0[0:1, 0:TS], func=AF.Ln,
                        bias=eps_sb[0:1, 0:1], scale=1.0), after=[t_rm], tag="ln_ln")
                    return S.add("scalar", lambda sc: sc.activation(
                        out=rstd0[0:1, 0:TS], in_=scr1[0:1, 0:TS], func=AF.Exp, scale=-0.5),
                        after=[t_ln], tag="ln_exp")

                def emit_bcn(t_ra):
                    def bcn(t):
                        return t.matmul(ps_bc5p[:, 0:TS], ones_rowf_sb[0:1, 0:128],
                                        negm0[0:1, 0:TS], start=True, stop=True)
                    return S.add("tensor", bcn, after=[t_ra], tag="ln_bcn")

                def ln_addpass(slc, t_bn):
                    toks = []
                    prev = None
                    for c0 in range(0, NCH, 2):
                        def ap_add(v, c0=c0):
                            return [v.tensor_tensor(out=slc(c), in0=slc(c),
                                                    in1=ps_bc5p[:, 0:TS], op=ALU.add)
                                    for c in (c0, c0 + 1)][-1]
                        prev = S.add("vector", ap_add, after=[t_bn, prev], tag="ap_add")
                        toks.append(prev)
                    return toks

                def ln_bcr(add_toks, t_rs):
                    def bcr(t):
                        return t.matmul(ps_bc5p[:, 0:TS], ones_rowf_sb[0:1, 0:128],
                                        rstd0[0:1, 0:TS], start=True, stop=True)
                    return S.add("tensor", bcr, after=add_toks + [t_rs], tag="ln_bcr")

                slc1 = lambda c: proj_t[:, c * TS:(c + 1) * TS]
                slc2 = lambda c: xn2_t[:, c * TS:(c + 1) * TS]

                def sq1h(h):
                    def go():
                        sq_half(slc1, h, st["pev"], "sq1")
                    return go

                def stats1_sum():
                    st["sum1"] = stats_sum(slc1, st["pev"])

                def stats1_sq():
                    st["sq1mm"] = stats_sq(st["sq1"])

                def ln1_math():
                    st["mathn"] = ln_math_dve(st["sum1"], st["sq1mm"])

                def ln1_mact():
                    st["rs1"] = ln_math_act(st["mathn"][1])

                def ln1_bcn():
                    st["bcn1"] = emit_bcn(st["mathn"][0])

                def ln1_add():
                    st["addp"] = ln_addpass(slc1, st["bcn1"])

                def ln1_bcr():
                    st["bcr1"] = ln_bcr(st["addp"], st["rs1"])

                def ln1_out():
                    t_br = st["bcr1"]
                    prev = None
                    toks = []
                    for c in range(NCH):
                        def ap_mul(v, c=c):
                            return v.tensor_tensor(out=tmpf5[0][:],
                                                   in0=proj_t[:, c * TS:(c + 1) * TS],
                                                   in1=ps_bc5p[:, 0:TS], op=ALU.mult)
                        t_m = S.add("vector", ap_mul, after=[t_br, prev, st["xr"]], tag="ap_mul")
                        def ap_out(v, c=c):
                            return v.tensor_tensor(out=xres[:, c * TS:(c + 1) * TS],
                                                   in0=tmpf5[0][:],
                                                   in1=xres[:, c * TS:(c + 1) * TS], op=ALU.add)
                        prev = S.add("vector", ap_out, after=[t_m], tag="ap_out")
                        toks.append(prev)
                    st["outp"] = toks
                    st["mul1"] = toks

                def x2_and_cp():
                    nonlocal prev_xres_user, prev_xn2_user
                    tok_sp = S.dma("gpsimd",
                                   bass.AP(x2buf.tensor, s * QS,
                                           [[NCH * L, 128], [L, NCH], [1, TS]]),
                                   xres.ap(), after=st["outp"], slot="x2og", tag="x2spill")
                    x2_spills.append(tok_sp)
                    prev_xres_user = tok_sp
                    def cpf(v):
                        return [v.tensor_copy(xn2_t[:, c * TS:(c + 1) * TS],
                                              xres[:, c * TS:(c + 1) * TS])
                                for c in range(NCH)][-1]
                    st["cp"] = S.add("vector", cpf,
                                     after=st["outp"] + [prev_xn2_user], tag="xn2cp")

                def sq2h(h):
                    def go():
                        sq_half(slc2, h, [st["cp"]], "sq2")
                    return go

                def stats2_sum():
                    st["sum2"] = stats_sum(slc2, [st["cp"]] + st["mul1"])

                def stats2_sq():
                    st["sq2mm"] = stats_sq(st["sq2"])

                def ln2_math():
                    st["mathn2"] = ln_math_dve(st["sum2"], st["sq2mm"])

                def ln2_mact():
                    st["rs2"] = ln_math_act(st["mathn2"][1])

                def ln2_bcn():
                    st["bcn2"] = emit_bcn(st["mathn2"][0])

                def ln2_add():
                    st["addp2"] = ln_addpass(slc2, st["bcn2"])

                def ln2_out():
                    nonlocal prev_xn2_user
                    t_br = ln_bcr(st["addp2"], st["rs2"])
                    prev = None
                    mul_toks = []
                    for c0 in range(0, NCH, 2):
                        def ap_mul2(v, c0=c0):
                            return [v.tensor_tensor(out=slc2(c), in0=slc2(c),
                                                    in1=ps_bc5p[:, 0:TS], op=ALU.mult)
                                    for c in (c0, c0 + 1)][-1]
                        prev = S.add("vector", ap_mul2, after=[t_br, prev], tag="ap_mul2")
                        mul_toks.append(prev)
                    pb_bank_war[0] = prev
                    tok_sp = S.dma("gpsimd",
                                   bass.AP(xn2buf.tensor, s * QS,
                                           [[NCH * L, 128], [L, NCH], [1, TS]]),
                                   xn2_t.ap(), after=mul_toks, slot="xn2og", tag="xn2spill")
                    xn2_spills.append(tok_sp)
                    prev_xn2_user = tok_sp

                return {(0, 8): [pmm_one(0)], (0, 20): [pmm_one(1)],
                        (1, 4): [pmm_one(2)], (1, 16): [pmm_one(3)],
                        (2, 4): [pmm_one(4)], (2, 16): [pmm_one(5)],
                        (3, 4): [pmm_one(6)], (3, 16): [pmm_one(7)],
                        (3, 24): [xres_load],
                        (4, 4): [sq1h(0)], (4, 12): [sq1h(1)],
                        (4, 20): [stats1_sum], (4, 26): [stats1_sq],
                        (5, 2): [ln1_math], (5, 8): [ln1_mact],
                        (5, 12): [ln1_bcn], (5, 18): [ln1_add],
                        (6, 4): [ln1_bcr], (6, 10): [ln1_out], (6, 22): [x2_and_cp],
                        (7, 2): [sq2h(0)], (7, 4): [sq2h(1)],
                        (7, 8): [stats2_sum], (7, 10): [stats2_sq],
                        (7, 14): [ln2_math], (7, 18): [ln2_mact],
                        (7, 20): [ln2_bcn], (7, 22): [ln2_add],
                        (7, 26): [ln2_out]}

            for sidx in range(NQS):
                with ExitStack() as stA:
                    PA = stA.enter_context
                    psS0 = [PA(nc.psum_tensor("b5_psA%d_%d" % (i, sidx), [128, QS], F32)) for i in range(2)]
                    psS1 = [PA(nc.psum_tensor("b5_psB%d_%d" % (i, sidx), [128, QS], F32)) for i in range(2)]
                    psOA = PA(nc.psum_tensor("b5_psOA_%d" % sidx, [65, QS], F32))
                    psOB = PA(nc.psum_tensor("b5_psOB_%d" % sidx, [65, QS], F32))

                    afm_cur = attn_fm[sidx % 2]
                    qsb = qsuper[sidx % 2]
                    tok_q = S.dma("sync", qsb.ap(),
                                  bass.AP(qbuf.tensor, sidx * QS,
                                          [[NCH * L, 128], [L, NCH], [1, QS]]),
                                  after=[qwar[sidx % 2]], slot="qsup%d" % (sidx % 2),
                                  tag="qload")
                    attn_toks = []
                    last_qk_super = None

                    def flush_znorm():
                        nonlocal pending_znorm, prev_zbc
                        nonlocal prev_n0, prev_n1
                        zpc, t_zc0, t_zc1, t_evA, t_evB, zafm, zpar, ztoks = pending_znorm
                        pending_znorm = None

                        def zbc(t):
                            t.matmul(zop[0:64, 0:QS], ones_row_sb[0:1, 0:64],
                                     zr0b[0:1, 0:QS], start=True, stop=True)
                            return t.matmul(zop[64:128, 0:QS], ones_row_sb[0:1, 0:64],
                                            zr1b[0:1, 0:QS], start=True, stop=True)
                        t_zbc = S.add("tensor", zbc, after=[t_zc0, t_zc1, zop_chain[0]],
                                      tag="zbc")
                        t_zrec = S.add("vector",
                                       lambda v: v.reciprocal(zrecf[:, 0:QS], zop[:, 0:QS]),
                                       after=[t_zbc, prev_n0, prev_n1],
                                       tag="zrec")
                        zop_chain[0] = t_zrec

                        def nrm0(v, zpc=zpc, afm=zafm):
                            return v.tensor_tensor(out=afm[0:64, zpc * QS:(zpc + 1) * QS],
                                                   in0=attn_st[0:64, 0:QS],
                                                   in1=zrecf[0:64, 0:QS], op=ALU.mult)
                        t_n0 = S.add("vector", nrm0,
                                     after=[t_zrec, t_evA, attnfm_war[zpar]], tag="nrm0")

                        def nrm1(v, zpc=zpc, afm=zafm):
                            return v.tensor_tensor(out=afm[64:128, zpc * QS:(zpc + 1) * QS],
                                                   in0=attn_st[64:128, 0:QS],
                                                   in1=zrecf[64:128, 0:QS], op=ALU.mult)
                        t_n1 = S.add("vector", nrm1,
                                     after=[t_zrec, t_evB, attnfm_war[zpar]], tag="nrm1")
                        ztoks.extend([t_n0, t_n1])
                        prev_zbc = t_zbc
                        prev_n0, prev_n1 = t_n0, t_n1

                    for pc in range(NCH):
                        h0, h1 = 2 * pc, 2 * pc + 1
                        tok_k = S.dma("sync", khp[pc % 2].ap(),
                                      kbuf[:, pc * L:(pc + 1) * L],
                                      after=[kwar[pc % 2]],
                                      slot="khp%d" % (pc % 2), tag="kload")
                        tok_exp0 = dict(prev_e0_tail)
                        tok_exp1 = dict(prev_e1_tail)
                        tok_av0 = dict(prev_av0_tail)
                        tok_av1 = dict(prev_av1_tail)
                        tok_qk0 = {}
                        tok_qk1 = {}
                        def qk0(t, pc=pc, kt=0, qsb=qsb):
                            return t.matmul(psS0[kt % 2][:, 0:QS],
                                            khp[pc % 2][0:64, kt * 128:(kt + 1) * 128],
                                            qsb[0:64, pc * QS: (pc + 1) * QS],
                                            start=True, stop=True)
                        def qk1(t, pc=pc, kt=0, qsb=qsb):
                            return t.matmul(psS1[kt % 2][:, 0:QS],
                                            khp[pc % 2][64:128, kt * 128:(kt + 1) * 128],
                                            qsb[64:128, pc * QS: (pc + 1) * QS],
                                            start=True, stop=True)
                        for kt0 in range(2):
                            tok_qk0[kt0] = S.add("tensor", (lambda t, kt=kt0, f=qk0: f(t, kt=kt)),
                                                 after=[tok_q, tok_k, tok_exp0[kt0 - 2]], tag="qk0")
                            tok_qk1[kt0] = S.add("tensor", (lambda t, kt=kt0, f=qk1: f(t, kt=kt)),
                                                 after=[tok_q, tok_k, tok_exp1[kt0 - 2]], tag="qk1")
                        if pending_znorm is not None:
                            flush_znorm()
                        _twci[1] += 1
                        if _twci[1] % 2 == 0:
                            tcast = emit_twcast(tok_k)
                            if tcast is not None:
                                t_twcast = tcast
                        for kt in range(NT):
                            if pb_plan is not None:
                                for f in pb_plan.get((pc, kt), []):
                                    f()
                            def ex0(sc, kt=kt, h0=h0):
                                return sc.activation(out=probs0[kt % 2][:], in_=psS0[kt % 2][:, 0:QS],
                                                     func=AF.Exp, bias=mbias[:, h0:h0 + 1], scale=1.0)
                            tok_exp0[kt] = S.add("scalar", ex0,
                                                 after=[tok_qk0[kt], tok_av0.get(kt - 2)], tag="exp0")
                            def ex1(sc, kt=kt, h1=h1):
                                return sc.activation(out=probs1[kt % 2][:], in_=psS1[kt % 2][:, 0:QS],
                                                     func=AF.Exp, bias=mbias[:, h1:h1 + 1], scale=1.0)
                            tok_exp1[kt] = S.add("scalar", ex1,
                                                 after=[tok_qk1[kt], tok_av1.get(kt - 2)], tag="exp1")
                            if kt + 2 < NT:
                                tok_qk0[kt + 2] = S.add("tensor", (lambda t, kt=kt, f=qk0: f(t, kt=kt + 2)),
                                                        after=[tok_exp0[kt]], tag="qk0")
                                tok_qk1[kt + 2] = S.add("tensor", (lambda t, kt=kt, f=qk1: f(t, kt=kt + 2)),
                                                        after=[tok_exp1[kt]], tag="qk1")
                            def av0(t, kt=kt, h0=h0):
                                return t.matmul(psOA[:, 0:QS],
                                                vaug[:, kt * 1040 + h0 * 65: kt * 1040 + h0 * 65 + 65],
                                                probs0[kt % 2][:, 0:QS],
                                                start=(kt == 0), stop=(kt == NT - 1))
                            tok_av0[kt] = S.add("tensor", av0,
                                after=[tok_exp0[kt]] + (prev_pa0 if kt == 0 else []), tag="av0")
                            def av1(t, kt=kt, h1=h1):
                                return t.matmul(psOB[:, 0:QS],
                                                vaug[:, kt * 1040 + h1 * 65: kt * 1040 + h1 * 65 + 65],
                                                probs1[kt % 2][:, 0:QS],
                                                start=(kt == 0), stop=(kt == NT - 1))
                            tok_av1[kt] = S.add("tensor", av1,
                                after=[tok_exp1[kt]] + (prev_pa1 if kt == 0 else []), tag="av1")
                        last_qk_super = tok_qk1[NT - 1]
                        kwar[pc % 2] = last_qk_super
                        t_zc0 = S.add("vector",
                                      lambda v: v.tensor_copy(zr0b[0:1, 0:QS], psOA[64:65, 0:QS]),
                                      after=[tok_av0[NT - 1], prev_zbc], tag="zc0")
                        t_evA = S.add("vector",
                                      lambda v: v.tensor_copy(attn_st[0:64, 0:QS], psOA[0:64, 0:QS]),
                                      after=[tok_av0[NT - 1], prev_n0], tag="evA")
                        t_zc1 = S.add("vector",
                                      lambda v: v.tensor_copy(zr1b[0:1, 0:QS], psOB[64:65, 0:QS]),
                                      after=[tok_av1[NT - 1], prev_zbc], tag="zc1")
                        t_evB = S.add("vector",
                                      lambda v: v.tensor_copy(attn_st[64:128, 0:QS], psOB[0:64, 0:QS]),
                                      after=[tok_av1[NT - 1], prev_n1], tag="evB")
                        pending_znorm = (pc, t_zc0, t_zc1, t_evA, t_evB,
                                         afm_cur, sidx % 2, attn_toks)
                        prev_pa0 = [t_zc0, t_evA]
                        prev_pa1 = [t_zc1, t_evB]
                        prev_e0_tail = {-1: tok_exp0[NT - 1], -2: tok_exp0[NT - 2]}
                        prev_e1_tail = {-1: tok_exp1[NT - 1], -2: tok_exp1[NT - 2]}
                        prev_av0_tail = {-2: tok_av0[NT - 2], -1: tok_av0[NT - 1]}
                        prev_av1_tail = {-2: tok_av1[NT - 2], -1: tok_av1[NT - 1]}
                    qwar[sidx % 2] = last_qk_super
                    pb_plan = make_phaseB(sidx, afm_cur, attn_toks)
            flush_znorm()
            for key in sorted(pb_plan.keys()):
                for f in pb_plan[key]:
                    f()
            S.emit_block()
        if "x2" in taps:
            S.dma("sync", tap_outs["x2"], x2buf, after=[], slot="tap")
        if "xn2" in taps:
            S.dma("sync", tap_outs["xn2"], xn2buf, after=[], slot="tap")
        S.emit_block()
        if stop_after <= 6:
            return nc, tap_outs

        outT_cm.__exit__(None, None, None)
        vaug_cm.__exit__(None, None, None)
        for _cm in reversed(_tau3_cms):
            _cm.__exit__(None, None, None)
        for _cm in reversed(_tau_cms):
            _cm.__exit__(None, None, None)
        mw1_cm = nc.sbuf_tensor("mW1T", [128, NCH * FF], BF16, side="left")
        mW1T = mw1_cm.__enter__()
        mw2_cm = nc.sbuf_tensor("mW2T", [128, 32 * D], BF16, side="left")
        mW2T = mw2_cm.__enter__()
        with ExitStack() as st6:
            EE = st6.enter_context
            gw1T_sb = EE(nc.sbuf_tensor("gw1T_sb", [128, NCH * D], BF16, side="right"))
            gw2T_sb = EE(nc.sbuf_tensor("gw2T_sb", [128, NCH * E], BF16, side="right"))
            gb1_sb = EE(nc.sbuf_tensor("gb1_sb", [128, NCH], F32, side="right"))
            gb2_sb = EE(nc.sbuf_tensor("gb2_sb", [E, 1], F32, side="right"))
            xn2s = EE(nc.sbuf_tensor("g_xn2s", [128, NCH * TS], BF16, side="right"))
            h_sb = [EE(nc.sbuf_tensor("g_h0", [128, TS], BF16, side="right"))] * 2
            tbs = [EE(nc.sbuf_tensor("g_tbs%d" % i, [E, 128], BF16, side="right")) for i in range(2)]
            b1pc_sb = EE(nc.sbuf_tensor("b1pc_sb", [128, 32], F32, side="right"))
            b2pc_sb = EE(nc.sbuf_tensor("b2pc_sb", [128, 8], F32, side="right"))
            acc6 = EE(nc.sbuf_tensor("acc6", [128, FF], F32, side="right"))
            twb = [EE(nc.sbuf_tensor("twb%d" % i, [128, FF // 2], BF16, side="right"))
                   for i in range(3)]
            ps_h = [EE(nc.psum_tensor("g_psh%d" % i, [128, TS], F32)) for i in range(2)]
            ps_g2 = EE(nc.psum_tensor("g_psg2", [E, TS], F32))
            ps_gr = EE(nc.psum_tensor("ps_gr", [1, E], F32))
            ps_gb = EE(nc.psum_tensor("ps_gb", [128, E], F32))
            ps_mb = EE(nc.psum_tensor("ps_mb", [128, 40], F32))

            lg = []
            for cc in range(NCH):
                lg.append(S.dma("gpsimd", gw1T_sb[:, cc * D:(cc + 1) * D],
                                gw1T_in[cc * 128:(cc + 1) * 128, :], slot="gw"))
                lg.append(S.dma("gpsimd", gw2T_sb[:, cc * E:(cc + 1) * E],
                                gw2T_in[cc * 128:(cc + 1) * 128, :], slot="gw"))
            lg.append(S.dma("sync", gb1_sb[:], gb1_pc_in, slot="gwf"))
            lg.append(S.dma("sync", gb2_sb[:], gb2_col_in, slot="gwf"))
            lb1 = S.dma("sync", b1pc_sb[:, 0:32], b1pc_in, slot="gwf")
            lb2 = S.dma("sync", b2pc_sb[:, 0:8], b2pc_in, slot="gwf")

            prev_he = [None, None]
            prev_g2 = None
            prev_acc = None
            prev_xl = None
            for tt in range(NTS):
                tok_x = S.dma("sync", xn2s.ap(),
                              bass.AP(xn2buf.tensor, tt * TS,
                                      [[NCH * L, 128], [L, NCH], [1, TS]]),
                              after=[prev_xl], slot="gxn", tag="gxn")
                last_g2 = None
                last_h = None
                for oc in range(NCH):
                    bi = oc % 2
                    def hmm_(t, oc=oc, bi=bi):
                        r = None
                        for ic in range(NCH):
                            r = t.matmul(ps_h[bi][:, 0:TS],
                                         gw1T_sb[:, ic * D + oc * 128: ic * D + (oc + 1) * 128],
                                         xn2s[:, ic * TS:(ic + 1) * TS],
                                         start=(ic == 0), stop=(ic == NCH - 1))
                        return r
                    t_h = S.add("tensor", hmm_, after=lg + [tok_x, prev_he[bi]], tag="gh")
                    last_h = t_h
                    def hev(sc, bi=bi, oc=oc):
                        return sc.activation(out=h_sb[bi][:], in_=ps_h[bi][:, 0:TS],
                                             func=AF.Relu, bias=gb1_sb[:, oc:oc + 1], scale=1.0)
                    t_he = S.add("scalar", hev, after=[t_h, last_g2, prev_g2], tag="ghe")
                    def g2mm(t, oc=oc, bi=bi):
                        return t.matmul(ps_g2[0:E, 0:TS], gw2T_sb[:, oc * E:(oc + 1) * E],
                                        h_sb[bi][:, 0:TS], start=(oc == 0), stop=(oc == NCH - 1))
                    last_g2 = S.add("tensor", g2mm, after=[t_he, prev_g2], tag="g2mm")
                    prev_he[bi] = last_g2
                prev_xl = last_h
                t_red = S.add("vector", lambda v: v.tensor_reduce(
                    out=g_sb[0:E, 1:2], in_=ps_g2[0:E, 0:TS], axis=AX.X, op=ALU.add),
                    after=[last_g2], tag="gred")
                prev_acc = S.add("vector", lambda v: v.tensor_tensor(
                    out=g_sb[0:E, 0:1], in0=g_sb[0:E, 0:1], in1=g_sb[0:E, 1:2], op=ALU.add),
                    after=[t_red], tag="gacc")
                prev_g2 = prev_acc
            t_gs = S.add("vector", lambda v: v.tensor_scalar(
                out=g_sb[0:E, 0:1], in0=g_sb[0:E, 0:1], scalar1=1.0 / L, scalar2=None,
                op0=ALU.mult), after=[prev_acc], tag="gscale")
            t_gb = S.add("vector", lambda v: v.tensor_tensor(
                out=g_sb[0:E, 0:1], in0=g_sb[0:E, 0:1], in1=gb2_sb[0:E, 0:1], op=ALU.add),
                after=[t_gs, lg[-1]], tag="gbias")
            t_gbf = S.add("vector", lambda v: v.tensor_copy(g_bf[0:E, 0:1], g_sb[0:E, 0:1]),
                          after=[t_gb], tag="gbf")
            t_grt = S.add("tensor", lambda t: t.transpose(ps_gr[0:1, 0:E], g_sb[0:E, 0:1],
                                                          ident_sb[0:E, 0:E]),
                          after=[t_gb], tag="grt")
            t_growb = S.add("vector", lambda v: v.tensor_copy(grow_bf[0:1, 0:E], ps_gr[0:1, 0:E]),
                            after=[t_grt], tag="growb")
            if "g" in taps:
                S.dma("sync", tap_outs["g"], g_sb[0:E, 0:2], after=[t_gb], slot="tap")

            def gbc(t):
                r = None
                for e in range(E):
                    r = t.matmul(ps_gb[:, e:e + 1], ones_row_sb[0:1, 0:128],
                                 grow_bf[0:1, e:e + 1], start=True, stop=True)
                return r
            t_gbc = S.add("tensor", gbc, after=[t_growb], tag="gbc")
            t_gbce = S.add("vector", lambda v: v.tensor_copy(g_bc[:], ps_gb[:]),
                           after=[t_gbc], tag="gbce")
            if "gbc" in taps:
                S.dma("sync", tap_outs["gbc"], g_bc.ap(), after=[t_gbce], slot="tap")

            for nm, src_in, bpc, npc, cbase, dstpc, lb in (
                    ("mb1", tb1_in, b1pc_sb, 32, 0, mb1_pc, lb1),
                    ("mb2", tb2_in, b2pc_sb, 8, 32, mb2_pc, lb2)):
                mb_toks = []
                prev_tb = [None, None]
                for cf in range(npc):
                    ltb = S.dma("gpsimd", tbs[cf % 2][0:E, 0:128],
                                src_in[0:E, cf * 128:(cf + 1) * 128],
                                after=[prev_tb[cf % 2]], slot="mbld")
                    def mbm(t, cf=cf, cbase=cbase):
                        return t.matmul(ps_mb[:, cbase + cf: cbase + cf + 1],
                                        tbs[cf % 2][0:E, 0:128], g_bf[0:E, 0:1],
                                        start=True, stop=True)
                    t_m = S.add("tensor", mbm, after=[ltb, t_gbf], tag="mbm")
                    prev_tb[cf % 2] = t_m
                    mb_toks.append(t_m)
                def mbe(v, dstpc=dstpc, bpc=bpc, npc=npc, cbase=cbase):
                    return v.tensor_tensor(out=dstpc[:, 0:npc],
                                           in0=ps_mb[:, cbase:cbase + npc],
                                           in1=bpc[:, 0:npc], op=ALU.add)
                t_te = S.add("vector", mbe, after=mb_toks + [lb], tag="mbe")
                if nm == "mb1" and "mb1pc" in taps:
                    S.dma("sync", tap_outs["mb1pc"], mb1_pc.ap(), after=[t_te], slot="tap")
                if nm == "mb2" and "mb2pc" in taps:
                    S.dma("sync", tap_outs["mb2pc"], mb2_pc.ap(), after=[t_te], slot="tap")

            def combine(dst, base_in, tw_in, nch_out, width, tag, seed=None):
                prev_cast = seed
                grp = FF // width

                def twload(j, ch, war):
                    e, hf = j // 2, j % 2
                    hw = grp * width // 2
                    if grp == 1:
                        src = bass.AP(tw_in.tensor,
                                      (e * nch_out * 128 + ch * 128) * width + hf * hw,
                                      [[width, 128], [1, hw]])
                    else:
                        src = bass.AP(tw_in.tensor,
                                      (e * nch_out * 128 + (ch + hf * (grp // 2)) * 128) * width,
                                      [[width, 128], [128 * width, grp // 2], [1, width]])
                    return S.dma("gpsimd", twb[j % 3][:, 0:hw], src,
                                 after=[war, t_twcast], slot="tw%d" % (j % 3), tag="tw")

                def wload(ch, hf, war):
                    hw = grp * width // 2
                    if grp == 1:
                        src = bass.AP(base_in.tensor, ch * 128 * width + hf * hw,
                                      [[width, 128], [1, hw]])
                    else:
                        src = bass.AP(base_in.tensor,
                                      (ch + hf * (grp // 2)) * 128 * width,
                                      [[width, 128], [128 * width, grp // 2], [1, width]])
                    return S.dma("sync", acc6[:, hf * hw:(hf + 1) * hw], src,
                                 after=[war], slot="wbase%d" % hf, tag="wb")

                pend = {}
                gw = None
                for ch in range(0, nch_out, grp):
                    gw = grp * width
                    hw = gw // 2
                    tok_w = [wload(ch, 0, prev_cast), wload(ch, 1, prev_cast)]
                    if ch == 0:
                        for j0 in range(3):
                            pend[j0] = twload(j0, ch, prev_cast)
                    lastH = [None, None]
                    lastSlot = {}
                    for j in range(2 * E):
                        e, hf = j // 2, j % 2
                        def stt(v, e=e, hf=hf, j=j, w=hw):
                            return v.scalar_tensor_tensor(
                                out=acc6[:, hf * w:(hf + 1) * w],
                                in0=twb[j % 3][:, 0:w],
                                scalar=g_bc[:, e:e + 1],
                                in1=acc6[:, hf * w:(hf + 1) * w],
                                op0=ALU.mult, op1=ALU.add)
                        deps = [pend[j % 3], lastH[hf] if e else tok_w[hf], t_gbce]
                        lastH[hf] = S.add("vector", stt, after=deps, tag="stt")
                        lastSlot[j % 3] = lastH[hf]
                        if j + 3 < 2 * E:
                            pend[j % 3] = twload(j + 3, ch, lastH[hf])
                    if ch + grp < nch_out:
                        for j0 in range(3):
                            pend[j0] = twload(j0, ch + grp, lastSlot[j0])
                    def cast(sc, ch=ch, dst=dst, w=gw):
                        return sc.copy(out=dst[:, ch * width: ch * width + w], in_=acc6[:, 0:w])
                    prev_cast = S.add("scalar", cast, after=lastH, tag="ccast")
                return prev_cast

            c1_done = combine(mW1T, w1T_in, tw1bf, NCH, FF, "c1")
            c2_done = combine(mW2T, w2T_in, tw2bf, 32, D, "c2", seed=c1_done)
            if "mw1t" in taps:
                S.dma("sync", tap_outs["mw1t"], mW1T.ap(), after=[c1_done], slot="tap")
            if "mw2t" in taps:
                S.dma("sync", tap_outs["mw2t"], mW2T.ap(), after=[c2_done], slot="tap")
            S.emit_block()
        if stop_after <= 7:
            mw2_cm.__exit__(None, None, None)
            mw1_cm.__exit__(None, None, None)
            return nc, tap_outs

        with ExitStack() as st7:
            EE = st7.enter_context
            u_sb = EE(nc.sbuf_tensor("u_sb", [128, 32 * TS], BF16, side="right"))
            xn2s7 = [EE(nc.sbuf_tensor("f_xn2s%d" % i, [128, NCH * TS], BF16, side="right"))
                     for i in range(2)]
            x2r = EE(nc.sbuf_tensor("f_x2r", [128, 2 * TS], F32, side="right"))
            fstage = EE(nc.sbuf_tensor("fstage", [128, NCH * TS], BF16, side="right"))
            tmpf7 = [EE(nc.sbuf_tensor("tmpf7%d" % i, [128, TS], F32, side="right")) for i in range(2)]
            sq7 = [EE(nc.sbuf_tensor("sq70", [128, TS], BF16, side="right"))] * 2
            ps_u = [EE(nc.psum_tensor("f_psu%d" % i, [128, TS], F32)) for i in range(2)]
            ps_o = [EE(nc.psum_tensor("f_pso%d" % i, [128, TS], F32)) for i in range(2)]
            ps_sum7 = EE(nc.psum_tensor("f_sum", [1, TS], F32))
            ps_sq7 = EE(nc.psum_tensor("f_sq", [1, TS], F32))
            ps_bc7 = EE(nc.psum_tensor("f_bc", [128, 2 * TS], F32))
            ln7 = make_ln(sq7, ps_sum7, ps_sq7, ps_bc7)

            prev_xl7 = [None, None]
            prev_ap7, prev_st7 = None, None
            prev_ue = [None, None]
            prev_oe = [None, None]
            ring_war = [None, None]
            for tt in range(NTS):
                tok_xn2 = S.dma("gpsimd", xn2s7[tt % 2].ap(),
                                bass.AP(xn2buf.tensor, tt * TS,
                                        [[NCH * L, 128], [L, NCH], [1, TS]]),
                                after=[prev_xl7[tt % 2]], slot="fxng%d" % (tt % 2), tag="fxn")
                u_toks = []
                last_umm = None
                for cf in range(32):
                    bi = cf % 2
                    def umm(t, cf=cf, tt=tt, bi=bi):
                        r = None
                        for ch in range(NCH):
                            r = t.matmul(ps_u[bi][:, 0:TS],
                                         mW1T[:, ch * FF + cf * 128: ch * FF + (cf + 1) * 128],
                                         xn2s7[tt % 2][:, ch * TS:(ch + 1) * TS],
                                         start=(ch == 0), stop=(ch == NCH - 1))
                        return r
                    t_u = S.add("tensor", umm, after=[tok_xn2, prev_ue[bi]], tag="umm")
                    last_umm = t_u
                    def uev(sc, cf=cf, bi=bi):
                        return sc.activation(out=u_sb[:, cf * TS:(cf + 1) * TS],
                                             in_=ps_u[bi][:, 0:TS], func=AF_GELUTANH,
                                             bias=mb1_pc[:, cf:cf + 1], scale=1.0)
                    t_ue = S.add("scalar", uev, after=[t_u], tag="uev")
                    prev_ue[bi] = t_ue
                    u_toks.append(t_ue)
                prev_xl7[tt % 2] = last_umm
                o_toks = []
                for ch in range(NCH):
                    bi = ch % 2
                    def omm(t, ch=ch, bi=bi):
                        r = None
                        for cf in range(32):
                            r = t.matmul(ps_o[bi][:, 0:TS],
                                         mW2T[:, cf * D + ch * 128: cf * D + (ch + 1) * 128],
                                         u_sb[:, cf * TS:(cf + 1) * TS],
                                         start=(cf == 0), stop=(cf == 31))
                        return r
                    t_o = S.add("tensor", omm, after=u_toks + [prev_oe[bi]], tag="omm")
                    def oev(sc, ch=ch, bi=bi):
                        return sc.activation(out=fstage[:, ch * TS:(ch + 1) * TS],
                                             in_=ps_o[bi][:, 0:TS], func=AF.Identity,
                                             bias=mb2_pc[:, ch:ch + 1], scale=1.0)
                    t_oe = S.add("scalar", oev, after=[t_o], tag="oev")
                    prev_oe[bi] = t_oe
                    o_toks.append(t_oe)
                slices = [fstage[:, c * TS:(c + 1) * TS] for c in range(NCH)]
                resid = [x2r[:, (c % 2) * TS:(c % 2 + 1) * TS] for c in range(NCH)]
                r_loads = {}

                def load_x2(c, tt=tt, r_loads=r_loads):
                    if c not in r_loads and c < NCH:
                        r_loads[c] = S.dma(
                            "sync", x2r[:, (c % 2) * TS:(c % 2 + 1) * TS],
                            bass.AP(x2buf.tensor, c * L + tt * TS, [[NCH * L, 128], [1, TS]]),
                            after=[ring_war[c % 2]], slot="fx2", tag="fx2")
                    return r_loads.get(c)

                def pair_extra7(c0):
                    return [load_x2(c0), load_x2(c0 + 1)]

                def on_pair7(c0, tok_ap, tt=tt, r_loads=r_loads):
                    tok_y = S.dma("sync",
                                  bass.AP(y_dev.tensor, c0 * L + tt * TS,
                                          [[NCH * L, 128], [L, 2], [1, TS]]),
                                  x2r.ap(), after=[tok_ap], slot="yout", tag="yout")
                    ring_war[0] = ring_war[1] = tok_y
                    load_x2(c0 + 2)
                    load_x2(c0 + 3)

                prev_ap7, prev_st7 = ln7(slices, resid_slices=resid, out_slices=resid,
                                         after=o_toks,
                                         prev_apply=prev_ap7, prev_stats=prev_st7,
                                         tmpf=tmpf7, pair_extra=pair_extra7,
                                         on_pair=on_pair7)
            S.emit_block()
        mw2_cm.__exit__(None, None, None)
        mw1_cm.__exit__(None, None, None)
    return nc, tap_outs


def prep_core_inputs(inputs):
    import ml_dtypes
    bf = ml_dtypes.bfloat16
    A = lambda v: np.ascontiguousarray(np.asarray(v), dtype=np.float32)

    def pc(v):
        return A(np.asarray(v).reshape(NCH, 128).T)

    i = inputs
    sh = {}
    sh["wq_nat"] = A(i["L0_attn_Wq_w"])
    sh["wk_nat"] = A(i["L0_attn_Wk_w"])
    sh["cqT"] = A(np.asarray(i["L0_attn_convq_w"]).transpose(2, 0, 1).reshape(3 * D, D))
    sh["ckT"] = A(np.asarray(i["L0_attn_convk_w"]).transpose(2, 0, 1).reshape(3 * D, D))
    qp = np.asarray(i["L0_attn_qproj_w"])
    kp = np.asarray(i["L0_attn_kproj_w"])
    sh["qpA_T"] = A(qp[:, :D].T)
    sh["qpB_T"] = A(qp[:, D:].T)
    sh["kpA_T"] = A(kp[:, :D].T)
    sh["kpB_T"] = A(kp[:, D:].T)
    sh["wvT"] = A(np.asarray(i["L0_attn_Wv_w"]).T)
    sh["outT"] = A(np.asarray(i["L0_attn_out_w"]).T)
    sh["gw1T"] = A(np.asarray(i["L0_ffn_gate_w1"]).T)
    sh["gw2T"] = A(np.asarray(i["L0_ffn_gate_w2"]).T)
    sh["w1T"] = A(np.asarray(i["L0_ffn_W1"]).T)
    sh["w2T"] = A(np.asarray(i["L0_ffn_W2"]).T)
    sh["tw1T"] = A(np.asarray(i["L0_ffn_TW1"]).transpose(0, 2, 1).reshape(E * D, FF))
    sh["tw2T"] = A(np.asarray(i["L0_ffn_TW2"]).transpose(0, 2, 1).reshape(E * FF, D))
    sh["tb1"] = A(i["L0_ffn_Tb1"])
    sh["tb2"] = A(i["L0_ffn_Tb2"])
    sh["b1row"] = A(np.asarray(i["L0_ffn_b1"])[None, :])
    sh["b2row"] = A(np.asarray(i["L0_ffn_b2"])[None, :])
    sh["b1pc"] = A(np.asarray(i["L0_ffn_b1"]).reshape(32, 128).T)
    sh["b2pc"] = A(np.asarray(i["L0_ffn_b2"]).reshape(8, 128).T)
    sh["wqb_pc"] = pc(i["L0_attn_Wq_b"])
    sh["wkb_pc"] = pc(i["L0_attn_Wk_b"])
    sh["qpb_pc"] = pc(i["L0_attn_qproj_b"])
    sh["kpb_pc"] = pc(i["L0_attn_kproj_b"])
    sh["cqb_pc"] = pc(i["L0_attn_convq_b"])
    sh["ckb_pc"] = pc(i["L0_attn_convk_b"])
    sh["outb_pc"] = pc(i["L0_attn_out_b"])
    sh["gb1_pc"] = pc(i["L0_ffn_gate_b1"])
    sh["gb2_col"] = A(np.asarray(i["L0_ffn_gate_b2"])[:, None])
    sh["wvb_row"] = A(np.asarray(i["L0_attn_Wv_b"])[None, :])
    t1w = np.asarray(i["L0_attn_tau1_w"])[:, 0, :]
    d1w = np.asarray(i["L0_attn_del1_w"])[:, 0, :]
    sh["tau1w_pc"] = A(t1w.reshape(NCH, 128, 2, 3).transpose(1, 0, 2, 3).reshape(128, NCH * 6))
    sh["del1w_pc"] = A(d1w.reshape(NCH, 128, 2, 3).transpose(1, 0, 2, 3).reshape(128, NCH * 6))
    sh["tau1b_pc"] = A(np.asarray(i["L0_attn_tau1_b"]).reshape(NCH, 128, 2).transpose(1, 0, 2).reshape(128, NCH * 2))
    sh["del1b_pc"] = A(np.asarray(i["L0_attn_del1_b"]).reshape(NCH, 128, 2).transpose(1, 0, 2).reshape(128, NCH * 2))
    t2w = np.asarray(i["L0_attn_tau2_w"])[:, :, 0]
    d2w = np.asarray(i["L0_attn_del2_w"])[:, :, 0]
    sh["tau2wT"] = A(t2w.T.reshape(NCH, 128, 2, H).transpose(1, 0, 2, 3).reshape(128, 16 * H))
    sh["del2wT"] = A(d2w.T.reshape(NCH, 128, 2, H).transpose(1, 0, 2, 3).reshape(128, 16 * H))
    sh["tau2b_row"] = A(np.asarray(i["L0_attn_tau2_b"])[None, :])
    sh["del2b_row"] = A(np.asarray(i["L0_attn_del2_b"])[None, :])
    sh["ones_row"] = np.ones((1, 128), dtype=bf)
    sh["ones_rowf"] = np.ones((1, 128), dtype=np.float32)
    sh["ones_col"] = np.ones((128, 1), dtype=bf)
    o2 = np.zeros((2, 256), dtype=bf)
    o2[0, :128] = 1
    o2[1, 128:] = 1
    sh["oneh2"] = o2
    o16 = np.zeros((H, H * DKH), dtype=bf)
    for h in range(H):
        o16[h, h * DKH:(h + 1) * DKH] = 1
    sh["oneh16"] = o16
    sh["ident_f32"] = np.eye(128, dtype=np.float32)

    x = np.asarray(i["L0_attn_x"], dtype=np.float32)
    cores = []
    for b in range(x.shape[0]):
        d = dict(sh)
        d["x_fm"] = A(x[b].reshape(L, NCH, 128).transpose(2, 1, 0).reshape(128, NCH * L))
        cores.append(d)
    return cores


def assemble_y(y_dev):
    return np.ascontiguousarray(
        y_dev.reshape(128, NCH, L).transpose(2, 1, 0).reshape(L, D))


def kernel(**inputs):
    from concourse.bass_utils import run_bass_kernel_spmd
    cores = prep_core_inputs(inputs)
    nc, _ = build(stop_after=99, taps=())
    res = run_bass_kernel_spmd(nc, cores, core_ids=list(range(len(cores))))
    outs = []
    for r in res.results:
        outs.append(assemble_y(np.asarray(r["y_dev"], dtype=np.float32)))
    return np.stack(outs, axis=0)


def kernel(**inputs):
    from concourse.bass_utils import run_bass_kernel_spmd
    cores = prep_core_inputs(inputs)
    nc, _ = build(stop_after=99, taps=())
    res = run_bass_kernel_spmd(nc, cores, core_ids=list(range(len(cores))))
    outs = []
    for r in res.results:
        outs.append(assemble_y(np.asarray(r["y_dev"], dtype=np.float32)))
    return np.stack(outs, axis=0)


# version 43
# speedup vs baseline: 1.1875x; 1.0032x over previous
import sys
sys.path.insert(0, "/opt/trn_rl_repo")
from contextlib import ExitStack
import numpy as np
import concourse.bass as bass
import concourse.mybir as mybir

F32 = mybir.dt.float32
BF16 = mybir.dt.bfloat16
AF = mybir.ActivationFunctionType
ALU = mybir.AluOpType
AX = mybir.AxisListType

L = 4096
D = 1024
H = 16
DKH = 64
E = 8
FF = 4096
NCH = 8
CW = L + 2
TS = 512
NTS = L // TS
NT = L // 128
QS = 512
NQS = L // QS
EPS = 1e-5

ENGINES = ("tensor", "vector", "scalar", "gpsimd", "sync")


class Sched:

    def __init__(self, nc, stack):
        self.nc = nc
        self.stack = stack
        self.semh = {}
        self.count = {}
        for e in ENGINES:
            k = e + "_c"
            self.semh[k] = stack.enter_context(nc.semaphore("S_" + k))
            self.count[k] = 0
        self.ops = {e: [] for e in ENGINES}
        self.hw = {e: {} for e in ENGINES}
        self.touched = {e: {} for e in ENGINES}

    def _slotkey(self, name):
        k = "sl_" + name
        if k not in self.semh:
            self.semh[k] = self.stack.enter_context(self.nc.semaphore(k))
            self.count[k] = 0
        return k

    def add(self, eng, emit, after=(), dma_slot=None, tag=""):
        waits = {}
        for tok in after:
            if tok is None:
                continue
            (pk, pc) = tok
            if pk == eng + "_c" and eng == "tensor":
                continue
            if pk.startswith("sl_"):
                pc = self.count[pk]
            waits[pk] = max(waits.get(pk, 0), pc)
        if dma_slot is None:
            key = eng + "_c"
            amt = 1
        else:
            key = self._slotkey(dma_slot)
            amt = 16
        self.count[key] += amt
        tok = (key, self.count[key])
        if dma_slot is not None:
            self.touched[eng][key] = self.count[key]
        self.ops[eng].append((emit, waits, key, amt, tag))
        return tok

    def dma(self, eng, out, in_, after=(), slot="misc", tag=""):
        return self.add(eng, (lambda e, o=out, i=in_: e.dma_start(out=o, in_=i)),
                        after=after, dma_slot=slot, tag=tag)

    def nopop(self, eng, after=(), tag="fence"):
        return self.add(eng, (lambda e: e.nop()), after=after, tag=tag)

    def emit_block(self):
        nc = self.nc
        for e in ("sync", "gpsimd", "scalar", "vector"):
            if self.touched[e]:
                toks = [(k, c) for (k, c) in self.touched[e].items()]
                self.nopop(e, after=toks, tag="dma_fence")
                self.touched[e] = {}
        with nc.Block() as blk:
            for eng in ENGINES:
                ops = self.ops[eng]
                if not ops:
                    continue

                def mk(eng=eng, ops=tuple(ops)):
                    def body(eobj):
                        for (emit, waits, key, amt, tag) in ops:
                            for (pk, pc) in sorted(waits.items()):
                                if self.hw[eng].get(pk, 0) >= pc:
                                    continue
                                eobj.wait_ge(self.semh[pk], pc)
                                self.hw[eng][pk] = pc
                            last = emit(eobj)
                            assert last is not None, f"op {tag} on {eng} returned None"
                            last.then_inc(self.semh[key], amt)
                    return body

                getattr(blk, eng)(mk())
        for e in ENGINES:
            self.ops[e] = []
        for e in ENGINES:
            for k in self.semh:
                self.hw[e][k] = max(self.hw[e].get(k, 0), self.count[k])


def build(stop_after=99, taps=(), sim=False):
    global AF_GELU, AF_SIG, AF_GELUTANH, AF_ERF
    AF_GELU = AF.Exp if sim else AF.Gelu
    AF_SIG = AF.Exp if sim else AF.Sigmoid
    AF_GELUTANH = AF.Exp if sim else AF.Gelu_apprx_tanh
    nc = bass.Bass("TRN2", target_bir_lowering=False, detect_race_conditions=not sim)
    taps = set(taps)

    def din(name, shape, dtype=F32):
        return nc.dram_tensor(name, list(shape), dtype, kind="ExternalInput").ap()

    x_in = din("x_fm", (128, NCH * L))
    wq_nat = din("wq_nat", (D, D))
    wk_nat = din("wk_nat", (D, D))
    cqT = din("cqT", (3 * D, D))
    ckT = din("ckT", (3 * D, D))
    qpA_T = din("qpA_T", (D, D))
    qpB_T = din("qpB_T", (D, D))
    kpA_T = din("kpA_T", (D, D))
    kpB_T = din("kpB_T", (D, D))
    wvT_in = din("wvT", (D, D))
    outT_in = din("outT", (D, D))
    gw1T_in = din("gw1T", (D, D))
    gw2T_in = din("gw2T", (D, E))
    w1T_in = din("w1T", (D, FF))
    w2T_in = din("w2T", (FF, D))
    tw1T_in = din("tw1T", (E * D, FF))
    tw2T_in = din("tw2T", (E * FF, D))
    tb1_in = din("tb1", (E, FF))
    tb2_in = din("tb2", (E, D))
    b1row_in = din("b1row", (1, FF))
    b2row_in = din("b2row", (1, D))
    b1pc_in = din("b1pc", (128, 32))
    b2pc_in = din("b2pc", (128, 8))
    wqb_pc = din("wqb_pc", (128, NCH))
    wkb_pc = din("wkb_pc", (128, NCH))
    qpb_pc = din("qpb_pc", (128, NCH))
    kpb_pc = din("kpb_pc", (128, NCH))
    cqb_pc = din("cqb_pc", (128, NCH))
    ckb_pc = din("ckb_pc", (128, NCH))
    outb_pc_in = din("outb_pc", (128, NCH))
    gb1_pc_in = din("gb1_pc", (128, NCH))
    gb2_col_in = din("gb2_col", (E, 1))
    wvb_row_in = din("wvb_row", (1, D))
    tau1w_in = din("tau1w_pc", (128, NCH * 6))
    del1w_in = din("del1w_pc", (128, NCH * 6))
    tau1b_in = din("tau1b_pc", (128, NCH * 2))
    del1b_in = din("del1b_pc", (128, NCH * 2))
    tau2wT_in = din("tau2wT", (128, 16 * H))
    del2wT_in = din("del2wT", (128, 16 * H))
    tau2b_in = din("tau2b_row", (1, H))
    del2b_in = din("del2b_row", (1, H))
    ones_row_in = din("ones_row", (1, 128), BF16)
    ones_rowf_in = din("ones_rowf", (1, 128))
    ones_col_in = din("ones_col", (128, 1), BF16)
    oneh2_in = din("oneh2", (2, 2 * 128), BF16)
    oneh16_in = din("oneh16", (H, H * DKH), BF16)
    ident_in = din("ident_f32", (128, 128))

    y_dev = nc.dram_tensor("y_dev", [128, NCH * L], F32, kind="ExternalOutput").ap()
    qbuf = nc.dram_tensor("qbuf", [128, NCH * L], BF16, kind="Internal").ap()
    kbuf = nc.dram_tensor("kbuf", [128, NCH * L], BF16, kind="Internal").ap()
    x2buf = nc.dram_tensor("x2buf", [128, NCH * L], F32, kind="Internal").ap()
    xn2buf = nc.dram_tensor("xn2buf", [128, NCH * L], BF16, kind="Internal").ap()
    tw1bf = nc.dram_tensor("tw1bf", [E * D, FF], BF16, kind="Internal").ap()
    tw2bf = nc.dram_tensor("tw2bf", [E * FF, D], BF16, kind="Internal").ap()

    tap_outs = {}
    tap_shapes = {
        "xn": ([128, NCH * CW], BF16),
        "tauT": ([128, NT * H], F32),
        "edelT": ([128, NT * H], F32),
        "taufm": ([H, L], BF16),
        "q": ([128, NCH * L], BF16),
        "k": ([128, NCH * L], BF16),
        "vaug": ([128, NT * H * 65], BF16),
        "mbias": ([128, H], F32),
        "attn": ([128, NCH * QS], BF16),
        "x2": ([128, NCH * L], F32),
        "xn2": ([128, NCH * L], BF16),
        "g": ([E, 2], F32),
        "mw1t": ([128, NCH * FF], BF16),
        "mw2t": ([128, 32 * D], BF16),
        "mb1pc": ([128, 32], F32),
        "mb2pc": ([128, 8], F32),
        "biask": ([128, NCH], F32),
        "vkcorr": ([128, 4 * NCH], F32),
        "vraw": ([128, TS], F32),
        "xn2nd": ([128, NCH * CW], BF16),
        "gbc": ([128, E], F32),
    }
    for t in taps:
        sh, dt = tap_shapes[t]
        tap_outs[t] = nc.dram_tensor("tap_" + t, sh, dt, kind="ExternalOutput").ap()

    with ExitStack() as top:
        TE = top.enter_context
        S = Sched(nc, top)

        def apx(sb_tensor, part0, nparts, off, dims):
            t = sb_tensor.ap().tensor
            w = sb_tensor.ap().ap[0][0]
            return bass.AP(t, part0 * w + off, [[w, nparts]] + [[s, c] for (s, c) in dims])


        tauT = None
        delT = None
        edelT = None
        tau_fm = None
        mbias = None
        nrm = None
        nrow = None
        negm_bf = None
        ones_row_sb = TE(nc.sbuf_tensor("ones_row_sb", [1, 128], BF16))
        ones_rowf_sb = TE(nc.sbuf_tensor("ones_rowf_sb", [1, 128], F32))
        ones_col_sb = TE(nc.sbuf_tensor("ones_col_sb", [128, 1], BF16))
        oneh2_sb = TE(nc.sbuf_tensor("oneh2_sb", [2, 2 * 128], BF16))
        oneh16_sb = None
        ident_sb = TE(nc.sbuf_tensor("ident_sb", [128, 128], F32))
        negm0 = TE(nc.sbuf_tensor("negm0", [1, TS], F32))
        rstd0 = TE(nc.sbuf_tensor("rstd0", [1, TS], F32))
        scr0 = TE(nc.sbuf_tensor("scr0", [1, TS], F32))
        scr1 = TE(nc.sbuf_tensor("scr1", [1, TS], F32))
        eps_sb = TE(nc.sbuf_tensor("eps_sb", [1, 1], F32))
        zb_sb = TE(nc.sbuf_tensor("zb_sb", [128, 1], F32))
        biasq = TE(nc.sbuf_tensor("biasq", [128, NCH], F32))
        biask = TE(nc.sbuf_tensor("biask", [128, NCH], F32))
        vq_corr = TE(nc.sbuf_tensor("vq_corr", [128, 4 * NCH], F32))
        vk_corr = TE(nc.sbuf_tensor("vk_corr", [128, 4 * NCH], F32))
        g_sb = TE(nc.sbuf_tensor("g_sb", [E, 2], F32))
        g_bf = TE(nc.sbuf_tensor("g_bf", [E, 1], BF16))
        g_bc = TE(nc.sbuf_tensor("g_bc", [128, E], F32))
        grow_f = TE(nc.sbuf_tensor("grow_f", [1, E], F32))
        grow_bf = TE(nc.sbuf_tensor("grow_bf", [1, E], BF16))
        mb1_pc = TE(nc.sbuf_tensor("mb1_pc", [128, 32], F32))
        mb2_pc = TE(nc.sbuf_tensor("mb2_pc", [128, 8], F32))

        t_or = S.dma("sync", ones_row_sb[:], ones_row_in, slot="consts")
        t_orf = S.dma("sync", ones_rowf_sb[:], ones_rowf_in, slot="consts")
        t_oc = S.dma("sync", ones_col_sb[:], ones_col_in, slot="consts")
        t_o2 = S.dma("sync", oneh2_sb[:], oneh2_in, slot="consts")
        t_id = S.dma("sync", ident_sb[:], ident_in, slot="consts")
        S.add("gpsimd", lambda g: g.memset(eps_sb[:], EPS))
        tok_zb = S.add("gpsimd", lambda g: g.memset(zb_sb[:], 0.0))
        S.add("gpsimd", lambda g: g.memset(nrm[:], 0.0))
        CONSTS = [t_or, t_orf, t_oc, t_o2, t_id]

        def make_ln(sq_st, ps_sum, ps_sq, ps_bc):
            def ln_stats_apply(slices, resid_slices=None, out_slices=None,
                               after=(), prev_apply=None, prev_stats=None,
                               tmpf=None, pair_extra=None, on_pair=None):
                after = list(after)
                def mm_sum(t):
                    r = None
                    for c in range(NCH):
                        r = t.matmul(ps_sum[:, 0:TS], ones_col_sb[:], slices[c],
                                     start=(c == 0), stop=(c == NCH - 1))
                    return r
                tok_sum = S.add("tensor", mm_sum, after=after + CONSTS + [prev_stats], tag="ln_sum")
                sq_toks = []
                mm_sq_toks = []
                single_sq = sq_st[0] is sq_st[1]
                for c in range(NCH):
                    def sq(v, c=c):
                        return v.tensor_tensor(out=sq_st[c % 2][:], in0=slices[c],
                                               in1=slices[c], op=ALU.mult)
                    wi = c - 1 if single_sq else c - 2
                    war = mm_sq_toks[wi] if wi >= 0 else None
                    tok_sq = S.add("vector", sq, after=after + [war], tag="ln_sq")
                    def mmsq(t, c=c):
                        return t.matmul(ps_sq[:, 0:TS], ones_col_sb[:], sq_st[c % 2][:],
                                        start=(c == 0), stop=(c == NCH - 1))
                    tok_mm = S.add("tensor", mmsq, after=[tok_sq, prev_stats], tag="ln_sqmm")
                    mm_sq_toks.append(tok_mm)
                    sq_toks.append(tok_sq)
                def rmath_a(v):
                    v.tensor_scalar(out=negm0[0:1, 0:TS], in0=ps_sum[:, 0:TS],
                                    scalar1=-1.0 / D, scalar2=None, op0=ALU.mult)
                    return v.tensor_scalar(out=scr0[0:1, 0:TS], in0=ps_sq[:, 0:TS],
                                           scalar1=1.0 / D, scalar2=None, op0=ALU.mult)
                tok_ra = S.add("vector", rmath_a, after=[tok_sum, mm_sq_toks[-1]], tag="ln_ra")
                tok_rb = S.add("vector", lambda v: v.tensor_tensor(
                    out=scr1[0:1, 0:TS], in0=negm0[0:1, 0:TS],
                    in1=negm0[0:1, 0:TS], op=ALU.mult), after=[tok_ra], tag="ln_rb")
                tok_rm = S.add("vector", lambda v: v.tensor_tensor(
                    out=scr0[0:1, 0:TS], in0=scr0[0:1, 0:TS], in1=scr1[0:1, 0:TS],
                    op=ALU.subtract), after=[tok_rb], tag="ln_rc")
                tok_ln1 = S.add("scalar", lambda s: s.activation(
                    out=scr1[0:1, 0:TS], in_=scr0[0:1, 0:TS], func=AF.Ln,
                    bias=eps_sb[0:1, 0:1], scale=1.0), after=[tok_rm], tag="ln_ln")
                tok_rs = S.add("scalar", lambda s: s.activation(
                    out=rstd0[0:1, 0:TS], in_=scr1[0:1, 0:TS], func=AF.Exp, scale=-0.5),
                    after=[tok_ln1], tag="ln_exp")
                def bcast(t):
                    t.matmul(ps_bc[:, 0:TS], ones_rowf_sb[0:1, 0:128], negm0[0:1, 0:TS],
                             start=True, stop=True)
                    return t.matmul(ps_bc[:, TS:2 * TS], ones_rowf_sb[0:1, 0:128],
                                    rstd0[0:1, 0:TS], start=True, stop=True)
                tok_bc = S.add("tensor", bcast, after=[tok_rs, tok_rm, prev_apply], tag="ln_bc")
                tok_ap = tok_bc
                for c0 in range(0, NCH, 2):
                    cpair = [c0, c0 + 1]
                    extra = list(pair_extra(c0)) if pair_extra is not None else []
                    if out_slices is None:
                        tok_a = S.add("vector", (lambda v, cp=cpair: [
                            v.tensor_tensor(out=slices[c], in0=slices[c],
                                            in1=ps_bc[:, 0:TS], op=ALU.add)
                            for c in cp][-1]), after=[tok_ap], tag="ap_add")
                        tok_ap = S.add("vector", (lambda v, cp=cpair: [
                            v.tensor_tensor(out=slices[c], in0=slices[c],
                                            in1=ps_bc[:, TS:2 * TS], op=ALU.mult)
                            for c in cp][-1]), after=[tok_a], tag="ap_mul")
                    else:
                        tok_a = S.add("vector", (lambda v, cp=cpair: [
                            v.tensor_tensor(out=tmpf[c % 2][:], in0=slices[c],
                                            in1=ps_bc[:, 0:TS], op=ALU.add)
                            for c in cp][-1]), after=[tok_ap], tag="ap_add")
                        tok_b = S.add("vector", (lambda v, cp=cpair: [
                            v.tensor_tensor(out=tmpf[c % 2][:], in0=tmpf[c % 2][:],
                                            in1=ps_bc[:, TS:2 * TS], op=ALU.mult)
                            for c in cp][-1]), after=[tok_a], tag="ap_mul")
                        if resid_slices is not None:
                            tok_ap = S.add("vector", (lambda v, cp=cpair: [
                                v.tensor_tensor(out=out_slices[c], in0=tmpf[c % 2][:],
                                                in1=resid_slices[c], op=ALU.add)
                                for c in cp][-1]), after=[tok_b] + extra, tag="ap_out")
                        else:
                            tok_ap = S.add("vector", (lambda v, cp=cpair: [
                                v.tensor_copy(out_slices[c], tmpf[c % 2][:])
                                for c in cp][-1]), after=[tok_b] + extra, tag="ap_out")
                    if on_pair is not None:
                        on_pair(c0, tok_ap)
                return tok_ap, tok_rm
            return ln_stats_apply

        FB = {}
        fold_war = {"cT_rd": [], "mjt_rd": [], "pA_rd": [], "wn_rd": [],
                    "pB_rd": [], "sm_rd": [], "psv_rd": None,
                    "last_evac": {0: None, 1: None}, "n": 0}

        def fold_path(pref, wnat_ap, cT_ap, pA_ap, pB_ap, wb_ap, cb_ap, pb_ap,
                      weff_sb, bias_sb, vcorr_sb):
            wn_sb, pB_sb, cT_h, mjt_h, pA_h = (FB[k] for k in
                                               ("wn", "pB", "cT", "mjt", "pA"))
            wb_bf, cb_bf, pb_f, ps, ps_v = (FB[k] for k in
                                            ("wb", "cb", "pb", "ps", "ps_v"))
            st = {}

            def setup():
                ldB, ldW = [], []
                for cc in range(NCH):
                    ldB.append(S.dma("gpsimd", pB_sb[:, cc * D:(cc + 1) * D],
                                     pB_ap[cc * 128:(cc + 1) * 128, :],
                                     after=fold_war["pB_rd"], slot=pref + "pB"))
                    ldW.append(S.dma("gpsimd", wn_sb[:, cc * D:(cc + 1) * D],
                                     wnat_ap[cc * 128:(cc + 1) * 128, :],
                                     after=fold_war["wn_rd"], slot=pref + "wn"))
                ld_wb = S.dma("gpsimd", wb_bf[:], wb_ap, after=fold_war["sm_rd"], slot=pref + "sm")
                ld_cb = S.dma("gpsimd", cb_bf[:], cb_ap, slot=pref + "sm")
                ld_pb = S.dma("gpsimd", pb_f[:], pb_ap, slot=pref + "sm")
                fold_war["pB_rd"] = []
                fold_war["wn_rd"] = []
                fold_war["sm_rd"] = []
                def bmm(t):
                    r = None
                    for oc in range(NCH):
                        for cc in range(NCH):
                            r = t.matmul(ps_v[:, 3 * NCH + oc: 3 * NCH + oc + 1],
                                         pB_sb[:, cc * D + oc * 128: cc * D + (oc + 1) * 128],
                                         cb_bf[:, cc:cc + 1],
                                         start=(cc == 0), stop=(cc == NCH - 1))
                    return r
                tok_bcb = S.add("tensor", bmm,
                                after=[ld_cb, fold_war["psv_rd"]] + ldB, tag="bcb")
                fold_war["sm_rd"].append(tok_bcb)
                st.update(ldB=ldB, ldW=ldW, ld_wb=ld_wb, ld_pb=ld_pb,
                          vm_toks=[tok_bcb], wn_reads=[], pB_reads=[tok_bcb])

            def one_iter(j, oh):
                if "ldB" not in st:
                    setup()
                ldA = []
                if j == 1:
                    for mc in range(NCH):
                        ldA.append(S.dma(
                            "gpsimd", pA_h[:, mc * TS:(mc + 1) * TS],
                            pA_ap[mc * 128:(mc + 1) * 128, oh * TS: oh * TS + TS],
                            after=fold_war["pA_rd"] if mc == 0 else (), slot=pref + "pA"))
                    fold_war["pA_rd"] = []
                mjt_toks = []
                mjt_mm_toks = []
                for mh in range(2):
                    ldc = []
                    for cc in range(NCH):
                        ldc.append(S.dma(
                            "gpsimd", cT_h[:, cc * TS:(cc + 1) * TS],
                            cT_ap[j * D + cc * 128: j * D + (cc + 1) * 128,
                                  mh * TS: mh * TS + TS],
                            after=fold_war["cT_rd"] if cc == 0 else (), slot=pref + "cT"))
                    fold_war["cT_rd"] = []
                    for mi in range(4):
                        mc = mh * 4 + mi
                        pi = fold_war["n"] % 2
                        def mm(t, mi=mi, oh=oh, pi=pi):
                            r = None
                            for cc in range(NCH):
                                r = t.matmul(ps[pi][:, 0:TS],
                                             cT_h[:, cc * TS + mi * 128: cc * TS + (mi + 1) * 128],
                                             pB_sb[:, cc * D + oh * TS: cc * D + oh * TS + TS],
                                             start=(cc == 0), stop=(cc == NCH - 1))
                            return r
                        tok_mm = S.add("tensor", mm,
                                       after=ldc + st["ldB"] + [fold_war["last_evac"][pi]] +
                                             fold_war["mjt_rd"], tag="mjt_mm")
                        fold_war["mjt_rd"] = []
                        st["pB_reads"].append(tok_mm)
                        fold_war["cT_rd"].append(tok_mm)
                        def ev(sc, mc=mc, pi=pi):
                            return sc.activation(out=mjt_h[:, mc * TS:(mc + 1) * TS],
                                                 in_=ps[pi][:, 0:TS], func=AF.Identity,
                                                 bias=zb_sb[:, 0:1], scale=1.0)
                        tok_ev = S.add("scalar", ev, after=[tok_mm, tok_zb], tag="mjt_ev")
                        fold_war["last_evac"][pi] = tok_ev
                        mjt_toks.append(tok_ev)
                        mjt_mm_toks.append(tok_mm)
                        fold_war["n"] += 1
                    if mh == 0:
                        S.nopop("gpsimd", after=mjt_mm_toks[-4:], tag="cT_gate")
                weff_toks = []
                for ic in range(NCH):
                    pi = fold_war["n"] % 2
                    def mm(t, ic=ic, pi=pi, j=j):
                        r = None
                        first = True
                        for mc in range(NCH):
                            r = t.matmul(ps[pi][:, 0:TS],
                                         wn_sb[:, mc * D + ic * 128: mc * D + (ic + 1) * 128],
                                         mjt_h[:, mc * TS:(mc + 1) * TS],
                                         start=first, stop=(j != 1 and mc == NCH - 1))
                            first = False
                        if j == 1:
                            for mc in range(NCH):
                                r = t.matmul(ps[pi][:, 0:TS],
                                             wn_sb[:, mc * D + ic * 128: mc * D + (ic + 1) * 128],
                                             pA_h[:, mc * TS:(mc + 1) * TS],
                                             start=False, stop=(mc == NCH - 1))
                        return r
                    tok_mm = S.add("tensor", mm,
                                   after=mjt_toks + st["ldW"] + ldA +
                                         [fold_war["last_evac"][pi]], tag="weff_mm")
                    st["wn_reads"].append(tok_mm)
                    weff_toks.append(tok_mm)
                    def ev(sc, j=j, ic=ic, oh=oh, pi=pi):
                        return sc.activation(
                            out=weff_sb[:, (j * NCH + ic) * D + oh * TS:(j * NCH + ic) * D + oh * TS + TS],
                            in_=ps[pi][:, 0:TS], func=AF.Identity,
                            bias=zb_sb[:, 0:1], scale=1.0)
                    tok_ev = S.add("scalar", ev, after=[tok_mm, tok_zb], tag="weff_ev")
                    fold_war["last_evac"][pi] = tok_ev
                    fold_war["n"] += 1
                def vmm(t, j=j, oh=oh):
                    r = None
                    for oi in range(4):
                        oc = oh * 4 + oi
                        for mc in range(NCH):
                            r = t.matmul(ps_v[:, j * NCH + oc: j * NCH + oc + 1],
                                         mjt_h[:, mc * TS + oi * 128: mc * TS + (oi + 1) * 128],
                                         wb_bf[:, mc:mc + 1],
                                         start=(mc == 0), stop=(j != 1 and mc == NCH - 1))
                        if j == 1:
                            for mc in range(NCH):
                                r = t.matmul(ps_v[:, j * NCH + oc: j * NCH + oc + 1],
                                             pA_h[:, mc * TS + oi * 128: mc * TS + (oi + 1) * 128],
                                             wb_bf[:, mc:mc + 1],
                                             start=False, stop=(mc == NCH - 1))
                    return r
                tok_v = S.add("tensor", vmm, after=mjt_toks + ldA + [st["ld_wb"]], tag="vmm")
                st["vm_toks"].append(tok_v)
                fold_war["mjt_rd"] = weff_toks + [tok_v]
                fold_war["pA_rd"] = weff_toks + [tok_v] if j == 1 else fold_war["pA_rd"]
                fold_war["sm_rd"].append(tok_v)

            def bias_emit():
                t0 = S.add("vector", lambda v: v.tensor_copy(vcorr_sb[:], ps_v[:, 0:4 * NCH]),
                           after=st["vm_toks"], tag="vc_copy")
                t1 = S.add("vector", lambda v: v.tensor_tensor(
                    out=bias_sb[:], in0=vcorr_sb[:, 0:NCH], in1=vcorr_sb[:, NCH:2 * NCH],
                    op=ALU.add), after=[t0], tag="b1")
                t2 = S.add("vector", lambda v: v.tensor_tensor(
                    out=bias_sb[:], in0=bias_sb[:], in1=vcorr_sb[:, 2 * NCH:3 * NCH],
                    op=ALU.add), after=[t1], tag="b2")
                t3 = S.add("vector", lambda v: v.tensor_tensor(
                    out=bias_sb[:], in0=bias_sb[:], in1=vcorr_sb[:, 3 * NCH:4 * NCH],
                    op=ALU.add), after=[t2], tag="b3")
                tok_bias = S.add("vector", lambda v: v.tensor_tensor(
                    out=bias_sb[:], in0=bias_sb[:], in1=pb_f[:], op=ALU.add),
                    after=[t3, st["ld_pb"]], tag="bias")
                fold_war["psv_rd"] = t0
                fold_war["wn_rd"] = st["wn_reads"]
                fold_war["pB_rd"] = st["pB_reads"]
                fold_war["sm_rd"].append(tok_bias)
                st["bias_tok"] = tok_bias
                return tok_bias

            iters = [(lambda j=j, oh=oh: one_iter(j, oh))
                     for j in range(3) for oh in range(2)]
            return iters, bias_emit, st

        def qk_production(weff_sb, bias_sb, vcorr_sb, buf, scaled, nrm_off, tap_name, tok_bias=None):
            pref = tap_name
            with ExitStack() as stP:
                EE = stP.enter_context
                tbc_sb = EE(nc.sbuf_tensor(pref + "_tbc", [128, L], BF16, side="right")) if scaled else None
                qstage = [EE(nc.sbuf_tensor(pref + "_st%d" % i, [128, TS], BF16, side="right")) for i in range(2)]
                sqq = [EE(nc.sbuf_tensor(pref + "_sqq%d" % i, [128, TS], BF16, side="right")) for i in range(2)]
                ps_q = [EE(nc.psum_tensor(pref + "_ps%d" % i, [128, TS], F32)) for i in range(2)]
                ps_tb = EE(nc.psum_tensor(pref + "_pstb", [128, TS], F32)) if scaled else None
                ps_n = [EE(nc.psum_tensor(pref + "_psn%d" % i, [1, TS], F32)) for i in range(2)]

                toks_spill = []
                prev_evac = [None, None]
                prev_spill = [None, None]
                prev_sq = [None, None]
                prev_tbc = None
                prev_nred = None
                pend_nmm = [None]
                for oc in range(NCH):
                    h0, h1 = 2 * oc, 2 * oc + 1
                    tbc_toks = []
                    if scaled:
                        for tsl in range(NTS):
                            def tbm(t, oc=oc, tsl=tsl):
                                t.matmul(ps_tb[0:64, 0:TS],
                                         oneh16_sb[:, (2 * oc) * 64:(2 * oc + 1) * 64],
                                         tau_fm[0:16, tsl * TS:(tsl + 1) * TS],
                                         start=True, stop=True)
                                return t.matmul(ps_tb[64:128, 0:TS],
                                                oneh16_sb[:, (2 * oc + 1) * 64:(2 * oc + 2) * 64],
                                                tau_fm[0:16, tsl * TS:(tsl + 1) * TS],
                                                start=True, stop=True)
                            tok_tbm = S.add("tensor", tbm, after=[prev_tbc], tag="tbm")
                            def tbe(v, tsl=tsl):
                                return v.tensor_copy(tbc_sb[:, tsl * TS:(tsl + 1) * TS], ps_tb[:, 0:TS])
                            prev_tbc = S.add("vector", tbe, after=[tok_tbm], tag="tbe")
                            tbc_toks.append(prev_tbc)
                    for tsl in range(NTS):
                        bi = tsl % 2
                        def qmm(t, oc=oc, tsl=tsl, bi=bi, weff_sb=weff_sb):
                            r = None
                            first = True
                            for j in range(3):
                                for ic in range(NCH):
                                    r = t.matmul(ps_q[bi][:, 0:TS],
                                                 weff_sb[:, (j * NCH + ic) * D + oc * 128:(j * NCH + ic) * D + (oc + 1) * 128],
                                                 xn[:, ic * CW + tsl * TS + j: ic * CW + tsl * TS + j + TS],
                                                 start=first, stop=(j == 2 and ic == NCH - 1))
                                    first = False
                            return r
                        tok_mm = S.add("tensor", qmm, after=[prev_evac[bi]], tag="qkmm")
                        fix = None
                        if tsl == 0:
                            def fx(v, bi=bi, oc=oc, vcorr_sb=vcorr_sb):
                                return v.tensor_tensor(out=ps_q[bi][:, 0:1], in0=ps_q[bi][:, 0:1],
                                                       in1=vcorr_sb[:, oc:oc + 1], op=ALU.subtract)
                            fix = S.add("vector", fx, after=[tok_mm, tok_bias], tag="qfix")
                        if tsl == NTS - 1:
                            def fx(v, bi=bi, oc=oc, vcorr_sb=vcorr_sb):
                                return v.tensor_tensor(out=ps_q[bi][:, TS - 1:TS], in0=ps_q[bi][:, TS - 1:TS],
                                                       in1=vcorr_sb[:, 2 * NCH + oc: 2 * NCH + oc + 1],
                                                       op=ALU.subtract)
                            fix = S.add("vector", fx, after=[tok_mm, tok_bias], tag="qfix")
                        war = [prev_spill[bi], prev_sq[bi]]
                        if scaled:
                            def evac(v, bi=bi, oc=oc, tsl=tsl, bias_sb=bias_sb):
                                return v.scalar_tensor_tensor(out=qstage[bi][:], in0=ps_q[bi][:, 0:TS],
                                                              scalar=bias_sb[:, oc:oc + 1],
                                                              in1=tbc_sb[:, tsl * TS:(tsl + 1) * TS],
                                                              op0=ALU.add, op1=ALU.mult)
                            tok_ev = S.add("vector", evac, after=[tok_mm, fix, tok_bias] + tbc_toks + war, tag="qevac")
                        else:
                            def evac(s, bi=bi, oc=oc, bias_sb=bias_sb):
                                return s.activation(out=qstage[bi][:], in_=ps_q[bi][:, 0:TS],
                                                    func=AF.Identity, bias=bias_sb[:, oc:oc + 1],
                                                    scale=1.0)
                            tok_ev = S.add("scalar", evac, after=[tok_mm, fix, tok_bias] + war, tag="kevac")
                        prev_evac[bi] = tok_ev
                        tok_sp = S.dma("sync", buf[:, oc * L + tsl * TS: oc * L + (tsl + 1) * TS],
                                       qstage[bi].ap(), after=[tok_ev], tag="spill",
                                       slot=pref + "sp%d" % bi)
                        prev_spill[bi] = tok_sp
                        toks_spill.append(tok_sp)
                        def sq(v, bi=bi):
                            return v.tensor_tensor(out=sqq[bi][:], in0=qstage[bi][:], in1=qstage[bi][:],
                                                   op=ALU.mult)
                        tok_sq = S.add("vector", sq, after=[tok_ev, prev_nred], tag="nsq")
                        prev_sq[bi] = tok_sq
                        def emit_nmm(tok_sq=tok_sq, bi=bi, h0=h0, h1=h1):
                            def nmm(t):
                                t.matmul(ps_n[0][:, 0:TS], ones_col_sb[0:64, :], sqq[bi][0:64, :],
                                         start=True, stop=True)
                                return t.matmul(ps_n[1][:, 0:TS], ones_col_sb[64:128, :],
                                                sqq[bi][64:128, :], start=True, stop=True)
                            tok_nmm = S.add("tensor", nmm, after=[tok_sq, prev_nred], tag="nmm")
                            def nred(v, h0=h0, h1=h1, nrm_off=nrm_off):
                                v.tensor_reduce(out=nrow[0:1, 30:31], in_=ps_n[0][:, 0:TS],
                                                axis=AX.X, op=ALU.max)
                                v.tensor_reduce(out=nrow[0:1, 31:32], in_=ps_n[1][:, 0:TS],
                                                axis=AX.X, op=ALU.max)
                                v.tensor_tensor(out=nrm[0:1, nrm_off + h0: nrm_off + h0 + 1],
                                                in0=nrm[0:1, nrm_off + h0: nrm_off + h0 + 1],
                                                in1=nrow[0:1, 30:31], op=ALU.max)
                                return v.tensor_tensor(out=nrm[0:1, nrm_off + h1: nrm_off + h1 + 1],
                                                       in0=nrm[0:1, nrm_off + h1: nrm_off + h1 + 1],
                                                       in1=nrow[0:1, 31:32], op=ALU.max)
                            return S.add("vector", nred, after=[tok_nmm], tag="nred")
                        if pend_nmm[0] is not None:
                            prev_nred = pend_nmm[0]()
                        pend_nmm[0] = emit_nmm
                if pend_nmm[0] is not None:
                    prev_nred = pend_nmm[0]()
                    pend_nmm[0] = None
                if tap_name in taps:
                    S.dma("sync", tap_outs[tap_name], buf, after=toks_spill, slot="tap")
                S.emit_block()

        _tau_cms = []
        def _lalloc(name, shape, dt):
            cm = nc.sbuf_tensor(name, shape, dt, side="left")
            _tau_cms.append(cm)
            return cm.__enter__()
        xn_cm = nc.sbuf_tensor("xn", [128, NCH * CW], BF16, side="right")
        xn = xn_cm.__enter__()
        mbias = _lalloc("mbias", [128, H], F32)
        edelT = _lalloc("edelT", [128, NT * H], F32)
        tau_fm = _lalloc("tau_fm", [H, L], BF16)
        _tau3_cms = []
        def _lalloc3(name, shape, dt):
            cm = nc.sbuf_tensor(name, shape, dt, side="left")
            _tau3_cms.append(cm)
            return cm.__enter__()
        tauT = _lalloc3("tauT", [128, NT * H], F32)
        delT = _lalloc3("delT", [128, NT * H], F32)
        nrm = _lalloc3("nrm", [1, 2 * H], F32)
        nrow = _lalloc3("nrow", [1, 2 * H], F32)
        negm_bf = _lalloc3("negm_bf", [1, H], BF16)
        oneh16_sb = _lalloc3("oneh16_sb", [H, H * DKH], BF16)
        t_o16 = S.dma("sync", oneh16_sb[:], oneh16_in, slot="consts")
        CONSTS.append(t_o16)
        S.add("gpsimd", lambda g: g.memset(
            apx(xn, 0, 128, 0, [(CW, NCH), (CW - 1, 2)]), 0.0))

        with ExitStack() as stWQ:
            WE = stWQ.enter_context
            weff = WE(nc.sbuf_tensor("weff", [128, 3 * NCH * D], BF16, side="right"))
            FB["wn"] = WE(nc.sbuf_tensor("f_wn", [128, NCH * D], BF16, side="right"))
            FB["pB"] = WE(nc.sbuf_tensor("f_pB", [128, NCH * D], BF16, side="right"))
            FB["cT"] = WE(nc.sbuf_tensor("f_cTh", [128, NCH * TS], BF16, side="right"))
            FB["mjt"] = WE(nc.sbuf_tensor("f_mjth", [128, NCH * TS], BF16, side="right"))
            FB["pA"] = WE(nc.sbuf_tensor("f_pAh", [128, NCH * TS], BF16, side="right"))
            FB["wb"] = WE(nc.sbuf_tensor("f_wb", [128, NCH], BF16, side="right"))
            FB["cb"] = WE(nc.sbuf_tensor("f_cb", [128, NCH], BF16, side="right"))
            FB["pb"] = WE(nc.sbuf_tensor("f_pb", [128, NCH], F32, side="right"))
            FB["ps"] = [WE(nc.psum_tensor("f_ps%d" % i, [128, TS], F32)) for i in range(2)]
            FB["ps_v"] = WE(nc.psum_tensor("f_psv", [128, 4 * NCH], F32))

            def apx(sb_tensor, part0, nparts, off, dims):
                t = sb_tensor.ap().tensor
                w = sb_tensor.ap().ap[0][0]
                return bass.AP(t, part0 * w + off, [[w, nparts]] + [[s, c] for (s, c) in dims])

            fq_iters, fq_bias, fq_st = fold_path(
                "fq", wq_nat, cqT, qpA_T, qpB_T, wqb_pc, cqb_pc, qpb_pc,
                weff, biasq, vq_corr)

            stH = ExitStack()
            HE = stH.enter_context
            sqst4 = HE(nc.sbuf_tensor("p1_sq4", [128, 2 * TS], BF16))
            psb2d = [HE(nc.psum_tensor("p1_psb%d" % i, [128, TS], F32)) for i in range(2)]
            t1w = HE(nc.sbuf_tensor("t1w", [128, NCH * 6], F32))
            d1w = HE(nc.sbuf_tensor("d1w", [128, NCH * 6], F32))
            t1b = HE(nc.sbuf_tensor("t1b", [128, NCH * 2], F32))
            d1b = HE(nc.sbuf_tensor("d1b", [128, NCH * 2], F32))
            t2w = HE(nc.sbuf_tensor("t2w", [128, 16 * H], BF16))
            d2w = HE(nc.sbuf_tensor("d2w", [128, 16 * H], BF16))
            t2b = HE(nc.sbuf_tensor("t2b", [1, H], BF16))
            d2b = HE(nc.sbuf_tensor("d2b", [1, H], BF16))
            dwst = [HE(nc.sbuf_tensor("dwst%d" % i, [128, TS], F32)) for i in range(2)]
            gstage = HE(nc.sbuf_tensor("gstage", [128, 4 * TS], BF16))
            ps_pw_t = HE(nc.psum_tensor("ps_pw_t", [128, 512], F32))
            ps_pw_d = HE(nc.psum_tensor("ps_pw_d", [128, 512], F32))
            ps_tr = HE(nc.psum_tensor("ps_tr", [16, 128], F32))

            l1 = S.dma("sync", t1w[:], tau1w_in, slot="t1")
            l2 = S.dma("sync", d1w[:], del1w_in, slot="t1")
            l3 = S.dma("sync", t1b[:], tau1b_in, slot="t1")
            l4 = S.dma("sync", d1b[:], del1b_in, slot="t1")
            l5 = S.dma("gpsimd", t2w[:], tau2wT_in, slot="t2")
            l6 = S.dma("gpsimd", d2w[:], del2wT_in, slot="t2")
            l7 = S.dma("gpsimd", t2b[:], tau2b_in, slot="t2")
            l8 = S.dma("gpsimd", d2b[:], del2b_in, slot="t2")
            LD3 = [l1, l2, l3, l4, l5, l6, l7, l8]

            xld = {}

            def emit_xload(ti):
                if ti % 2 == 1:
                    xld[ti] = xld[ti - 1]
                    return
                tok = None
                for c in range(NCH):
                    tok = S.dma("gpsimd",
                                xn[:, c * CW + 1 + ti * TS: c * CW + 1 + (ti + 2) * TS],
                                x_in[:, c * L + ti * TS: c * L + (ti + 2) * TS],
                                slot="xload")
                xld[ti] = tok
                xld[ti + 1] = tok

            b2war = {"bank": {0: [], 1: []}, "mmsq": None}
            xn_toks = {}

            def mk_ln_x(ti):
                psb2 = psb2d[ti % 2]
                slc = lambda c: xn[:, c * CW + 1 + ti * TS: c * CW + 1 + (ti + 1) * TS]
                st = {}

                def s_stats():
                    def mm_sum(t):
                        r = None
                        for c in range(NCH):
                            r = t.matmul(psb2[0:1, 0:TS], ones_col_sb[:], slc(c),
                                         start=(c == 0), stop=(c == NCH - 1))
                        return r
                    t_sum = S.add("tensor", mm_sum,
                                  after=[xld[ti]] + CONSTS + b2war["bank"][ti % 2],
                                  tag="ln_sum")
                    mmh = None
                    for h in range(4):
                        def sqh(v, h=h):
                            return [v.tensor_tensor(out=sqst4[:, i * TS:(i + 1) * TS],
                                                    in0=slc(2 * h + i), in1=slc(2 * h + i),
                                                    op=ALU.mult) for i in range(2)][-1]
                        t_sq = S.add("vector", sqh,
                                     after=[xld[ti], mmh if h else b2war["mmsq"]],
                                     tag="ln_sq")
                        def mmsq(t, h=h):
                            r = None
                            for i in range(2):
                                r = t.matmul(psb2[32:33, 0:TS], ones_col_sb[:],
                                             sqst4[:, i * TS:(i + 1) * TS],
                                             start=(h == 0 and i == 0),
                                             stop=(h == 3 and i == 1))
                            return r
                        mmh = S.add("tensor", mmsq, after=[t_sq], tag="ln_sqmm")
                    b2war["mmsq"] = mmh
                    st["sum"], st["sq"] = t_sum, mmh

                def s_rmath():
                    def rmath_a(v):
                        v.tensor_scalar(out=negm0[0:1, 0:TS], in0=psb2[0:1, 0:TS],
                                        scalar1=-1.0 / D, scalar2=None, op0=ALU.mult)
                        return v.tensor_scalar(out=scr0[0:1, 0:TS], in0=psb2[32:33, 0:TS],
                                               scalar1=1.0 / D, scalar2=None, op0=ALU.mult)
                    t_ra = S.add("vector", rmath_a,
                                 after=[st["sum"], st["sq"], b2war.get("lnact")],
                                 tag="ln_ra")
                    t_rb = S.add("vector", lambda v: v.tensor_tensor(
                        out=scr1[0:1, 0:TS], in0=negm0[0:1, 0:TS], in1=negm0[0:1, 0:TS],
                        op=ALU.mult), after=[t_ra], tag="ln_rb")
                    st["rm"] = S.add("vector", lambda v: v.tensor_tensor(
                        out=scr0[0:1, 0:TS], in0=scr0[0:1, 0:TS], in1=scr1[0:1, 0:TS],
                        op=ALU.subtract), after=[t_rb], tag="ln_rc")
                    st["ra"] = t_ra

                def s_act():
                    t_ln = S.add("scalar", lambda sc: sc.activation(
                        out=scr1[0:1, 0:TS], in_=scr0[0:1, 0:TS], func=AF.Ln,
                        bias=eps_sb[0:1, 0:1], scale=1.0), after=[st["rm"]], tag="ln_ln")
                    st["rs"] = S.add("scalar", lambda sc: sc.activation(
                        out=rstd0[0:1, 0:TS], in_=scr1[0:1, 0:TS], func=AF.Exp, scale=-0.5),
                        after=[t_ln], tag="ln_exp")
                    b2war["lnact"] = st["rs"]
                    def bcn(t):
                        return t.matmul(psb2[:, 0:TS], ones_rowf_sb[0:1, 0:128],
                                        negm0[0:1, 0:TS], start=True, stop=True)
                    st["bn"] = S.add("tensor", bcn, after=[st["ra"]], tag="ln_bcn")

                def s_adds():
                    add_toks = []
                    prev = None
                    for c0 in range(0, NCH, 2):
                        def ap_add(v, c0=c0):
                            return [v.tensor_tensor(out=slc(c), in0=slc(c),
                                                    in1=psb2[:, 0:TS], op=ALU.add)
                                    for c in (c0, c0 + 1)][-1]
                        prev = S.add("vector", ap_add, after=[st["bn"], prev], tag="ap_add")
                        add_toks.append(prev)
                    st["adds"] = add_toks

                def s_muls():
                    def bcr(t):
                        return t.matmul(psb2[:, 0:TS], ones_rowf_sb[0:1, 0:128],
                                        rstd0[0:1, 0:TS], start=True, stop=True)
                    t_br = S.add("tensor", bcr, after=st["adds"] + [st["rs"]], tag="ln_bcr")
                    mul_toks = []
                    prev = None
                    for c0 in range(0, NCH, 2):
                        def ap_mul(v, c0=c0):
                            return [v.tensor_tensor(out=slc(c), in0=slc(c),
                                                    in1=psb2[:, 0:TS], op=ALU.mult)
                                    for c in (c0, c0 + 1)][-1]
                        prev = S.add("vector", ap_mul, after=[t_br, prev], tag="ap_mul")
                        mul_toks.append(prev)
                    b2war["bank"][ti % 2] = mul_toks
                    xn_toks[ti] = prev

                return [s_stats, s_rmath, s_act, s_adds, s_muls]

            all_pw = {"tau": [], "del": []}
            dw_state = {"prev_pw": [], "prev_ge": [None, None]}

            def mk_dw(ts_i):
                units = []
                pst = {}
                for path, w1, b1p, w2, b2p in (
                        ("tau", t1w, t1b, t2w, t2b),
                        ("del", d1w, d1b, d2w, d2b)):
                    pspw = ps_pw_t if path == "tau" else ps_pw_d
                    for q in range(4):
                        def unit(path=path, w1=w1, b1p=b1p, w2=w2, b2p=b2p,
                                 pspw=pspw, q=q):
                            pw_toks = pst.setdefault(path, [])
                            ge_toks = []
                            for cj in range(4 * q, 4 * q + 4):
                                c, j = divmod(cj, 2)
                                bi = cj % 2
                                def dw_s(v, k, c=c, j=j, bi=bi, w1=w1, ts_i=ts_i):
                                    base = c * CW + ts_i * TS
                                    wcol = w1[:, c * 6 + j * 3 + k: c * 6 + j * 3 + k + 1]
                                    if k == 0:
                                        return v.tensor_scalar(out=dwst[bi][:], in0=xn[:, base: base + TS],
                                                               scalar1=wcol, scalar2=None, op0=ALU.mult)
                                    return v.scalar_tensor_tensor(
                                        out=dwst[bi][:], in0=xn[:, base + k: base + k + TS],
                                        scalar=wcol, in1=dwst[bi][:], op0=ALU.mult, op1=ALU.add)
                                t_dw = S.add("vector", (lambda v, f=dw_s: f(v, 0)),
                                             after=LD3 + [dw_state["prev_ge"][bi]], tag="dw0")
                                t_dw = S.add("vector", (lambda v, f=dw_s: f(v, 1)), after=[t_dw], tag="dw1")
                                t_dw = S.add("vector", (lambda v, f=dw_s: f(v, 2)), after=[t_dw], tag="dw2")
                                def ge(s, c=c, j=j, bi=bi, cj=cj, b1p=b1p):
                                    return s.activation(out=gstage[:, (cj % 4) * TS:((cj % 4) + 1) * TS],
                                                        in_=dwst[bi][:], func=AF_GELU,
                                                        bias=b1p[:, c * 2 + j: c * 2 + j + 1], scale=1.0)
                                war = dw_state["prev_pw"] if q == 0 else pw_toks[-4:]
                                tok_ge = S.add("scalar", ge, after=[t_dw] + war, tag="gelu")
                                dw_state["prev_ge"][bi] = tok_ge
                                ge_toks.append(tok_ge)
                            for tt in range(4):
                                def pw(t, tt=tt, w2=w2, b2p=b2p, pspw=pspw, ts_i=ts_i, q=q):
                                    co = (ts_i * 4 + tt) * 16
                                    r = None
                                    for cj4 in range(4):
                                        cj = q * 4 + cj4
                                        r = t.matmul(pspw[:, co: co + 16],
                                                     gstage[:, cj4 * TS + tt * 128: cj4 * TS + (tt + 1) * 128],
                                                     w2[:, cj * H:(cj + 1) * H],
                                                     start=(cj == 0), stop=False)
                                    if q == 3:
                                        r = t.matmul(pspw[:, co: co + 16],
                                                     ones_row_sb[0:1, 0:128], b2p[0:1, :],
                                                     start=False, stop=True)
                                    return r
                                pw_toks.append(S.add("tensor", pw, after=ge_toks, tag="pw"))
                            if q == 3:
                                all_pw[path] += pw_toks[-4:]
                                dw_state["prev_pw"] = pw_toks[-4:]
                        units.append(unit)
                return units

            for ti in range(NTS):
                emit_xload(ti)
            tok_bq = None
            for ti in range(NTS):
                stages = mk_ln_x(ti)
                dwq = mk_dw(ti - 2) if ti >= 2 else None
                stages[0]()
                if dwq:
                    dwq[0](); dwq[1]()
                stages[1]()
                stages[2]()
                if dwq:
                    dwq[2](); dwq[3]()
                stages[3]()
                if dwq:
                    dwq[4](); dwq[5]()
                stages[4]()
                if dwq:
                    dwq[6](); dwq[7]()
                if ti >= 2 and ti - 2 < len(fq_iters):
                    fq_iters[ti - 2]()
            for ts_t in (6, 7):
                for u in mk_dw(ts_t):
                    u()
            tok_bq = fq_bias()

            def sig_t(s):
                return s.activation(out=tauT[:], in_=ps_pw_t[:], func=AF_SIG)
            tok_sig_t = S.add("scalar", sig_t, after=all_pw["tau"], tag="sig_t")
            def sig_d(s):
                return s.activation(out=delT[:], in_=ps_pw_d[:], func=AF_SIG)
            tok_sig_d = S.add("scalar", sig_d, after=all_pw["del"], tag="sig_d")
            tok_sig = {"tau": tok_sig_t, "del": tok_sig_d}
            def edel(s):
                return s.activation(out=edelT[:], in_=delT[:], func=AF.Exp)
            tok_edel = S.add("scalar", edel, after=[tok_sig["del"]], tag="edel")
            tfm_toks = []
            prev_ev_t = None
            for tt in range(NT):
                def tr(t, tt=tt):
                    return t.transpose(ps_tr[:], tauT[:, tt * H:(tt + 1) * H], ident_sb[:])
                tok_tr = S.add("tensor", tr, after=[tok_sig["tau"], prev_ev_t], tag="tfm_tr")
                def ev(v, tt=tt):
                    return v.tensor_scalar(out=tau_fm[0:16, tt * 128:(tt + 1) * 128],
                                           in0=ps_tr[0:16, :], scalar1=0.125,
                                           scalar2=None, op0=ALU.mult)
                prev_ev_t = S.add("vector", ev, after=[tok_tr], tag="tfm_ev")
                tfm_toks.append(prev_ev_t)
            if "xn" in taps:
                S.dma("sync", tap_outs["xn"], xn.ap(),
                      after=[xn_toks[ti] for ti in range(NTS)], slot="tap")
            if "tauT" in taps:
                S.dma("sync", tap_outs["tauT"], tauT.ap(), after=[tok_sig["tau"]], slot="tap")
            if "edelT" in taps:
                S.dma("sync", tap_outs["edelT"], edelT.ap(), after=[tok_edel], slot="tap")
            if "taufm" in taps:
                S.dma("sync", tap_outs["taufm"], tau_fm.ap(), after=tfm_toks, slot="tap")
            S.emit_block()
            stH.close()
            if stop_after <= 3:
                return nc, tap_outs
            qk_production(weff, biasq, vq_corr, qbuf, True, 0, "q", tok_bq)
            fk_iters, fk_bias, fk_st = fold_path(
                "fk", wk_nat, ckT, kpA_T, kpB_T, wkb_pc, ckb_pc, kpb_pc,
                weff, biask, vk_corr)
            for it in fk_iters:
                it()
            tok_bk = fk_bias()
            if "biask" in taps:
                S.dma("sync", tap_outs["biask"], biask.ap(), after=[], slot="tap")
            if "vkcorr" in taps:
                S.dma("sync", tap_outs["vkcorr"], vk_corr.ap(), after=[], slot="tap")
            S.emit_block()
            qk_production(weff, biask, vk_corr, kbuf, False, H, "k", tok_bk)
        if stop_after <= 4:
            return nc, tap_outs

        vaug_cm = nc.sbuf_tensor("vaug", [128, NT * H * 65], BF16, side="left")
        vaug = vaug_cm.__enter__()
        outT_cm = nc.sbuf_tensor("outT_sb", [128, NCH * D], BF16, side="left")
        outT_sb = outT_cm.__enter__()
        with ExitStack() as stV:
            EE = stV.enter_context
            wvT_sb = EE(nc.sbuf_tensor("wvT_sb", [128, NCH * D], BF16, side="right"))
            wvb_bf = EE(nc.sbuf_tensor("wvb_bf", [1, D], BF16, side="right"))
            ps_v = [EE(nc.psum_tensor("v_ps%d" % i, [128, TS], F32)) for i in range(2)]
            ps_m = EE(nc.psum_tensor("v_psm", [128, H], F32))

            vraw_sb = EE(nc.sbuf_tensor("vraw_sb", [128, TS], F32, side="right")) if "vraw" in taps else None
            for cc in range(NCH):
                S.dma("gpsimd", wvT_sb[:, cc * D:(cc + 1) * D],
                      wvT_in[cc * 128:(cc + 1) * 128, :], slot="vw")
                S.dma("gpsimd", outT_sb[:, cc * D:(cc + 1) * D],
                      outT_in[cc * 128:(cc + 1) * 128, :], slot="vw")
            lvb = S.dma("gpsimd", wvb_bf[:], wvb_row_in, slot="vw")
            lv = lo = lvb

            def mrow(v):
                return v.tensor_tensor(out=nrow[0:1, 0:H], in0=nrm[0:1, 0:H],
                                       in1=nrm[0:1, H:2 * H], op=ALU.mult)
            tok_mr = S.add("vector", mrow, tag="mrow")
            def mln(s):
                s.activation(out=nrow[0:1, 0:H], in_=nrow[0:1, 0:H], func=AF.Ln)
                return s.activation(out=nrow[0:1, 0:H], in_=nrow[0:1, 0:H], func=AF.Exp, scale=0.5)
            tok_mln = S.add("scalar", mln, after=[tok_mr], tag="mln")
            def mneg(v):
                return v.tensor_scalar(out=negm_bf[0:1, 0:H], in0=nrow[0:1, 0:H],
                                       scalar1=-1.0, scalar2=None, op0=ALU.mult)
            tok_mneg = S.add("vector", mneg, after=[tok_mln], tag="mneg")
            def mbc(t):
                r = None
                for h in range(H):
                    r = t.matmul(ps_m[:, h:h + 1], ones_row_sb[0:1, 0:128],
                                 negm_bf[0:1, h:h + 1], start=True, stop=True)
                return r
            tok_mbc = S.add("tensor", mbc, after=[tok_mneg], tag="mbc")
            tok_mev = S.add("vector", lambda v: v.tensor_copy(mbias[:], ps_m[:]),
                            after=[tok_mbc], tag="mev")

            vaug_toks = []
            prev_vev = [None, None]
            for tt in range(NT):
                def augc(v, tt=tt):
                    return v.tensor_copy(apx(vaug, 0, 128, tt * 1040 + 64, [(65, H)]),
                                         edelT[:, tt * H:(tt + 1) * H])
                tok_aug = S.add("vector", augc, tag="aug")
                vaug_toks.append(tok_aug)
                for dh in range(2):
                    bi = (tt * 2 + dh) % 2
                    def vmm(t, tt=tt, dh=dh, bi=bi):
                        r = None
                        for ic in range(NCH):
                            r = t.matmul(ps_v[bi][:, 0:TS],
                                         xn[:, ic * CW + 1 + tt * 128: ic * CW + 1 + (tt + 1) * 128],
                                         wvT_sb[:, ic * D + dh * TS: ic * D + (dh + 1) * TS],
                                         start=(ic == 0), stop=False)
                        return t.matmul(ps_v[bi][:, 0:TS], ones_row_sb[0:1, 0:128],
                                        wvb_bf[0:1, dh * TS:(dh + 1) * TS], start=False, stop=True)
                    tok_vmm = S.add("tensor", vmm, after=[lv, lvb, prev_vev[bi]], tag="vmm")
                    def vev(v, tt=tt, dh=dh, bi=bi):
                        r = None
                        for hh in range(8):
                            h = dh * 8 + hh
                            r = v.tensor_scalar(
                                out=vaug[:, tt * 1040 + h * 65: tt * 1040 + h * 65 + 64],
                                in0=ps_v[bi][:, hh * 64:(hh + 1) * 64],
                                scalar1=edelT[:, tt * H + h: tt * H + h + 1],
                                scalar2=None, op0=ALU.mult)
                        return r
                    if tt == 0 and dh == 0 and "vraw" in taps:
                        tr = S.add("vector", (lambda v, bi=bi: v.tensor_copy(
                            vraw_sb[:], ps_v[bi][:, 0:TS])), after=[tok_vmm], tag="vrawcp")
                        S.dma("sync", tap_outs["vraw"], vraw_sb.ap(), after=[tr], slot="tap")
                    tok_vev = S.add("vector", vev, after=[tok_vmm], tag="vev")
                    prev_vev[bi] = tok_vev
                    vaug_toks.append(tok_vev)
            if "xn2nd" in taps:
                S.dma("sync", tap_outs["xn2nd"], xn.ap(), after=[], slot="tap")
            if "mbias" in taps:
                S.dma("sync", tap_outs["mbias"], mbias.ap(), after=[tok_mev], slot="tap")
            if "vaug" in taps:
                S.dma("sync", tap_outs["vaug"], vaug.ap(), after=vaug_toks, slot="tap")
            S.emit_block()
        if stop_after <= 5:
            return nc, tap_outs

        xn_cm.__exit__(None, None, None)
        with ExitStack() as st5:
            EE = st5.enter_context
            qsuper = [EE(nc.sbuf_tensor("qsuper%d" % i, [128, NCH * QS], BF16, side="right"))
                      for i in range(2)]
            khp = [EE(nc.sbuf_tensor("khp%d" % i, [128, L], BF16, side="right")) for i in range(2)]
            probs0 = [EE(nc.sbuf_tensor("pb0%d" % i, [128, QS], BF16, side="right")) for i in range(2)]
            probs1 = [EE(nc.sbuf_tensor("pb1%d" % i, [128, QS], BF16, side="right")) for i in range(2)]
            attn_fm = [EE(nc.sbuf_tensor("attn_fm%d" % i, [128, NCH * QS], BF16, side="right"))
                       for i in range(2)]
            zr0b = EE(nc.sbuf_tensor("zr0b", [1, QS], BF16, side="right"))
            zr1b = EE(nc.sbuf_tensor("zr1b", [1, QS], BF16, side="right"))
            zrecf = EE(nc.sbuf_tensor("zrecf", [128, QS], F32, side="right"))
            attn_st = EE(nc.sbuf_tensor("attn_st", [128, QS], BF16, side="right"))
            outb_pc = EE(nc.sbuf_tensor("outb_pc_sb", [128, NCH], F32, side="right"))
            proj_t = EE(nc.sbuf_tensor("proj_t", [128, NCH * TS], BF16, side="right"))
            tmpf5 = [EE(nc.sbuf_tensor("tmpf50", [128, TS], F32, side="right"))] * 2
            xres = EE(nc.sbuf_tensor("xres", [128, NCH * TS], F32, side="right"))
            xn2_t = EE(nc.sbuf_tensor("xn2_t", [128, NCH * TS], BF16, side="right"))
            sq_all = EE(nc.sbuf_tensor("sq_all", [128, NCH * TS], BF16, side="right"))
            zop = EE(nc.psum_tensor("b5_zop", [128, QS], F32))
            ps_bc5p = EE(nc.psum_tensor("b5_bcp", [128, TS], F32))

            lob = S.dma("sync", outb_pc[:], outb_pc_in, slot="tap")

            tw_chunks = ([(tw1bf, tw1T_in, i * 512, 512) for i in range(16)] +
                         [(tw2bf, tw2T_in, i * 2048, 2048) for i in range(16)])
            t_twcast = None
            _twci = [0, 0]

            def emit_twcast(pace):
                if _twci[0] < len(tw_chunks):
                    dstb, srcb, r0, nr = tw_chunks[_twci[0]]
                    _twci[0] += 1
                    return S.dma("gpsimd", dstb[r0:r0 + nr, :], srcb[r0:r0 + nr, :],
                                 after=[pace], slot="twc")
                return None

            kwar = [None, None]
            qwar = [None, None]
            x2_spills = []
            xn2_spills = []
            prev_xres_user = None
            prev_xn2_user = None
            prev_n0 = prev_n1 = None
            prev_pa0 = []
            prev_pa1 = []
            prev_av0_tail = {}
            prev_av1_tail = {}
            prev_e0_tail = {-1: None, -2: None}
            prev_e1_tail = {-1: None, -2: None}
            prev_zbc = None
            pending_znorm = None
            zop_chain = [None]
            attnfm_war = [None, None]
            pb_bank_war = [None]
            sq_war = [None]
            pb_plan = None

            def make_phaseB(s, afm, attn_deps):
                st = {}

                def pmm_one(oc):
                    def go():
                        def pmm(t, oc=oc):
                            r = None
                            for hc in range(NCH):
                                r = t.matmul(zop[:, 0:TS],
                                             outT_sb[:, hc * D + oc * 128: hc * D + (oc + 1) * 128],
                                             afm[:, hc * QS: hc * QS + TS],
                                             start=(hc == 0), stop=(hc == NCH - 1))
                            return r
                        t_pm = S.add("tensor", pmm,
                                     after=attn_deps + [zop_chain[0]], tag="pmm")
                        def pevd(v, oc=oc):
                            return v.tensor_scalar(
                                out=proj_t[:, oc * TS:(oc + 1) * TS],
                                in0=zop[:, 0:TS], scalar1=outb_pc[:, oc:oc + 1],
                                scalar2=None, op0=ALU.add)
                        t_pe = S.add("vector", pevd, after=[t_pm, lob], tag="pevd")
                        zop_chain[0] = t_pe
                        st.setdefault("pev", []).append(t_pe)
                        if oc == 7:
                            attnfm_war[s % 2] = t_pm
                    return go

                def xres_load():
                    st["xr"] = S.dma("gpsimd", xres.ap(),
                                     bass.AP(x_in.tensor, s * QS,
                                             [[NCH * L, 128], [L, NCH], [1, TS]]),
                                     after=[prev_xres_user], slot="xresg", tag="xres")

                def sq_half(slc, h, dep_extra, key):
                    def sqf(v, h=h):
                        return [v.tensor_tensor(out=sq_all[:, c * TS:(c + 1) * TS],
                                                in0=slc(c), in1=slc(c), op=ALU.mult)
                                for c in range(4 * h, 4 * h + 4)][-1]
                    st.setdefault(key, []).append(
                        S.add("vector", sqf, after=dep_extra + [sq_war[0]], tag="ln_sq"))

                def stats_sum(slc, dep_extra):
                    def mm_sum(t):
                        r = None
                        for c in range(NCH):
                            r = t.matmul(ps_bc5p[0:1, 0:TS], ones_col_sb[:], slc(c),
                                         start=(c == 0), stop=(c == NCH - 1))
                        return r
                    return S.add("tensor", mm_sum,
                                 after=dep_extra + CONSTS + [pb_bank_war[0]], tag="ln_sum")

                def stats_sq(sq_deps):
                    def mmsq(t):
                        r = None
                        for c in range(NCH):
                            r = t.matmul(ps_bc5p[32:33, 0:TS], ones_col_sb[:],
                                         sq_all[:, c * TS:(c + 1) * TS],
                                         start=(c == 0), stop=(c == NCH - 1))
                        return r
                    tok = S.add("tensor", mmsq, after=sq_deps + [pb_bank_war[0]],
                                tag="ln_sqmm")
                    sq_war[0] = tok
                    return tok

                def ln_math_dve(t_sum, t_sq):
                    def rmath_a(v):
                        v.tensor_scalar(out=negm0[0:1, 0:TS], in0=ps_bc5p[0:1, 0:TS],
                                        scalar1=-1.0 / D, scalar2=None, op0=ALU.mult)
                        return v.tensor_scalar(out=scr0[0:1, 0:TS], in0=ps_bc5p[32:33, 0:TS],
                                               scalar1=1.0 / D, scalar2=None, op0=ALU.mult)
                    t_ra = S.add("vector", rmath_a, after=[t_sum, t_sq], tag="ln_ra")
                    t_rb = S.add("vector", lambda v: v.tensor_tensor(
                        out=scr1[0:1, 0:TS], in0=negm0[0:1, 0:TS], in1=negm0[0:1, 0:TS],
                        op=ALU.mult), after=[t_ra], tag="ln_rb")
                    t_rm = S.add("vector", lambda v: v.tensor_tensor(
                        out=scr0[0:1, 0:TS], in0=scr0[0:1, 0:TS], in1=scr1[0:1, 0:TS],
                        op=ALU.subtract), after=[t_rb], tag="ln_rc")
                    return t_ra, t_rm

                def ln_math_act(t_rm):
                    t_ln = S.add("scalar", lambda sc: sc.activation(
                        out=scr1[0:1, 0:TS], in_=scr0[0:1, 0:TS], func=AF.Ln,
                        bias=eps_sb[0:1, 0:1], scale=1.0), after=[t_rm], tag="ln_ln")
                    return S.add("scalar", lambda sc: sc.activation(
                        out=rstd0[0:1, 0:TS], in_=scr1[0:1, 0:TS], func=AF.Exp, scale=-0.5),
                        after=[t_ln], tag="ln_exp")

                def emit_bcn(t_ra):
                    def bcn(t):
                        return t.matmul(ps_bc5p[:, 0:TS], ones_rowf_sb[0:1, 0:128],
                                        negm0[0:1, 0:TS], start=True, stop=True)
                    return S.add("tensor", bcn, after=[t_ra], tag="ln_bcn")

                def ln_addpass(slc, t_bn):
                    toks = []
                    prev = None
                    for c0 in range(0, NCH, 2):
                        def ap_add(v, c0=c0):
                            return [v.tensor_tensor(out=slc(c), in0=slc(c),
                                                    in1=ps_bc5p[:, 0:TS], op=ALU.add)
                                    for c in (c0, c0 + 1)][-1]
                        prev = S.add("vector", ap_add, after=[t_bn, prev], tag="ap_add")
                        toks.append(prev)
                    return toks

                def ln_bcr(add_toks, t_rs):
                    def bcr(t):
                        return t.matmul(ps_bc5p[:, 0:TS], ones_rowf_sb[0:1, 0:128],
                                        rstd0[0:1, 0:TS], start=True, stop=True)
                    return S.add("tensor", bcr, after=add_toks + [t_rs], tag="ln_bcr")

                slc1 = lambda c: proj_t[:, c * TS:(c + 1) * TS]
                slc2 = lambda c: xn2_t[:, c * TS:(c + 1) * TS]

                def sq1h(h):
                    def go():
                        sq_half(slc1, h, st["pev"], "sq1")
                    return go

                def stats1_sum():
                    st["sum1"] = stats_sum(slc1, st["pev"])

                def stats1_sq():
                    st["sq1mm"] = stats_sq(st["sq1"])

                def ln1_math():
                    st["mathn"] = ln_math_dve(st["sum1"], st["sq1mm"])

                def ln1_mact():
                    st["rs1"] = ln_math_act(st["mathn"][1])

                def ln1_bcn():
                    st["bcn1"] = emit_bcn(st["mathn"][0])

                def ln1_add():
                    st["addp"] = ln_addpass(slc1, st["bcn1"])

                def ln1_bcr():
                    st["bcr1"] = ln_bcr(st["addp"], st["rs1"])

                def ln1_out():
                    t_br = st["bcr1"]
                    prev = None
                    toks = []
                    for c in range(NCH):
                        def ap_mul(v, c=c):
                            return v.tensor_tensor(out=tmpf5[0][:],
                                                   in0=proj_t[:, c * TS:(c + 1) * TS],
                                                   in1=ps_bc5p[:, 0:TS], op=ALU.mult)
                        t_m = S.add("vector", ap_mul, after=[t_br, prev, st["xr"]], tag="ap_mul")
                        def ap_out(v, c=c):
                            return v.tensor_tensor(out=xres[:, c * TS:(c + 1) * TS],
                                                   in0=tmpf5[0][:],
                                                   in1=xres[:, c * TS:(c + 1) * TS], op=ALU.add)
                        prev = S.add("vector", ap_out, after=[t_m], tag="ap_out")
                        toks.append(prev)
                    st["outp"] = toks
                    st["mul1"] = toks

                def x2_and_cp():
                    nonlocal prev_xres_user, prev_xn2_user
                    tok_sp = S.dma("gpsimd",
                                   bass.AP(x2buf.tensor, s * QS,
                                           [[NCH * L, 128], [L, NCH], [1, TS]]),
                                   xres.ap(), after=st["outp"], slot="x2og", tag="x2spill")
                    x2_spills.append(tok_sp)
                    prev_xres_user = tok_sp
                    def cpf(v):
                        return [v.tensor_copy(xn2_t[:, c * TS:(c + 1) * TS],
                                              xres[:, c * TS:(c + 1) * TS])
                                for c in range(NCH)][-1]
                    st["cp"] = S.add("vector", cpf,
                                     after=st["outp"] + [prev_xn2_user], tag="xn2cp")

                def sq2h(h):
                    def go():
                        sq_half(slc2, h, [st["cp"]], "sq2")
                    return go

                def stats2_sum():
                    st["sum2"] = stats_sum(slc2, [st["cp"]] + st["mul1"])

                def stats2_sq():
                    st["sq2mm"] = stats_sq(st["sq2"])

                def ln2_math():
                    st["mathn2"] = ln_math_dve(st["sum2"], st["sq2mm"])

                def ln2_mact():
                    st["rs2"] = ln_math_act(st["mathn2"][1])

                def ln2_bcn():
                    st["bcn2"] = emit_bcn(st["mathn2"][0])

                def ln2_add():
                    st["addp2"] = ln_addpass(slc2, st["bcn2"])

                def ln2_out():
                    nonlocal prev_xn2_user
                    t_br = ln_bcr(st["addp2"], st["rs2"])
                    prev = None
                    mul_toks = []
                    for c0 in range(0, NCH, 2):
                        def ap_mul2(v, c0=c0):
                            return [v.tensor_tensor(out=slc2(c), in0=slc2(c),
                                                    in1=ps_bc5p[:, 0:TS], op=ALU.mult)
                                    for c in (c0, c0 + 1)][-1]
                        prev = S.add("vector", ap_mul2, after=[t_br, prev], tag="ap_mul2")
                        mul_toks.append(prev)
                    pb_bank_war[0] = prev
                    tok_sp = S.dma("gpsimd",
                                   bass.AP(xn2buf.tensor, s * QS,
                                           [[NCH * L, 128], [L, NCH], [1, TS]]),
                                   xn2_t.ap(), after=mul_toks, slot="xn2og", tag="xn2spill")
                    xn2_spills.append(tok_sp)
                    prev_xn2_user = tok_sp

                return {(0, 8): [pmm_one(0)], (0, 20): [pmm_one(1)],
                        (1, 4): [pmm_one(2)], (1, 16): [pmm_one(3)],
                        (2, 4): [pmm_one(4)], (2, 16): [pmm_one(5)],
                        (3, 4): [pmm_one(6)], (3, 16): [pmm_one(7)],
                        (3, 24): [xres_load],
                        (4, 4): [sq1h(0)], (4, 12): [sq1h(1)],
                        (4, 20): [stats1_sum], (4, 26): [stats1_sq],
                        (5, 2): [ln1_math], (5, 8): [ln1_mact],
                        (5, 12): [ln1_bcn], (5, 18): [ln1_add],
                        (6, 4): [ln1_bcr], (6, 10): [ln1_out], (6, 22): [x2_and_cp],
                        (7, 2): [sq2h(0)], (7, 4): [sq2h(1)],
                        (7, 8): [stats2_sum], (7, 10): [stats2_sq],
                        (7, 14): [ln2_math], (7, 18): [ln2_mact],
                        (7, 20): [ln2_bcn], (7, 22): [ln2_add],
                        (7, 26): [ln2_out]}

            for sidx in range(NQS):
                with ExitStack() as stA:
                    PA = stA.enter_context
                    psS0 = [PA(nc.psum_tensor("b5_psA%d_%d" % (i, sidx), [128, QS], F32)) for i in range(2)]
                    psS1 = [PA(nc.psum_tensor("b5_psB%d_%d" % (i, sidx), [128, QS], F32)) for i in range(2)]
                    psOA = PA(nc.psum_tensor("b5_psOA_%d" % sidx, [65, QS], F32))
                    psOB = PA(nc.psum_tensor("b5_psOB_%d" % sidx, [65, QS], F32))

                    afm_cur = attn_fm[sidx % 2]
                    qsb = qsuper[sidx % 2]
                    tok_q = S.dma("sync", qsb.ap(),
                                  bass.AP(qbuf.tensor, sidx * QS,
                                          [[NCH * L, 128], [L, NCH], [1, QS]]),
                                  after=[qwar[sidx % 2]], slot="qsup%d" % (sidx % 2),
                                  tag="qload")
                    attn_toks = []
                    last_qk_super = None

                    def flush_znorm():
                        nonlocal pending_znorm, prev_zbc
                        nonlocal prev_n0, prev_n1
                        zpc, t_zc0, t_zc1, t_evA, t_evB, zafm, zpar, ztoks = pending_znorm
                        pending_znorm = None

                        def zbc(t):
                            t.matmul(zop[0:64, 0:QS], ones_row_sb[0:1, 0:64],
                                     zr0b[0:1, 0:QS], start=True, stop=True)
                            return t.matmul(zop[64:128, 0:QS], ones_row_sb[0:1, 0:64],
                                            zr1b[0:1, 0:QS], start=True, stop=True)
                        t_zbc = S.add("tensor", zbc, after=[t_zc0, t_zc1, zop_chain[0]],
                                      tag="zbc")
                        t_zrec = S.add("vector",
                                       lambda v: v.reciprocal(zrecf[:, 0:QS], zop[:, 0:QS]),
                                       after=[t_zbc, prev_n0, prev_n1],
                                       tag="zrec")
                        zop_chain[0] = t_zrec

                        def nrm0(v, zpc=zpc, afm=zafm):
                            return v.tensor_tensor(out=afm[0:64, zpc * QS:(zpc + 1) * QS],
                                                   in0=attn_st[0:64, 0:QS],
                                                   in1=zrecf[0:64, 0:QS], op=ALU.mult)
                        t_n0 = S.add("vector", nrm0,
                                     after=[t_zrec, t_evA, attnfm_war[zpar]], tag="nrm0")

                        def nrm1(v, zpc=zpc, afm=zafm):
                            return v.tensor_tensor(out=afm[64:128, zpc * QS:(zpc + 1) * QS],
                                                   in0=attn_st[64:128, 0:QS],
                                                   in1=zrecf[64:128, 0:QS], op=ALU.mult)
                        t_n1 = S.add("vector", nrm1,
                                     after=[t_zrec, t_evB, attnfm_war[zpar]], tag="nrm1")
                        ztoks.extend([t_n0, t_n1])
                        prev_zbc = t_zbc
                        prev_n0, prev_n1 = t_n0, t_n1

                    for pc in range(NCH):
                        h0, h1 = 2 * pc, 2 * pc + 1
                        tok_k = S.dma("sync", khp[pc % 2].ap(),
                                      kbuf[:, pc * L:(pc + 1) * L],
                                      after=[kwar[pc % 2]],
                                      slot="khp%d" % (pc % 2), tag="kload")
                        tok_exp0 = dict(prev_e0_tail)
                        tok_exp1 = dict(prev_e1_tail)
                        tok_av0 = dict(prev_av0_tail)
                        tok_av1 = dict(prev_av1_tail)
                        tok_qk0 = {}
                        tok_qk1 = {}
                        def qk0(t, pc=pc, kt=0, qsb=qsb):
                            return t.matmul(psS0[kt % 2][:, 0:QS],
                                            khp[pc % 2][0:64, kt * 128:(kt + 1) * 128],
                                            qsb[0:64, pc * QS: (pc + 1) * QS],
                                            start=True, stop=True)
                        def qk1(t, pc=pc, kt=0, qsb=qsb):
                            return t.matmul(psS1[kt % 2][:, 0:QS],
                                            khp[pc % 2][64:128, kt * 128:(kt + 1) * 128],
                                            qsb[64:128, pc * QS: (pc + 1) * QS],
                                            start=True, stop=True)
                        for kt0 in range(2):
                            tok_qk0[kt0] = S.add("tensor", (lambda t, kt=kt0, f=qk0: f(t, kt=kt)),
                                                 after=[tok_q, tok_k, tok_exp0[kt0 - 2]], tag="qk0")
                            tok_qk1[kt0] = S.add("tensor", (lambda t, kt=kt0, f=qk1: f(t, kt=kt)),
                                                 after=[tok_q, tok_k, tok_exp1[kt0 - 2]], tag="qk1")
                        if pending_znorm is not None:
                            flush_znorm()
                        _twci[1] += 1
                        if _twci[1] % 2 == 0:
                            tcast = emit_twcast(tok_k)
                            if tcast is not None:
                                t_twcast = tcast
                        for kt in range(NT):
                            if pb_plan is not None:
                                for f in pb_plan.get((pc, kt), []):
                                    f()
                            def ex0(sc, kt=kt, h0=h0):
                                return sc.activation(out=probs0[kt % 2][:], in_=psS0[kt % 2][:, 0:QS],
                                                     func=AF.Exp, bias=mbias[:, h0:h0 + 1], scale=1.0)
                            tok_exp0[kt] = S.add("scalar", ex0,
                                                 after=[tok_qk0[kt], tok_av0.get(kt - 2)], tag="exp0")
                            def ex1(sc, kt=kt, h1=h1):
                                return sc.activation(out=probs1[kt % 2][:], in_=psS1[kt % 2][:, 0:QS],
                                                     func=AF.Exp, bias=mbias[:, h1:h1 + 1], scale=1.0)
                            tok_exp1[kt] = S.add("scalar", ex1,
                                                 after=[tok_qk1[kt], tok_av1.get(kt - 2)], tag="exp1")
                            if kt + 2 < NT:
                                tok_qk0[kt + 2] = S.add("tensor", (lambda t, kt=kt, f=qk0: f(t, kt=kt + 2)),
                                                        after=[tok_exp0[kt]], tag="qk0")
                                tok_qk1[kt + 2] = S.add("tensor", (lambda t, kt=kt, f=qk1: f(t, kt=kt + 2)),
                                                        after=[tok_exp1[kt]], tag="qk1")
                            def av0(t, kt=kt, h0=h0):
                                return t.matmul(psOA[:, 0:QS],
                                                vaug[:, kt * 1040 + h0 * 65: kt * 1040 + h0 * 65 + 65],
                                                probs0[kt % 2][:, 0:QS],
                                                start=(kt == 0), stop=(kt == NT - 1))
                            tok_av0[kt] = S.add("tensor", av0,
                                after=[tok_exp0[kt]] + (prev_pa0 if kt == 0 else []), tag="av0")
                            def av1(t, kt=kt, h1=h1):
                                return t.matmul(psOB[:, 0:QS],
                                                vaug[:, kt * 1040 + h1 * 65: kt * 1040 + h1 * 65 + 65],
                                                probs1[kt % 2][:, 0:QS],
                                                start=(kt == 0), stop=(kt == NT - 1))
                            tok_av1[kt] = S.add("tensor", av1,
                                after=[tok_exp1[kt]] + (prev_pa1 if kt == 0 else []), tag="av1")
                        last_qk_super = tok_qk1[NT - 1]
                        kwar[pc % 2] = last_qk_super
                        t_zc0 = S.add("vector",
                                      lambda v: v.tensor_copy(zr0b[0:1, 0:QS], psOA[64:65, 0:QS]),
                                      after=[tok_av0[NT - 1], prev_zbc], tag="zc0")
                        t_evA = S.add("vector",
                                      lambda v: v.tensor_copy(attn_st[0:64, 0:QS], psOA[0:64, 0:QS]),
                                      after=[tok_av0[NT - 1], prev_n0], tag="evA")
                        t_zc1 = S.add("vector",
                                      lambda v: v.tensor_copy(zr1b[0:1, 0:QS], psOB[64:65, 0:QS]),
                                      after=[tok_av1[NT - 1], prev_zbc], tag="zc1")
                        t_evB = S.add("vector",
                                      lambda v: v.tensor_copy(attn_st[64:128, 0:QS], psOB[0:64, 0:QS]),
                                      after=[tok_av1[NT - 1], prev_n1], tag="evB")
                        pending_znorm = (pc, t_zc0, t_zc1, t_evA, t_evB,
                                         afm_cur, sidx % 2, attn_toks)
                        prev_pa0 = [t_zc0, t_evA]
                        prev_pa1 = [t_zc1, t_evB]
                        prev_e0_tail = {-1: tok_exp0[NT - 1], -2: tok_exp0[NT - 2]}
                        prev_e1_tail = {-1: tok_exp1[NT - 1], -2: tok_exp1[NT - 2]}
                        prev_av0_tail = {-2: tok_av0[NT - 2], -1: tok_av0[NT - 1]}
                        prev_av1_tail = {-2: tok_av1[NT - 2], -1: tok_av1[NT - 1]}
                    qwar[sidx % 2] = last_qk_super
                    pb_plan = make_phaseB(sidx, afm_cur, attn_toks)
            flush_znorm()
            for key in sorted(pb_plan.keys()):
                for f in pb_plan[key]:
                    f()
            S.emit_block()
        if "x2" in taps:
            S.dma("sync", tap_outs["x2"], x2buf, after=[], slot="tap")
        if "xn2" in taps:
            S.dma("sync", tap_outs["xn2"], xn2buf, after=[], slot="tap")
        S.emit_block()
        if stop_after <= 6:
            return nc, tap_outs

        outT_cm.__exit__(None, None, None)
        vaug_cm.__exit__(None, None, None)
        for _cm in reversed(_tau3_cms):
            _cm.__exit__(None, None, None)
        for _cm in reversed(_tau_cms):
            _cm.__exit__(None, None, None)
        mw1_cm = nc.sbuf_tensor("mW1T", [128, NCH * FF], BF16, side="left")
        mW1T = mw1_cm.__enter__()
        mw2_cm = nc.sbuf_tensor("mW2T", [128, 32 * D], BF16, side="left")
        mW2T = mw2_cm.__enter__()
        with ExitStack() as st6:
            EE = st6.enter_context
            gw1T_sb = EE(nc.sbuf_tensor("gw1T_sb", [128, NCH * D], BF16, side="right"))
            gw2T_sb = EE(nc.sbuf_tensor("gw2T_sb", [128, NCH * E], BF16, side="right"))
            gb1_sb = EE(nc.sbuf_tensor("gb1_sb", [128, NCH], F32, side="right"))
            gb2_sb = EE(nc.sbuf_tensor("gb2_sb", [E, 1], F32, side="right"))
            xn2s = EE(nc.sbuf_tensor("g_xn2s", [128, NCH * TS], BF16, side="right"))
            h_sb = [EE(nc.sbuf_tensor("g_h0", [128, TS], BF16, side="right"))] * 4
            tbs = [EE(nc.sbuf_tensor("g_tbs%d" % i, [E, 128], BF16, side="right")) for i in range(2)]
            b1pc_sb = EE(nc.sbuf_tensor("b1pc_sb", [128, 32], F32, side="right"))
            b2pc_sb = EE(nc.sbuf_tensor("b2pc_sb", [128, 8], F32, side="right"))
            acc6 = EE(nc.sbuf_tensor("acc6", [128, FF], F32, side="right"))
            twb = [EE(nc.sbuf_tensor("twb%d" % i, [128, FF // 2], BF16, side="right"))
                   for i in range(3)]
            ps_h = [EE(nc.psum_tensor("g_psh%d" % i, [128, TS], F32)) for i in range(4)]
            ps_g2 = EE(nc.psum_tensor("g_psg2", [E, TS], F32))
            ps_gr = EE(nc.psum_tensor("ps_gr", [1, E], F32))
            ps_gb = EE(nc.psum_tensor("ps_gb", [128, E], F32))
            ps_mb = EE(nc.psum_tensor("ps_mb", [128, 40], F32))

            lg = []
            for cc in range(NCH):
                lg.append(S.dma("gpsimd", gw1T_sb[:, cc * D:(cc + 1) * D],
                                gw1T_in[cc * 128:(cc + 1) * 128, :], slot="gw"))
                lg.append(S.dma("gpsimd", gw2T_sb[:, cc * E:(cc + 1) * E],
                                gw2T_in[cc * 128:(cc + 1) * 128, :], slot="gw"))
            lg.append(S.dma("sync", gb1_sb[:], gb1_pc_in, slot="gwf"))
            lg.append(S.dma("sync", gb2_sb[:], gb2_col_in, slot="gwf"))
            lb1 = S.dma("sync", b1pc_sb[:, 0:32], b1pc_in, slot="gwf")
            lb2 = S.dma("sync", b2pc_sb[:, 0:8], b2pc_in, slot="gwf")

            prev_he = [None, None, None, None]
            prev_g2 = None
            prev_acc = None
            prev_xl = None
            for tt in range(NTS):
                tok_x = S.dma("sync", xn2s.ap(),
                              bass.AP(xn2buf.tensor, tt * TS,
                                      [[NCH * L, 128], [L, NCH], [1, TS]]),
                              after=[prev_xl], slot="gxn", tag="gxn")
                last_g2 = None
                last_h = None
                for oc in range(NCH):
                    bi = oc % 4
                    def hmm_(t, oc=oc, bi=bi):
                        r = None
                        for ic in range(NCH):
                            r = t.matmul(ps_h[bi][:, 0:TS],
                                         gw1T_sb[:, ic * D + oc * 128: ic * D + (oc + 1) * 128],
                                         xn2s[:, ic * TS:(ic + 1) * TS],
                                         start=(ic == 0), stop=(ic == NCH - 1))
                        return r
                    t_h = S.add("tensor", hmm_, after=lg + [tok_x, prev_he[bi]], tag="gh")
                    last_h = t_h
                    def hev(sc, bi=bi, oc=oc):
                        return sc.activation(out=h_sb[bi][:], in_=ps_h[bi][:, 0:TS],
                                             func=AF.Relu, bias=gb1_sb[:, oc:oc + 1], scale=1.0)
                    t_he = S.add("scalar", hev, after=[t_h, last_g2, prev_g2], tag="ghe")
                    def g2mm(t, oc=oc, bi=bi):
                        return t.matmul(ps_g2[0:E, 0:TS], gw2T_sb[:, oc * E:(oc + 1) * E],
                                        h_sb[bi][:, 0:TS], start=(oc == 0), stop=(oc == NCH - 1))
                    last_g2 = S.add("tensor", g2mm, after=[t_he, prev_g2], tag="g2mm")
                    prev_he[bi] = last_g2
                prev_xl = last_h
                t_red = S.add("vector", lambda v: v.tensor_reduce(
                    out=g_sb[0:E, 1:2], in_=ps_g2[0:E, 0:TS], axis=AX.X, op=ALU.add),
                    after=[last_g2], tag="gred")
                prev_acc = S.add("vector", lambda v: v.tensor_tensor(
                    out=g_sb[0:E, 0:1], in0=g_sb[0:E, 0:1], in1=g_sb[0:E, 1:2], op=ALU.add),
                    after=[t_red], tag="gacc")
                prev_g2 = prev_acc
            t_gs = S.add("vector", lambda v: v.tensor_scalar(
                out=g_sb[0:E, 0:1], in0=g_sb[0:E, 0:1], scalar1=1.0 / L, scalar2=None,
                op0=ALU.mult), after=[prev_acc], tag="gscale")
            t_gb = S.add("vector", lambda v: v.tensor_tensor(
                out=g_sb[0:E, 0:1], in0=g_sb[0:E, 0:1], in1=gb2_sb[0:E, 0:1], op=ALU.add),
                after=[t_gs, lg[-1]], tag="gbias")
            t_gbf = S.add("vector", lambda v: v.tensor_copy(g_bf[0:E, 0:1], g_sb[0:E, 0:1]),
                          after=[t_gb], tag="gbf")
            t_grt = S.add("tensor", lambda t: t.transpose(ps_gr[0:1, 0:E], g_sb[0:E, 0:1],
                                                          ident_sb[0:E, 0:E]),
                          after=[t_gb], tag="grt")
            t_growb = S.add("vector", lambda v: v.tensor_copy(grow_bf[0:1, 0:E], ps_gr[0:1, 0:E]),
                            after=[t_grt], tag="growb")
            if "g" in taps:
                S.dma("sync", tap_outs["g"], g_sb[0:E, 0:2], after=[t_gb], slot="tap")

            def gbc(t):
                r = None
                for e in range(E):
                    r = t.matmul(ps_gb[:, e:e + 1], ones_row_sb[0:1, 0:128],
                                 grow_bf[0:1, e:e + 1], start=True, stop=True)
                return r
            t_gbc = S.add("tensor", gbc, after=[t_growb], tag="gbc")
            t_gbce = S.add("vector", lambda v: v.tensor_copy(g_bc[:], ps_gb[:]),
                           after=[t_gbc], tag="gbce")
            if "gbc" in taps:
                S.dma("sync", tap_outs["gbc"], g_bc.ap(), after=[t_gbce], slot="tap")

            for nm, src_in, bpc, npc, cbase, dstpc, lb in (
                    ("mb1", tb1_in, b1pc_sb, 32, 0, mb1_pc, lb1),
                    ("mb2", tb2_in, b2pc_sb, 8, 32, mb2_pc, lb2)):
                mb_toks = []
                prev_tb = [None, None]
                for cf in range(npc):
                    ltb = S.dma("gpsimd", tbs[cf % 2][0:E, 0:128],
                                src_in[0:E, cf * 128:(cf + 1) * 128],
                                after=[prev_tb[cf % 2]], slot="mbld")
                    def mbm(t, cf=cf, cbase=cbase):
                        return t.matmul(ps_mb[:, cbase + cf: cbase + cf + 1],
                                        tbs[cf % 2][0:E, 0:128], g_bf[0:E, 0:1],
                                        start=True, stop=True)
                    t_m = S.add("tensor", mbm, after=[ltb, t_gbf], tag="mbm")
                    prev_tb[cf % 2] = t_m
                    mb_toks.append(t_m)
                def mbe(v, dstpc=dstpc, bpc=bpc, npc=npc, cbase=cbase):
                    return v.tensor_tensor(out=dstpc[:, 0:npc],
                                           in0=ps_mb[:, cbase:cbase + npc],
                                           in1=bpc[:, 0:npc], op=ALU.add)
                t_te = S.add("vector", mbe, after=mb_toks + [lb], tag="mbe")
                if nm == "mb1" and "mb1pc" in taps:
                    S.dma("sync", tap_outs["mb1pc"], mb1_pc.ap(), after=[t_te], slot="tap")
                if nm == "mb2" and "mb2pc" in taps:
                    S.dma("sync", tap_outs["mb2pc"], mb2_pc.ap(), after=[t_te], slot="tap")

            def combine(dst, base_in, tw_in, nch_out, width, tag, seed=None):
                prev_cast = seed
                grp = FF // width

                def twload(j, ch, war):
                    e, hf = j // 2, j % 2
                    hw = grp * width // 2
                    if grp == 1:
                        src = bass.AP(tw_in.tensor,
                                      (e * nch_out * 128 + ch * 128) * width + hf * hw,
                                      [[width, 128], [1, hw]])
                    else:
                        src = bass.AP(tw_in.tensor,
                                      (e * nch_out * 128 + (ch + hf * (grp // 2)) * 128) * width,
                                      [[width, 128], [128 * width, grp // 2], [1, width]])
                    return S.dma("gpsimd", twb[j % 3][:, 0:hw], src,
                                 after=[war, t_twcast], slot="tw%d" % (j % 3), tag="tw")

                def wload(ch, hf, war):
                    hw = grp * width // 2
                    if grp == 1:
                        src = bass.AP(base_in.tensor, ch * 128 * width + hf * hw,
                                      [[width, 128], [1, hw]])
                    else:
                        src = bass.AP(base_in.tensor,
                                      (ch + hf * (grp // 2)) * 128 * width,
                                      [[width, 128], [128 * width, grp // 2], [1, width]])
                    return S.dma("sync", acc6[:, hf * hw:(hf + 1) * hw], src,
                                 after=[war], slot="wbase%d" % hf, tag="wb")

                pend = {}
                gw = None
                for ch in range(0, nch_out, grp):
                    gw = grp * width
                    hw = gw // 2
                    tok_w = [wload(ch, 0, prev_cast), wload(ch, 1, prev_cast)]
                    if ch == 0:
                        for j0 in range(3):
                            pend[j0] = twload(j0, ch, prev_cast)
                    lastH = [None, None]
                    lastSlot = {}
                    for j in range(2 * E):
                        e, hf = j // 2, j % 2
                        def stt(v, e=e, hf=hf, j=j, w=hw):
                            return v.scalar_tensor_tensor(
                                out=acc6[:, hf * w:(hf + 1) * w],
                                in0=twb[j % 3][:, 0:w],
                                scalar=g_bc[:, e:e + 1],
                                in1=acc6[:, hf * w:(hf + 1) * w],
                                op0=ALU.mult, op1=ALU.add)
                        deps = [pend[j % 3], lastH[hf] if e else tok_w[hf], t_gbce]
                        lastH[hf] = S.add("vector", stt, after=deps, tag="stt")
                        lastSlot[j % 3] = lastH[hf]
                        if j + 3 < 2 * E:
                            pend[j % 3] = twload(j + 3, ch, lastH[hf])
                    if ch + grp < nch_out:
                        for j0 in range(3):
                            pend[j0] = twload(j0, ch + grp, lastSlot[j0])
                    def cast(sc, ch=ch, dst=dst, w=gw):
                        return sc.copy(out=dst[:, ch * width: ch * width + w], in_=acc6[:, 0:w])
                    prev_cast = S.add("scalar", cast, after=lastH, tag="ccast")
                return prev_cast

            c1_done = combine(mW1T, w1T_in, tw1bf, NCH, FF, "c1")
            c2_done = combine(mW2T, w2T_in, tw2bf, 32, D, "c2", seed=c1_done)
            if "mw1t" in taps:
                S.dma("sync", tap_outs["mw1t"], mW1T.ap(), after=[c1_done], slot="tap")
            if "mw2t" in taps:
                S.dma("sync", tap_outs["mw2t"], mW2T.ap(), after=[c2_done], slot="tap")
            S.emit_block()
        if stop_after <= 7:
            mw2_cm.__exit__(None, None, None)
            mw1_cm.__exit__(None, None, None)
            return nc, tap_outs

        with ExitStack() as st7:
            EE = st7.enter_context
            u_sb = EE(nc.sbuf_tensor("u_sb", [128, 32 * TS], BF16, side="right"))
            xn2s7 = [EE(nc.sbuf_tensor("f_xn2s%d" % i, [128, NCH * TS], BF16, side="right"))
                     for i in range(2)]
            x2r = EE(nc.sbuf_tensor("f_x2r", [128, 2 * TS], F32, side="right"))
            fstage = EE(nc.sbuf_tensor("fstage", [128, NCH * TS], BF16, side="right"))
            tmpf7 = [EE(nc.sbuf_tensor("tmpf7%d" % i, [128, TS], F32, side="right")) for i in range(2)]
            sq7 = [EE(nc.sbuf_tensor("sq70", [128, TS], BF16, side="right"))] * 2
            ps_u = [EE(nc.psum_tensor("f_psu%d" % i, [128, TS], F32)) for i in range(2)]
            ps_o = [EE(nc.psum_tensor("f_pso%d" % i, [128, TS], F32)) for i in range(2)]
            ps_sum7 = EE(nc.psum_tensor("f_sum", [1, TS], F32))
            ps_sq7 = EE(nc.psum_tensor("f_sq", [1, TS], F32))
            ps_bc7 = EE(nc.psum_tensor("f_bc", [128, 2 * TS], F32))
            ln7 = make_ln(sq7, ps_sum7, ps_sq7, ps_bc7)

            prev_xl7 = [None, None]
            prev_ap7, prev_st7 = None, None
            prev_ue = [None, None]
            prev_oe = [None, None]
            ring_war = [None, None]
            for tt in range(NTS):
                tok_xn2 = S.dma("gpsimd", xn2s7[tt % 2].ap(),
                                bass.AP(xn2buf.tensor, tt * TS,
                                        [[NCH * L, 128], [L, NCH], [1, TS]]),
                                after=[prev_xl7[tt % 2]], slot="fxng%d" % (tt % 2), tag="fxn")
                u_toks = []
                last_umm = None
                for cf in range(32):
                    bi = cf % 2
                    def umm(t, cf=cf, tt=tt, bi=bi):
                        r = None
                        for ch in range(NCH):
                            r = t.matmul(ps_u[bi][:, 0:TS],
                                         mW1T[:, ch * FF + cf * 128: ch * FF + (cf + 1) * 128],
                                         xn2s7[tt % 2][:, ch * TS:(ch + 1) * TS],
                                         start=(ch == 0), stop=(ch == NCH - 1))
                        return r
                    t_u = S.add("tensor", umm, after=[tok_xn2, prev_ue[bi]], tag="umm")
                    last_umm = t_u
                    def uev(sc, cf=cf, bi=bi):
                        return sc.activation(out=u_sb[:, cf * TS:(cf + 1) * TS],
                                             in_=ps_u[bi][:, 0:TS], func=AF_GELUTANH,
                                             bias=mb1_pc[:, cf:cf + 1], scale=1.0)
                    t_ue = S.add("scalar", uev, after=[t_u], tag="uev")
                    prev_ue[bi] = t_ue
                    u_toks.append(t_ue)
                prev_xl7[tt % 2] = last_umm
                o_toks = []
                for ch in range(NCH):
                    bi = ch % 2
                    def omm(t, ch=ch, bi=bi):
                        r = None
                        for cf in range(32):
                            r = t.matmul(ps_o[bi][:, 0:TS],
                                         mW2T[:, cf * D + ch * 128: cf * D + (ch + 1) * 128],
                                         u_sb[:, cf * TS:(cf + 1) * TS],
                                         start=(cf == 0), stop=(cf == 31))
                        return r
                    t_o = S.add("tensor", omm, after=u_toks + [prev_oe[bi]], tag="omm")
                    def oev(sc, ch=ch, bi=bi):
                        return sc.activation(out=fstage[:, ch * TS:(ch + 1) * TS],
                                             in_=ps_o[bi][:, 0:TS], func=AF.Identity,
                                             bias=mb2_pc[:, ch:ch + 1], scale=1.0)
                    t_oe = S.add("scalar", oev, after=[t_o], tag="oev")
                    prev_oe[bi] = t_oe
                    o_toks.append(t_oe)
                slices = [fstage[:, c * TS:(c + 1) * TS] for c in range(NCH)]
                resid = [x2r[:, (c % 2) * TS:(c % 2 + 1) * TS] for c in range(NCH)]
                r_loads = {}

                def load_x2(c, tt=tt, r_loads=r_loads):
                    if c not in r_loads and c < NCH:
                        r_loads[c] = S.dma(
                            "sync", x2r[:, (c % 2) * TS:(c % 2 + 1) * TS],
                            bass.AP(x2buf.tensor, c * L + tt * TS, [[NCH * L, 128], [1, TS]]),
                            after=[ring_war[c % 2]], slot="fx2", tag="fx2")
                    return r_loads.get(c)

                def pair_extra7(c0):
                    return [load_x2(c0), load_x2(c0 + 1)]

                def on_pair7(c0, tok_ap, tt=tt, r_loads=r_loads):
                    tok_y = S.dma("sync",
                                  bass.AP(y_dev.tensor, c0 * L + tt * TS,
                                          [[NCH * L, 128], [L, 2], [1, TS]]),
                                  x2r.ap(), after=[tok_ap], slot="yout", tag="yout")
                    ring_war[0] = ring_war[1] = tok_y
                    load_x2(c0 + 2)
                    load_x2(c0 + 3)

                prev_ap7, prev_st7 = ln7(slices, resid_slices=resid, out_slices=resid,
                                         after=o_toks,
                                         prev_apply=prev_ap7, prev_stats=prev_st7,
                                         tmpf=tmpf7, pair_extra=pair_extra7,
                                         on_pair=on_pair7)
            S.emit_block()
        mw2_cm.__exit__(None, None, None)
        mw1_cm.__exit__(None, None, None)
    return nc, tap_outs


def prep_core_inputs(inputs):
    import ml_dtypes
    bf = ml_dtypes.bfloat16
    A = lambda v: np.ascontiguousarray(np.asarray(v), dtype=np.float32)

    def pc(v):
        return A(np.asarray(v).reshape(NCH, 128).T)

    i = inputs
    sh = {}
    sh["wq_nat"] = A(i["L0_attn_Wq_w"])
    sh["wk_nat"] = A(i["L0_attn_Wk_w"])
    sh["cqT"] = A(np.asarray(i["L0_attn_convq_w"]).transpose(2, 0, 1).reshape(3 * D, D))
    sh["ckT"] = A(np.asarray(i["L0_attn_convk_w"]).transpose(2, 0, 1).reshape(3 * D, D))
    qp = np.asarray(i["L0_attn_qproj_w"])
    kp = np.asarray(i["L0_attn_kproj_w"])
    sh["qpA_T"] = A(qp[:, :D].T)
    sh["qpB_T"] = A(qp[:, D:].T)
    sh["kpA_T"] = A(kp[:, :D].T)
    sh["kpB_T"] = A(kp[:, D:].T)
    sh["wvT"] = A(np.asarray(i["L0_attn_Wv_w"]).T)
    sh["outT"] = A(np.asarray(i["L0_attn_out_w"]).T)
    sh["gw1T"] = A(np.asarray(i["L0_ffn_gate_w1"]).T)
    sh["gw2T"] = A(np.asarray(i["L0_ffn_gate_w2"]).T)
    sh["w1T"] = A(np.asarray(i["L0_ffn_W1"]).T)
    sh["w2T"] = A(np.asarray(i["L0_ffn_W2"]).T)
    sh["tw1T"] = A(np.asarray(i["L0_ffn_TW1"]).transpose(0, 2, 1).reshape(E * D, FF))
    sh["tw2T"] = A(np.asarray(i["L0_ffn_TW2"]).transpose(0, 2, 1).reshape(E * FF, D))
    sh["tb1"] = A(i["L0_ffn_Tb1"])
    sh["tb2"] = A(i["L0_ffn_Tb2"])
    sh["b1row"] = A(np.asarray(i["L0_ffn_b1"])[None, :])
    sh["b2row"] = A(np.asarray(i["L0_ffn_b2"])[None, :])
    sh["b1pc"] = A(np.asarray(i["L0_ffn_b1"]).reshape(32, 128).T)
    sh["b2pc"] = A(np.asarray(i["L0_ffn_b2"]).reshape(8, 128).T)
    sh["wqb_pc"] = pc(i["L0_attn_Wq_b"])
    sh["wkb_pc"] = pc(i["L0_attn_Wk_b"])
    sh["qpb_pc"] = pc(i["L0_attn_qproj_b"])
    sh["kpb_pc"] = pc(i["L0_attn_kproj_b"])
    sh["cqb_pc"] = pc(i["L0_attn_convq_b"])
    sh["ckb_pc"] = pc(i["L0_attn_convk_b"])
    sh["outb_pc"] = pc(i["L0_attn_out_b"])
    sh["gb1_pc"] = pc(i["L0_ffn_gate_b1"])
    sh["gb2_col"] = A(np.asarray(i["L0_ffn_gate_b2"])[:, None])
    sh["wvb_row"] = A(np.asarray(i["L0_attn_Wv_b"])[None, :])
    t1w = np.asarray(i["L0_attn_tau1_w"])[:, 0, :]
    d1w = np.asarray(i["L0_attn_del1_w"])[:, 0, :]
    sh["tau1w_pc"] = A(t1w.reshape(NCH, 128, 2, 3).transpose(1, 0, 2, 3).reshape(128, NCH * 6))
    sh["del1w_pc"] = A(d1w.reshape(NCH, 128, 2, 3).transpose(1, 0, 2, 3).reshape(128, NCH * 6))
    sh["tau1b_pc"] = A(np.asarray(i["L0_attn_tau1_b"]).reshape(NCH, 128, 2).transpose(1, 0, 2).reshape(128, NCH * 2))
    sh["del1b_pc"] = A(np.asarray(i["L0_attn_del1_b"]).reshape(NCH, 128, 2).transpose(1, 0, 2).reshape(128, NCH * 2))
    t2w = np.asarray(i["L0_attn_tau2_w"])[:, :, 0]
    d2w = np.asarray(i["L0_attn_del2_w"])[:, :, 0]
    sh["tau2wT"] = A(t2w.T.reshape(NCH, 128, 2, H).transpose(1, 0, 2, 3).reshape(128, 16 * H))
    sh["del2wT"] = A(d2w.T.reshape(NCH, 128, 2, H).transpose(1, 0, 2, 3).reshape(128, 16 * H))
    sh["tau2b_row"] = A(np.asarray(i["L0_attn_tau2_b"])[None, :])
    sh["del2b_row"] = A(np.asarray(i["L0_attn_del2_b"])[None, :])
    sh["ones_row"] = np.ones((1, 128), dtype=bf)
    sh["ones_rowf"] = np.ones((1, 128), dtype=np.float32)
    sh["ones_col"] = np.ones((128, 1), dtype=bf)
    o2 = np.zeros((2, 256), dtype=bf)
    o2[0, :128] = 1
    o2[1, 128:] = 1
    sh["oneh2"] = o2
    o16 = np.zeros((H, H * DKH), dtype=bf)
    for h in range(H):
        o16[h, h * DKH:(h + 1) * DKH] = 1
    sh["oneh16"] = o16
    sh["ident_f32"] = np.eye(128, dtype=np.float32)

    x = np.asarray(i["L0_attn_x"], dtype=np.float32)
    cores = []
    for b in range(x.shape[0]):
        d = dict(sh)
        d["x_fm"] = A(x[b].reshape(L, NCH, 128).transpose(2, 1, 0).reshape(128, NCH * L))
        cores.append(d)
    return cores


def assemble_y(y_dev):
    return np.ascontiguousarray(
        y_dev.reshape(128, NCH, L).transpose(2, 1, 0).reshape(L, D))


def kernel(**inputs):
    from concourse.bass_utils import run_bass_kernel_spmd
    cores = prep_core_inputs(inputs)
    nc, _ = build(stop_after=99, taps=())
    res = run_bass_kernel_spmd(nc, cores, core_ids=list(range(len(cores))))
    outs = []
    for r in res.results:
        outs.append(assemble_y(np.asarray(r["y_dev"], dtype=np.float32)))
    return np.stack(outs, axis=0)


def kernel(**inputs):
    from concourse.bass_utils import run_bass_kernel_spmd
    cores = prep_core_inputs(inputs)
    nc, _ = build(stop_after=99, taps=())
    res = run_bass_kernel_spmd(nc, cores, core_ids=list(range(len(cores))))
    outs = []
    for r in res.results:
        outs.append(assemble_y(np.asarray(r["y_dev"], dtype=np.float32)))
    return np.stack(outs, axis=0)
